# version 18
# baseline (speedup 1.0000x reference)
import numpy as np
import ml_dtypes
import concourse.bass as bass
import concourse.mybir as mybir
from contextlib import ExitStack
from concourse.bass import IndirectOffsetOnAxis
from concourse.bass_utils import run_bass_kernel_spmd

F32 = mybir.dt.float32
BF16 = mybir.dt.bfloat16
I32 = mybir.dt.int32
AF = mybir.ActivationFunctionType
ALU = mybir.AluOpType
AX = mybir.AxisListType
BF16NP = ml_dtypes.bfloat16

N_CORES = 8
BPC = 2
L = 4096
E = 1024
TOK = BPC * L


class Sem:
    __slots__ = ("h", "val")

    def __init__(self, handle):
        self.h = handle
        self.val = 0

    def inc(self, n=1):
        self.val += n
        return self.h


SELF_SERIALIZE = True


class Eng:

    def __init__(self, sem, serialize=None):
        if serialize is None:
            serialize = SELF_SERIALIZE
        self.sem = sem
        self.serialize = serialize
        self.cur = None
        self.marks = {}

    def bind(self, e):
        self.cur = e
        return self

    def w(self, sem, val=None):
        val = sem.val if val is None else val
        if val <= 0 or self.marks.get(id(sem), -1) >= val:
            return
        self.cur.wait_ge(sem.h, val)
        self.marks[id(sem)] = val

    def do(self, name, *a, **k):
        if self.serialize:
            self.w(self.sem, self.sem.val)
        inst = getattr(self.cur, name)(*a, **k)
        inst.then_inc(self.sem.inc(1), 1)
        return inst

    def raw(self, name, *a, **k):
        return getattr(self.cur, name)(*a, **k)


class Q:

    def __init__(self):
        self.cur = None
        self.marks = {}

    def bind(self, e):
        self.cur = e
        return self

    def w(self, sem, val=None):
        val = sem.val if val is None else val
        if val <= 0 or self.marks.get(id(sem), -1) >= val:
            return
        self.cur.wait_ge(sem.h, val)
        self.marks[id(sem)] = val

    def dma(self, out, in_, sem, **kw):
        self.cur.dma_start(out, in_, **kw).then_inc(sem.inc(16), 16)
        return sem.val

    def dmat(self, out, in_, sem):
        self.cur.dma_start_transpose(out, in_).then_inc(sem.inc(16), 16)
        return sem.val



def build_stage_b(nc=None, pf="", ext=None, ctx=None, phases=4):
    ext = ext or {}
    if nc is None:
        nc = bass.Bass("TRN2", target_bir_lowering=False)

    def exio(name, shape, dt, kind):
        return ext[name] if name in ext else nc.dram_tensor(name, shape, dt, kind=kind).ap()
    xin = exio("xin", [TOK, E], F32, "ExternalInput")
    w1 = exio("w1", [4096, 1024], F32, "ExternalInput")
    b1 = exio("b1", [1, 4096], F32, "ExternalInput")
    w2 = exio("w2", [1024, 4096], F32, "ExternalInput")
    b2 = exio("b2", [1, 1024], F32, "ExternalInput")
    xout = exio("xout", [TOK, E], F32, "ExternalOutput")

    wq1d = nc.dram_tensor(pf + "wq1d", [4096, 1024], BF16)
    wq2d = nc.dram_tensor(pf + "wq2d", [1024, 4096], BF16)
    xlnd = nc.dram_tensor(pf + "xlnd", [TOK, E], BF16)
    gacc = nc.dram_tensor(pf + "gacc", [256], F32)

    NW1 = float(4096 * 1024)
    NW2 = float(1024 * 4096)

    with ExitStack() as stk:
        En = stk.enter_context
        SV = Eng(Sem(En(nc.semaphore(pf + "sv"))))
        SA = Eng(Sem(En(nc.semaphore(pf + "sa"))))
        SG = Eng(Sem(En(nc.semaphore(pf + "sg"))))
        SP = Eng(Sem(En(nc.semaphore(pf + "sp"))), serialize=False)
        QS = Q()
        QA = Q()
        QG = Q()
        dm = {k: Sem(En(nc.semaphore(pf + k))) for k in
              ["dw0", "dw1", "dx2", "dx3", "dg", "dq0", "dq1", "dx", "dxs0", "dxs1", "dxt", "dost", "dmisc"]}

        onesf = En(nc.sbuf_tensor(pf + "onesf", [1, 128], F32))
        epsbc = En(nc.sbuf_tensor(pf + "epsbc", [128, 1], F32))

        GROUPS = [("w1", g, 1024) for g in range(32)] + [("w2", g, 4096) for g in range(8)]
        w1g = w1.rearrange("(g p) d -> g p d", p=128)
        w2g = w2.rearrange("(g p) d -> g p d", p=128)
        wq1g = wq1d.rearrange("(g p) d -> g p d", p=128)
        wq2g = wq2d.rearrange("(g p) d -> g p d", p=128)
        wsrc = lambda i: (w1g if GROUPS[i][0] == "w1" else w2g)[GROUPS[i][1]]
        wdst = lambda i: (wq1g if GROUPS[i][0] == "w1" else wq2g)[GROUPS[i][1]]
        NG = len(GROUPS)

        with (
            nc.sbuf_tensor(pf + "wtile", [128, 2 * 4096], F32) as wtile,
            nc.sbuf_tensor(pf + "acc8", [128, 48], F32) as acc8,
            nc.sbuf_tensor(pf + "asum", [128, 2], F32) as asum,
            nc.sbuf_tensor(pf + "accrow", [1, 256], F32) as accrow,
            nc.sbuf_tensor(pf + "thr", [1, 2], F32) as thr,
            nc.sbuf_tensor(pf + "thrbc", [128, 2], F32) as thrbc,
            nc.sbuf_tensor(pf + "sgn", [128, 2 * 4096], F32) as sgn,
            nc.sbuf_tensor(pf + "qout", [128, 2 * 4096], BF16) as qout,
            nc.psum_tensor([128, 2], F32) as psb,
        ):
            ldv = {}
            redv = {}
            with nc.Block() as blk:
                @blk.gpsimd
                def _(g):
                    SG.bind(g)
                    SG.do("memset", onesf[:, :], 1.0)
                    SG.do("memset", epsbc[:, :], 1e-5)

                for i in range(NG):
                    W = GROUPS[i][2]
                    sl = i % 2
                    dsem = dm["dw0"] if sl == 0 else dm["dw1"]

                    @blk.sync
                    def _(s, i=i, W=W, sl=sl, dsem=dsem):
                        QS.bind(s)
                        if i >= 2:
                            QS.w(SV.sem, redv[i - 2])
                        ldv[i] = QS.dma(wtile[:, sl * 4096:sl * 4096 + W], wsrc(i), dsem)

                    @blk.vector
                    def _(v, i=i, W=W, sl=sl, dsem=dsem):
                        SV.bind(v)
                        SV.w(dsem, ldv[i])
                        SV.do("tensor_reduce", acc8[:, i:i + 1],
                              wtile[:, sl * 4096:sl * 4096 + W],
                              AX.X, ALU.add, apply_absolute_value=True)
                        redv[i] = SV.sem.val

                @blk.vector
                def _(v):
                    SV.bind(v)
                    SV.do("tensor_reduce", asum[:, 0:1], acc8[:, 0:32], AX.X, ALU.add)
                    SV.do("tensor_reduce", asum[:, 1:2], acc8[:, 32:40], AX.X, ALU.add)
                    ldv["fold"] = SV.sem.val

                @blk.gpsimd
                def _(g):
                    QG.bind(g)
                    QG.w(SV.sem, ldv["fold"])
                    QG.dma(gacc.rearrange("(c p) -> p c", p=128), asum[:, :], dm["dg"],
                           allow_slow_non_contiguous=True)
                    QG.w(dm["dg"], dm["dg"].val)
                    ldv["accrow"] = QG.dma(accrow[:, :], gacc.rearrange("(b a) -> b a", b=1), dm["dg"])

                @blk.vector
                def _(v):
                    SV.bind(v)
                    SV.w(dm["dg"], ldv["accrow"])
                    SV.do("tensor_reduce", thr[:, 0:1], accrow[:1, 0:128], AX.X, ALU.add)
                    SV.do("tensor_reduce", thr[:, 1:2], accrow[:1, 128:256], AX.X, ALU.add)
                    SV.do("tensor_scalar", thr[:, 0:1], thr[:, 0:1], 0.5 / NW1, 0.5e-5, ALU.mult, ALU.add)
                    SV.do("tensor_scalar", thr[:, 1:2], thr[:, 1:2], 0.5 / NW2, 0.5e-5, ALU.mult, ALU.add)
                    ldv["thr"] = SV.sem.val

                @blk.tensor
                def _(t):
                    SP.bind(t)
                    SP.w(SV.sem, ldv["thr"])
                    SP.w(SG.sem, 1)
                    SP.do("matmul", psb[:, :], onesf[:1, :], thr[:1, :], start=True, stop=True)
                    ldv["psb"] = SP.sem.val

                @blk.vector
                def _(v):
                    SV.bind(v)
                    SV.w(SP.sem, ldv["psb"])
                    SV.do("tensor_copy", thrbc[:, :], psb[:, :])
                    ldv["thrbc"] = SV.sem.val

            qst = {}
            with nc.Block() as blk:
                for i in range(NG):
                    tc = 0 if GROUPS[i][0] == "w1" else 1
                    sl = i % 2
                    W = GROUPS[i][2]
                    dsem = dm["dw0"] if sl == 0 else dm["dw1"]
                    stsem = dm["dq0"] if sl == 0 else dm["dq1"]

                    @blk.sync
                    def _(s, i=i, sl=sl, W=W, dsem=dsem):
                        QS.bind(s)
                        if i >= 2:
                            QS.w(SV.sem, qst[("q", i - 2)])
                        ldv[("q", i)] = QS.dma(wtile[:, sl * 4096:sl * 4096 + W], wsrc(i), dsem)

                    @blk.scalar
                    def _(sc, i=i, sl=sl, W=W, dsem=dsem):
                        SA.bind(sc)
                        SA.w(dsem, ldv[("q", i)])
                        if i >= 2:
                            SA.w(SV.sem, qst[("q", i - 2)])
                        SA.do("activation", sgn[:, sl * 4096:sl * 4096 + W],
                              wtile[:, sl * 4096:sl * 4096 + W], AF.Sign)
                        SA.do("activation", wtile[:, sl * 4096:sl * 4096 + W],
                              wtile[:, sl * 4096:sl * 4096 + W], AF.Abs)
                        qst[("sgn", i)] = SA.sem.val

                    @blk.vector
                    def _(v, i=i, sl=sl, tc=tc, W=W, stsem=stsem):
                        SV.bind(v)
                        SV.w(SA.sem, qst[("sgn", i)])
                        SV.w(SV.sem, ldv["thrbc"])
                        if i >= 2:
                            SV.w(stsem, qst[("st", i - 2)])
                        SV.do("tensor_scalar", wtile[:, sl * 4096:sl * 4096 + W],
                              wtile[:, sl * 4096:sl * 4096 + W],
                              thrbc[:, tc:tc + 1], None, ALU.is_ge)
                        SV.do("tensor_tensor", qout[:, sl * 4096:sl * 4096 + W],
                              sgn[:, sl * 4096:sl * 4096 + W],
                              wtile[:, sl * 4096:sl * 4096 + W], ALU.mult)
                        qst[("q", i)] = SV.sem.val

                    @blk.sync
                    def _(s, i=i, sl=sl, W=W, stsem=stsem):
                        QS.bind(s)
                        QS.w(SV.sem, qst[("q", i)])
                        qst[("st", i)] = QS.dma(wdst(i), qout[:, sl * 4096:sl * 4096 + W], stsem)

        w1t = En(nc.sbuf_tensor(pf + "w1t", [128, 8 * 4096], BF16))
        w2t = En(nc.sbuf_tensor(pf + "w2t", [128, 32 * 1024], BF16))
        b1col = En(nc.sbuf_tensor(pf + "b1col", [128, 32], F32))
        b2bc = En(nc.sbuf_tensor(pf + "b2bc", [128, 1024], F32))
        xsb = En(nc.sbuf_tensor(pf + "xsb", [128, 4096], F32))
        xlnb = En(nc.sbuf_tensor(pf + "xlnb", [128, 2 * 1024], F32))
        xlnsb = En(nc.sbuf_tensor(pf + "xlnsb", [128, 1024], BF16))
        xc2 = En(nc.sbuf_tensor(pf + "xc2", [128, 1024], F32))
        red = En(nc.sbuf_tensor(pf + "red", [128, 16], F32))
        xt2 = En(nc.sbuf_tensor(pf + "xt2", [128, 4096], BF16))
        ht = En(nc.sbuf_tensor(pf + "ht", [128, 32 * 512], BF16))
        osb = En(nc.sbuf_tensor(pf + "osb", [128, 1024], F32))
        ph = En(nc.psum_tensor([128, 2048], F32))
        po = En(nc.psum_tensor([128, 1024], F32))

        M = {}
        with (
            nc.psum_tensor([128, 1024], F32) as pb2,
            nc.Block() as blk,
        ):
            b2row = osb
            @blk.scalar
            def _(sc):
                QA.bind(sc)
                QA.w(dm["dq0"])
                QA.w(dm["dq1"])
                QA.dmat(w1t[:, :].rearrange("p (c f) -> p c f", c=8), wq1d.ap(), dm["dxt"])
                M["wt"] = QA.dmat(w2t[:, :].rearrange("p (c f) -> p c f", c=32), wq2d.ap(), dm["dxt"])

            @blk.gpsimd
            def _(g):
                QG.bind(g)
                QG.dma(b1col[:, :].rearrange("p (a c) -> p a c", a=1),
                       b1.rearrange("a (c p) -> p a c", p=128), dm["dmisc"],
                       allow_slow_non_contiguous=True)
                M["b"] = QG.dma(b2row[0:1, :], b2, dm["dmisc"])

            @blk.tensor
            def _(t):
                SP.bind(t)
                SP.w(dm["dmisc"], M["b"])
                SP.raw("matmul", pb2[:, 0:512], onesf[:1, :], b2row[:1, 0:512], start=True, stop=True)
                SP.do("matmul", pb2[:, 512:1024], onesf[:1, :], b2row[:1, 512:1024], start=True, stop=True)
                M["pb2"] = SP.sem.val

            @blk.vector
            def _(v):
                SV.bind(v)
                SV.w(SP.sem, M["pb2"])
                SV.do("tensor_copy", b2bc[:, :], pb2[:, :])
                M["b2bc"] = SV.sem.val

        NT = TOK // 512
        with nc.Block() as blk:
            for j in range(NT):
                off = j * 512

                @blk.sync
                def _(s, j=j, off=off):
                    QS.bind(s)
                    if j == 0:
                        for si in range(2):
                            M[("x2", si)] = QS.dma(
                                xlnb[:, (si % 2) * 1024:(si % 2 + 1) * 1024],
                                xin[si * 128:(si + 1) * 128, :],
                                dm["dx2"] if si % 2 == 0 else dm["dx3"])
                    if j >= 1:
                        QS.w(SV.sem, M[("evac", j - 1)])
                    for u in range(4):
                        QS.dma(xsb[:, u * 1024:(u + 1) * 1024],
                               xin[off + u * 128:off + (u + 1) * 128, :], dm["dx"])
                    M[("x", j)] = dm["dx"].val

                for u in range(4):
                    sidx = j * 4 + u
                    ssl = sidx % 2
                    xs_sem = dm["dxs0"] if ssl == 0 else dm["dxs1"]

                    @blk.vector
                    def _(v, j=j, u=u, sidx=sidx):
                        SV.bind(v)
                        x_ap = xlnb[:, (sidx % 2) * 1024:(sidx % 2 + 1) * 1024]
                        sl = sidx % 2
                        r = [red[:, sl * 8 + k:sl * 8 + k + 1] for k in range(5)]
                        SV.w(dm["dx2"] if sidx % 2 == 0 else dm["dx3"], M[("x2", sidx)])
                        if sidx >= 1:
                            SV.w(SA.sem, M[("xln", sidx - 1)])
                            SV.w(dm["dxs0"], M[("xst", sidx - 1)])
                        SV.do("tensor_reduce", r[0], x_ap, AX.X, ALU.add)
                        SV.do("tensor_scalar", r[1], r[0], -1.0 / E, None, ALU.mult)
                        SV.do("tensor_scalar", xc2[:, :], x_ap, r[1], None, ALU.add)
                        SV.do("scalar_tensor_tensor", xlnsb[:, :], xc2[:, :], 1.0, xc2[:, :],
                              ALU.mult, ALU.mult, accum_out=r[2])
                        M[("stt", sidx)] = SV.sem.val

                    @blk.scalar
                    def _(sc, sidx=sidx, ssl=ssl, xs_sem=xs_sem):
                        SA.bind(sc)
                        sl = sidx % 2
                        r = [red[:, sl * 8 + k:sl * 8 + k + 1] for k in range(5)]
                        SA.w(SV.sem, M[("stt", sidx)])
                        SA.do("activation", r[3], r[2], AF.Ln, bias=epsbc[:, 0:1], scale=1.0 / E)
                        SA.do("activation", r[4], r[3], AF.Exp, bias=0.0, scale=-0.5)
                        SA.do("activation", xlnsb[:, :],
                              xc2[:, :], AF.Copy, bias=0.0, scale=r[4])
                        M[("xln", sidx)] = SA.sem.val

                    @blk.sync
                    def _(s, sidx=sidx, ssl=ssl, off=off, u=u, xs_sem=xs_sem):
                        QS.bind(s)
                        QS.w(SA.sem, M[("xln", sidx)])
                        M[("xst", sidx)] = QS.dma(xlnd.ap()[off + u * 128:off + (u + 1) * 128, :],
                                                  xlnsb[:, :], dm["dxs0"])
                        if sidx + 2 < (TOK // 128):
                            QS.w(SV.sem, M[("stt", sidx)])
                            M[("x2", sidx + 2)] = QS.dma(
                                xlnb[:, (sidx % 2) * 1024:(sidx % 2 + 1) * 1024],
                                xin[(sidx + 2) * 128:(sidx + 3) * 128, :],
                                dm["dx2"] if sidx % 2 == 0 else dm["dx3"])

                @blk.scalar
                def _(sc, j=j, off=off):
                    QA.bind(sc)
                    QA.w(dm["dxs0"], M[("xst", j * 4 + 3)])
                    if j >= 1:
                        QA.w(SP.sem, M[("hmm", j - 1, 31)])
                    M[("xt", j)] = QA.dmat(
                        xt2[:, :].rearrange("p (c t) -> p c t", c=8),
                        xlnd.ap()[off:off + 512, :], dm["dxt"])

                for m in range(32):
                    @blk.tensor
                    def _(t, j=j, m=m):
                        SP.bind(t)
                        if m == 0:
                            SP.w(dm["dxt"], M[("xt", j)])
                        rp = j * 32 + m - 4
                        if rp >= 0:
                            SP.w(SA.sem, M[("relu", rp // 32, rp % 32)])
                        bank = m % 4
                        for c in range(8):
                            mi = SP.raw("matmul", ph[:, bank * 512:(bank + 1) * 512],
                                        w1t[:, c * 4096 + m * 128:c * 4096 + (m + 1) * 128],
                                        xt2[:, c * 512:(c + 1) * 512],
                                        start=(c == 0), stop=(c == 7))
                            if c == 7:
                                mi.then_inc(SP.sem.inc(1), 1)
                        M[("hmm", j, m)] = SP.sem.val

                    @blk.scalar
                    def _(sc, j=j, m=m):
                        SA.bind(sc)
                        SA.w(SP.sem, M[("hmm", j, m)])
                        bank = m % 4
                        SA.do("activation", ht[:, m * 512:(m + 1) * 512],
                              ph[:, bank * 512:(bank + 1) * 512],
                              AF.Relu, bias=b1col[:, m:m + 1], scale=1.0)
                        M[("relu", j, m)] = SA.sem.val

                @blk.vector
                def _(v, j=j):
                    SV.bind(v)
                    SV.w(SV.sem, M["b2bc"])
                    SV.w(SV.sem, M[("stt", j * 4 + 3)])
                    SV.w(dm["dx"], M[("x", j)])
                    for u in range(4):
                        SV.do("tensor_tensor", xsb[:, u * 1024:(u + 1) * 1024],
                              xsb[:, u * 1024:(u + 1) * 1024], b2bc[:, :], ALU.add)
                    M[("xb", j)] = SV.sem.val

                for idx in range(8):
                    ts, dh = idx // 2, idx % 2

                    @blk.tensor
                    def _(t, j=j, idx=idx, ts=ts, dh=dh):
                        SP.bind(t)
                        if idx == 0:
                            SP.w(SA.sem, M[("relu", j, 31)])
                        og = j * 8 + idx - 2
                        if og >= 0:
                            SP.w(SV.sem, M[("evacg", og)])
                        bank = idx % 2
                        for m in range(32):
                            mi = SP.raw("matmul", po[:, bank * 512:(bank + 1) * 512],
                                        ht[:, m * 512 + ts * 128:m * 512 + (ts + 1) * 128],
                                        w2t[:, m * 1024 + dh * 512:m * 1024 + (dh + 1) * 512],
                                        start=(m == 0), stop=(m == 31))
                            if m == 31:
                                mi.then_inc(SP.sem.inc(1), 1)
                        M[("omm", j, idx)] = SP.sem.val

                    @blk.vector
                    def _(v, j=j, idx=idx, ts=ts, dh=dh):
                        SV.bind(v)
                        SV.w(SP.sem, M[("omm", j, idx)])
                        SV.w(SV.sem, M[("xb", j)])
                        if j * 4 + ts >= 1:
                            SV.w(dm["dost"], M.get(("ost", j * 4 + ts - 1), 0))
                        bank = idx % 2
                        SV.do("tensor_tensor", osb[:, dh * 512:(dh + 1) * 512],
                              po[:, bank * 512:(bank + 1) * 512],
                              xsb[:, ts * 1024 + dh * 512:ts * 1024 + (dh + 1) * 512],
                              ALU.add)
                        M[("evacg", j * 8 + idx)] = SV.sem.val
                        M[("evac", j)] = SV.sem.val

                    if dh == 1:
                        @blk.sync
                        def _(s, j=j, ts=ts, off=off):
                            QS.bind(s)
                            QS.w(SV.sem, M[("evacg", j * 8 + ts * 2 + 1)])
                            M[("ost", j * 4 + ts)] = QS.dma(
                                xout[off + ts * 128:off + (ts + 1) * 128, :], osb[:, :], dm["dost"])

            @blk.sync
            def _(s):
                s.wait_ge(dm["dost"].h, dm["dost"].val)

    return nc



def run_b(x1_full, inputs):
    nc = build_stage_b()
    maps = []
    for c in range(N_CORES):
        maps.append(dict(
            xin=np.ascontiguousarray(x1_full[c * TOK:(c + 1) * TOK]),
            w1=inputs["L0_ffn_W1"], b1=inputs["L0_ffn_b1"].reshape(1, -1),
            w2=inputs["L0_ffn_W2"], b2=inputs["L0_ffn_b2"].reshape(1, -1),
        ))
    res = run_bass_kernel_spmd(nc, maps, core_ids=list(range(N_CORES)))
    return np.concatenate([res.results[c]["xout"] for c in range(N_CORES)], axis=0)





def build_stage_c(nc=None, pf="", ext=None, ctx=None, phases=4):
    ext = ext or {}
    if nc is None:
        nc = bass.Bass("TRN2", target_bir_lowering=False)

    def exio(name, shape, dt, kind):
        return ext[name] if name in ext else nc.dram_tensor(name, shape, dt, kind=kind).ap()
    xin = exio("xin", [TOK, E], F32, "ExternalInput")
    wq = exio("wq", [E, E], F32, "ExternalInput")
    wk = exio("wk", [E, E], F32, "ExternalInput")
    wv = exio("wv", [E, E], F32, "ExternalInput")
    wo = exio("wo", [E, E], F32, "ExternalInput")
    bq = exio("bq", [1, E], F32, "ExternalInput")
    bk = exio("bk", [1, E], F32, "ExternalInput")
    bv = exio("bv", [1, E], F32, "ExternalInput")
    bo = exio("bo", [1, E], F32, "ExternalInput")
    xout = exio("xout", [TOK, E], F32, "ExternalOutput")

    wbd = {n: nc.dram_tensor(pf + f"w{n}bd", [E, E], BF16) for n in "qkvo"}
    xlnd = nc.dram_tensor(pf + "xlnd", [L, E], BF16)
    qtd = nc.dram_tensor(pf + "qtd", [8, 128, L], BF16)

    with ExitStack() as stk:
        En = stk.enter_context
        if ctx is None:
            SV = Eng(Sem(En(nc.semaphore(pf + "sv"))))
            SA = Eng(Sem(En(nc.semaphore(pf + "sa"))))
            SG = Eng(Sem(En(nc.semaphore(pf + "sg"))))
            SP = Eng(Sem(En(nc.semaphore(pf + "sp"))), serialize=False)
        else:
            SV, SA, SG, SP = ctx["SV"], ctx["SA"], ctx["SG"], ctx["SP"]
        QS = Q()
        QA = Q()
        QG = Q()
        dm = {k: Sem(En(nc.semaphore(pf + k))) for k in
              ["dwc", "dx", "dxs0", "dxs1", "dxt", "dwt", "dqs0", "dqs1", "dql", "dost", "dmisc", "dres0", "dres1"]}

        onesf = En(nc.sbuf_tensor(pf + "onesf", [1, 128], F32))
        epsbc = En(nc.sbuf_tensor(pf + "epsbc", [128, 1], F32))
        bcol = En(nc.sbuf_tensor(pf + "bcol", [128, 16], F32))
        bvbc = En(nc.sbuf_tensor(pf + "bvbc", [128, 1024], F32))
        bobc = En(nc.sbuf_tensor(pf + "bobc", [128, 1024], F32))
        ksb = En(nc.sbuf_tensor(pf + "ksb", [128, 8 * L], BF16))
        vsb = En(nc.sbuf_tensor(pf + "vsb", [128, 32 * E], BF16))
        wtb = En(nc.sbuf_tensor(pf + "wtb", [128, 8 * 1024], BF16))
        xts = En(nc.sbuf_tensor(pf + "xts", [128, 4096], BF16))
        xlnsb = En(nc.sbuf_tensor(pf + "xlnsb", [128, 2 * 1024], BF16))
        xc2 = En(nc.sbuf_tensor(pf + "xc2", [128, 1024], F32))
        sqs = En(nc.sbuf_tensor(pf + "sqs", [128, 1024], BF16))
        red = En(nc.sbuf_tensor(pf + "red", [128, 16], F32))
        xsb = En(nc.sbuf_tensor(pf + "xsb", [128, 2048], F32))
        qts = En(nc.sbuf_tensor(pf + "qts", [128, 4096], BF16))
        ptl = En(nc.sbuf_tensor(pf + "ptl", [128, 2 * 1024], BF16))
        aout = En(nc.sbuf_tensor(pf + "aout", [128, 8 * 512], BF16))
        qbuf = En(nc.sbuf_tensor(pf + "qbuf", [128, 2 * 512], BF16))
        osb = En(nc.sbuf_tensor(pf + "osb", [128, 1024], F32))
        ps_s = En(nc.psum_tensor([128, 4 * 512], F32))
        ps_pv = En(nc.psum_tensor([128, 2 * 512], F32))
        ps_o = En(nc.psum_tensor([128, 2 * 512], F32))

        M = {}

        with nc.Block() as blk:
            @blk.gpsimd
            def _(g):
                SG.bind(g)
                SG.do("memset", onesf[:, :], 1.0)
                SG.do("memset", epsbc[:, :], 1e-5)

            @blk.gpsimd
            def _(g):
                QG.bind(g)
                for i, (nm, w) in enumerate(zip("qkvo", [wq, wk, wv, wo])):
                    stg = ksb[:, (i % 2) * 8192:(i % 2) * 8192 + 8192]
                    QG.dma(stg.rearrange("p (c d) -> p c d", c=8),
                           w.rearrange("(c p) d -> p c d", p=128), dm["dwc"])
                    QG.w(dm["dwc"], dm["dwc"].val)
                    M[("wc", i)] = QG.dma(wbd[nm].ap().rearrange("(c p) d -> p c d", p=128),
                                          stg.rearrange("p (c d) -> p c d", c=8), dm["dwc"])
                    QG.w(dm["dwc"], dm["dwc"].val)
                QG.dma(bcol[:, 0:8].rearrange("p (a c) -> p a c", a=1),
                       bq.rearrange("a (c p) -> p a c", p=128), dm["dmisc"],
                       allow_slow_non_contiguous=True)
                QG.dma(bcol[:, 8:16].rearrange("p (a c) -> p a c", a=1),
                       bk.rearrange("a (c p) -> p a c", p=128), dm["dmisc"],
                       allow_slow_non_contiguous=True)
                QG.dma(osb[0:1, 0:1024], bv, dm["dmisc"])
                M["bst"] = QG.dma(xc2[0:1, 0:1024], bo, dm["dmisc"])

            @blk.tensor
            def _(t):
                SP.bind(t)
                SP.w(dm["dmisc"], M["bst"])
                SP.w(SG.sem, 1)
                SP.raw("matmul", ps_s[:, 0:512], onesf[:1, :], osb[:1, 0:512], start=True, stop=True)
                SP.raw("matmul", ps_s[:, 512:1024], onesf[:1, :], osb[:1, 512:1024], start=True, stop=True)
                SP.raw("matmul", ps_pv[:, 0:512], onesf[:1, :], xc2[:1, 0:512], start=True, stop=True)
                SP.do("matmul", ps_pv[:, 512:1024], onesf[:1, :], xc2[:1, 512:1024], start=True, stop=True)
                M["bmm"] = SP.sem.val

            @blk.vector
            def _(v):
                SV.bind(v)
                SV.w(SP.sem, M["bmm"])
                SV.do("tensor_copy", bvbc[:, :], ps_s[:, 0:1024])
                SV.do("tensor_copy", bobc[:, :], ps_pv[:, 0:1024])
                M["bprep"] = SV.sem.val

        for b in range(BPC):
            B0 = b * L
            with nc.Block() as blk:
                for u in range(L // 128):
                    sidx = u
                    ssl = u % 2
                    xs_sem = dm["dxs0"] if ssl == 0 else dm["dxs1"]
                    xl = u % 2

                    @blk.sync
                    def _(s, b=b, u=u, xl=xl, sidx=sidx):
                        QS.bind(s)
                        if u >= 2:
                            QS.w(SV.sem, M[("c1stt", b, u - 2)])
                        elif b > 0:
                            QS.w(SV.sem, M[("oevac", b - 1, 7, 7)])
                        M[("c1x", b, u)] = QS.dma(xsb[:, xl * 1024:(xl + 1) * 1024],
                                                  xin[B0 + u * 128:B0 + (u + 1) * 128, :], dm["dx"])

                    @blk.vector
                    def _(v, b=b, u=u, xl=xl, sidx=sidx):
                        SV.bind(v)
                        x_ap = xsb[:, xl * 1024:(xl + 1) * 1024]
                        sl = sidx % 2
                        r = [red[:, sl * 8 + k:sl * 8 + k + 1] for k in range(5)]
                        SV.w(dm["dx"], M[("c1x", b, u)])
                        if b == 0 and u == 0:
                            SV.w(SP.sem, M["bmm"])
                        if sidx >= 1:
                            SV.w(SA.sem, M[("c1xln", b, sidx - 1)])
                        SV.do("tensor_reduce", r[0], x_ap, AX.X, ALU.add)
                        SV.do("tensor_scalar", r[1], r[0], -1.0 / E, None, ALU.mult)
                        SV.do("tensor_scalar", xc2[:, :], x_ap, r[1], None, ALU.add)
                        SV.do("scalar_tensor_tensor", sqs[:, :], xc2[:, :], 1.0, xc2[:, :],
                              ALU.mult, ALU.mult, accum_out=r[2])
                        M[("c1stt", b, sidx)] = SV.sem.val

                    @blk.scalar
                    def _(sc, b=b, sidx=sidx, ssl=ssl, xs_sem=xs_sem):
                        SA.bind(sc)
                        sl = sidx % 2
                        r = [red[:, sl * 8 + k:sl * 8 + k + 1] for k in range(5)]
                        SA.w(SV.sem, M[("c1stt", b, sidx)])
                        if sidx >= 2:
                            SA.w(xs_sem, M[("c1xst", b, sidx - 2)])
                        SA.do("activation", r[3], r[2], AF.Ln, bias=epsbc[:, 0:1], scale=1.0 / E)
                        SA.do("activation", r[4], r[3], AF.Exp, bias=0.0, scale=-0.5)
                        SA.do("activation", xlnsb[:, ssl * 1024:(ssl + 1) * 1024],
                              xc2[:, :], AF.Copy, bias=0.0, scale=r[4])
                        M[("c1xln", b, sidx)] = SA.sem.val

                    @blk.sync
                    def _(s, b=b, u=u, sidx=sidx, ssl=ssl, xs_sem=xs_sem):
                        QS.bind(s)
                        QS.w(SA.sem, M[("c1xln", b, sidx)])
                        M[("c1xst", b, sidx)] = QS.dma(
                            xlnd.ap()[u * 128:(u + 1) * 128, :],
                            xlnsb[:, ssl * 1024:(ssl + 1) * 1024], xs_sem)

            for iw, nm in enumerate(["k", "v", "q"]):
                with nc.Block() as blk:
                    @blk.scalar
                    def _(sc, b=b, iw=iw, nm=nm):
                        QA.bind(sc)
                        QA.w(dm["dwc"], M[("wc", {"k": 1, "v": 2, "q": 0}[nm])])
                        if iw > 0:
                            QA.w(SP.sem, M[("pmm", b, iw - 1, 7)])
                        elif b > 0:
                            QA.w(SP.sem, M[("omm", b - 1, 7, 7)])
                        M[("wt", b, iw)] = QA.dmat(wtb[:, :].rearrange("p (c e) -> p c e", c=8),
                                                   wbd[nm].ap(), dm["dwt"])

                    for j in range(8):
                        @blk.scalar
                        def _(sc, b=b, iw=iw, j=j, nm=nm):
                            QA.bind(sc)
                            pj = M.get(("pmm", b, iw, j - 1)) if j > 0 else \
                                (M.get(("pmm", b, iw - 1, 7)) if iw > 0 else M.get(("pmm", b - 1, 2, 7)))
                            if pj:
                                QA.w(SP.sem, pj)
                            QA.w(dm["dxs0"], M[("c1xst", b, (L // 128) - 2)])
                            QA.w(dm["dxs1"], M[("c1xst", b, (L // 128) - 1)])
                            M[("xt", b, iw, j)] = QA.dmat(
                                xts[:, :].rearrange("p (c t) -> p c t", c=8),
                                xlnd.ap()[j * 512:(j + 1) * 512, :], dm["dxt"])

                        @blk.tensor
                        def _(t, b=b, iw=iw, nm=nm, j=j):
                            SP.bind(t)
                            SP.w(dm["dwt"], M[("wt", b, iw)])
                            SP.w(dm["dxt"], M[("xt", b, iw, j)])

                        for m in range(8):
                            @blk.tensor
                            def _(t, b=b, iw=iw, nm=nm, j=j, m=m):
                                SP.bind(t)
                                SP.w(SV.sem, M["bprep"])
                                pr = M.get(("pevac", b, iw, j, m - 2)) if m >= 2 else \
                                    (M.get(("pevac", b, iw, j - 1, 6 + m)) if j > 0 else
                                     (M.get(("pevac", b, iw - 1, 7, 6 + m)) if iw > 0 else
                                      M.get(("pevac", b - 1, 2, 7, 6 + m))))
                                if pr:
                                    SP.w(SV.sem, pr)
                                if nm in ("k", "q"):
                                    for c in range(8):
                                        mi = SP.raw("matmul", ps_s[:, (m % 2) * 512:(m % 2 + 1) * 512],
                                                    wtb[:, c * 1024 + m * 128:c * 1024 + (m + 1) * 128],
                                                    xts[:, c * 512:(c + 1) * 512],
                                                    start=(c == 0), stop=(c == 7))
                                        if c == 7:
                                            mi.then_inc(SP.sem.inc(1), 1)
                                else:
                                    ts, dh = m // 2, m % 2
                                    for c in range(8):
                                        mi = SP.raw("matmul", ps_s[:, (m % 2) * 512:(m % 2 + 1) * 512],
                                                    xts[:, c * 512 + ts * 128:c * 512 + (ts + 1) * 128],
                                                    wtb[:, c * 1024 + dh * 512:c * 1024 + (dh + 1) * 512],
                                                    start=(c == 0), stop=(c == 7))
                                        if c == 7:
                                            mi.then_inc(SP.sem.inc(1), 1)
                                M[("pmmg", b, iw, j, m)] = SP.sem.val
                                if m == 7:
                                    M[("pmm", b, iw, j)] = SP.sem.val

                            if nm == "k":
                                @blk.vector
                                def _(v, b=b, iw=iw, j=j, m=m):
                                    SV.bind(v)
                                    SV.w(dm["dmisc"], M["bst"])
                                    SV.w(dm["dwc"], M[("wc", 3)])
                                    SV.w(SP.sem, M[("pmmg", b, iw, j, m)])
                                    SV.do("tensor_scalar", ksb[:, m * L + j * 512:m * L + (j + 1) * 512],
                                          ps_s[:, (m % 2) * 512:(m % 2 + 1) * 512],
                                          bcol[:, 8 + m:9 + m], None, ALU.add)
                                    M[("pevac", b, iw, j, m)] = SV.sem.val
                            elif nm == "q":
                                @blk.vector
                                def _(v, b=b, iw=iw, j=j, m=m):
                                    SV.bind(v)
                                    SV.w(SV.sem, M["bprep"])
                                    SV.w(SP.sem, M[("pmmg", b, iw, j, m)])
                                    qsl = m % 2
                                    if (j * 8 + m) >= 2:
                                        SV.w(dm["dqs0"] if qsl == 0 else dm["dqs1"],
                                             M.get(("qst", b, j * 8 + m - 2), 0))
                                    SV.do("tensor_scalar", qbuf[:, qsl * 512:(qsl + 1) * 512],
                                          ps_s[:, (m % 2) * 512:(m % 2 + 1) * 512],
                                          bcol[:, m:m + 1], 30.0 / 8.0, ALU.add, ALU.mult)
                                    M[("pevac", b, iw, j, m)] = SV.sem.val
                                    M[("qact", b, j * 8 + m)] = SV.sem.val

                                @blk.sync
                                def _(s, b=b, j=j, m=m):
                                    QS.bind(s)
                                    qsl = m % 2
                                    QS.w(SV.sem, M[("qact", b, j * 8 + m)])
                                    M[("qst", b, j * 8 + m)] = QS.dma(
                                        qtd.ap()[m, :, j * 512:(j + 1) * 512],
                                        qbuf[:, qsl * 512:(qsl + 1) * 512],
                                        dm["dqs0"] if qsl == 0 else dm["dqs1"])
                            else:
                                @blk.vector
                                def _(v, b=b, iw=iw, j=j, m=m):
                                    SV.bind(v)
                                    ts, dh = m // 2, m % 2
                                    SV.w(SP.sem, M[("pmmg", b, iw, j, m)])
                                    SV.do("tensor_tensor",
                                          vsb[:, (j * 4 + ts) * 1024 + dh * 512:(j * 4 + ts) * 1024 + (dh + 1) * 512],
                                          ps_s[:, (m % 2) * 512:(m % 2 + 1) * 512],
                                          bvbc[:, dh * 512:(dh + 1) * 512], ALU.add)
                                    M[("pevac", b, iw, j, m)] = SV.sem.val

            with nc.Block() as blk:
                for jq in range(8):
                    @blk.scalar
                    def _(sc, b=b, jq=jq):
                        QA.bind(sc)
                        if jq > 0:
                            QA.w(SP.sem, M[("smm", b, jq - 1, 7, 31)])
                        QA.w(dm["dqs0"], M[("qst", b, 62)])
                        QA.w(dm["dqs1"], M[("qst", b, 63)])
                        M[("ql", b, jq)] = QA.dma(
                            qts[:, :].rearrange("p (c t) -> p c t", c=8),
                            qtd.ap().rearrange("c p t -> p c t")[:, :, jq * 512:(jq + 1) * 512], dm["dql"])


                    for hh in range(8):
                        hA, hB = 2 * hh, 2 * hh + 1
                        sa0 = SA.sem.val
                        pv_prev = M.get(("pvg_last", b))

                        @blk.tensor
                        def _(t, b=b, jq=jq, hh=hh, hA=hA, hB=hB, sa0=sa0):
                            SP.bind(t)
                            if hh == 0:
                                SP.w(dm["dql"], M[("ql", b, jq)])

                            def emit_s(kc):
                                for hi, h in enumerate((hA, hB)):
                                    hp = 64 * hi
                                    bank = (2 * kc + hi) % 4
                                    if kc < 2:
                                        SP.w(SA.sem, sa0)
                                    else:
                                        SP.w(SA.sem, sa0 + kc - 1)
                                    SP.raw("matmul", ps_s[:, bank * 512:(bank + 1) * 512],
                                           ksb[hp:hp + 64, hh * L + kc * 128:hh * L + (kc + 1) * 128],
                                           qts[hp:hp + 64, hh * 512:(hh + 1) * 512],
                                           start=True, stop=True).then_inc(SP.sem.inc(1), 1)
                                M[("smm", b, jq, hh, kc)] = SP.sem.val

                            def emit_pv(kc):
                                SP.w(SA.sem, sa0 + kc + 1)
                                if kc == 0:
                                    pr = (M.get(("aevac", b, jq, hh - 1)) if hh > 0
                                          else M.get(("aevac", b, jq - 1, 7)))
                                    if pr:
                                        SP.w(SV.sem, pr)
                                sl = (kc % 2) * 1024
                                SP.raw("matmul", ps_pv[0:64, 0:512],
                                       vsb[:, kc * 1024 + 64 * hA:kc * 1024 + 64 * hA + 64],
                                       ptl[:, sl:sl + 512],
                                       start=(kc == 0), stop=(kc == 31)).then_inc(SP.sem.inc(1), 1)
                                SP.raw("matmul", ps_pv[64:128, 512:1024],
                                       vsb[:, kc * 1024 + 64 * hB:kc * 1024 + 64 * hB + 64],
                                       ptl[:, sl + 512:sl + 1024],
                                       start=(kc == 0), stop=(kc == 31),
                                       tile_position=(0, 64)).then_inc(SP.sem.inc(1), 1)
                                M[("pvmm", b, jq, hh, kc)] = SP.sem.val

                            for kc in range(32):
                                emit_s(kc)
                                if kc >= 2:
                                    emit_pv(kc - 2)
                            for kc in range(30, 32):
                                emit_pv(kc)
                            M[("pvg", b, jq, hh)] = SP.sem.val
                            M[("pvg_last", b)] = SP.sem.val

                        @blk.scalar
                        def _(sc, b=b, jq=jq, hh=hh, sa0=sa0, pv_prev=pv_prev):
                            SA.bind(sc)
                            for kc in range(32):
                                SA.w(SP.sem, M[("smm", b, jq, hh, kc)])
                                if kc >= 2:
                                    SA.w(SP.sem, M[("pvmm", b, jq, hh, kc - 2)])
                                elif pv_prev:
                                    SA.w(SP.sem, pv_prev)
                                sl2 = (2 * kc) % 4
                                SA.do("activation", ptl[:, (kc % 2) * 1024:(kc % 2 + 1) * 1024],
                                      ps_s[:, sl2 * 512:(sl2 + 2) * 512], AF.Sin)
                            assert SA.sem.val == sa0 + 32

                        @blk.vector
                        def _(v, b=b, jq=jq, hh=hh, hA=hA, hB=hB):
                            SV.bind(v)
                            SV.w(SP.sem, M[("pvg", b, jq, hh)])
                            SV.do("tensor_copy", aout[0:64, hh * 512:(hh + 1) * 512],
                                  ps_pv[0:64, 0:512])
                            SV.do("tensor_copy", aout[64:128, hh * 512:(hh + 1) * 512],
                                  ps_pv[64:128, 512:1024])
                            M[("aevac", b, jq, hh)] = SV.sem.val

                    @blk.scalar
                    def _(sc, b=b, jq=jq):
                        QA.bind(sc)
                        if jq == 0:
                            QA.w(dm["dwc"], M[("wc", 3)])
                            QA.w(SP.sem, M[("pmm", b, 2, 7)])
                            M[("wo_t", b)] = QA.dmat(wtb[:, :].rearrange("p (c e) -> p c e", c=8),
                                                     wbd["o"].ap(), dm["dwt"])
                    for g in range(8):
                        ts, dh = g // 2, g % 2

                        if dh == 0:
                            @blk.sync
                            def _(s, b=b, jq=jq, ts=ts):
                                QS.bind(s)
                                tsg = jq * 4 + ts
                                rsem = dm["dres0"] if ts % 2 == 0 else dm["dres1"]
                                if tsg >= 2:
                                    QS.w(SV.sem, M.get(("resfree", b, tsg - 2), 0))
                                M[("res", b, tsg)] = QS.dma(
                                    xsb[:, (ts % 2) * 1024:(ts % 2 + 1) * 1024],
                                    xin[B0 + jq * 512 + ts * 128:B0 + jq * 512 + (ts + 1) * 128, :], rsem)

                        @blk.tensor
                        def _(t, b=b, jq=jq, g=g, ts=ts, dh=dh):
                            SP.bind(t)
                            if g == 0:
                                SP.w(dm["dwt"], M[("wo_t", b)])
                                SP.w(SV.sem, M[("aevac", b, jq, 7)])
                            og = M.get(("oevac", b, jq, g - 2)) if g >= 2 else M.get(("oevac", b, jq - 1, 6 + g))
                            if og:
                                SP.w(SV.sem, og)
                            for m in range(8):
                                mi = SP.raw("matmul", ps_o[:, (g % 2) * 512:(g % 2 + 1) * 512],
                                            aout[:, m * 512 + ts * 128:m * 512 + (ts + 1) * 128],
                                            wtb[:, m * 1024 + dh * 512:m * 1024 + (dh + 1) * 512],
                                            start=(m == 0), stop=(m == 7))
                                if m == 7:
                                    mi.then_inc(SP.sem.inc(1), 1)
                            M[("omm", b, jq, g)] = SP.sem.val

                        @blk.vector
                        def _(v, b=b, jq=jq, g=g, ts=ts, dh=dh):
                            SV.bind(v)
                            SV.w(SP.sem, M[("omm", b, jq, g)])
                            tsg = jq * 4 + ts
                            rsem = dm["dres0"] if ts % 2 == 0 else dm["dres1"]
                            SV.w(rsem, M[("res", b, tsg)])
                            if tsg >= 1:
                                SV.w(dm["dost"], M.get(("ost", b, tsg - 1), 0))
                            elif b > 0:
                                SV.w(dm["dost"], M.get(("ost", b - 1, 31), 0))
                            SV.do("scalar_tensor_tensor", osb[:, dh * 512:(dh + 1) * 512],
                                  ps_o[:, (g % 2) * 512:(g % 2 + 1) * 512], 1.0,
                                  bobc[:, dh * 512:(dh + 1) * 512], ALU.mult, ALU.add)
                            SV.do("tensor_tensor", osb[:, dh * 512:(dh + 1) * 512],
                                  osb[:, dh * 512:(dh + 1) * 512],
                                  xsb[:, (ts % 2) * 1024 + dh * 512:(ts % 2) * 1024 + (dh + 1) * 512], ALU.add)
                            M[("oevac", b, jq, g)] = SV.sem.val
                            if dh == 1:
                                M[("resfree", b, tsg)] = SV.sem.val

                        if dh == 1:
                            @blk.sync
                            def _(s, b=b, jq=jq, ts=ts):
                                QS.bind(s)
                                QS.w(SV.sem, M[("oevac", b, jq, ts * 2 + 1)])
                                M[("ost", b, jq * 4 + ts)] = QS.dma(
                                    xout[B0 + jq * 512 + ts * 128:B0 + jq * 512 + (ts + 1) * 128, :],
                                    osb[:, :], dm["dost"])

        with nc.Block() as blk:
            @blk.sync
            def _(s):
                for _ds in dm.values():
                    if _ds.val:
                        s.wait_ge(_ds.h, _ds.val)

    return nc


def run_c(x2_full, inputs):
    nc = build_stage_c()
    maps = []
    for c in range(N_CORES):
        maps.append(dict(
            xin=np.ascontiguousarray(x2_full[c * TOK:(c + 1) * TOK]),
            wq=inputs["L1_attn_Wq"], bq=inputs["L1_attn_bq"].reshape(1, -1),
            wk=inputs["L1_attn_Wk"], bk=inputs["L1_attn_bk"].reshape(1, -1),
            wv=inputs["L1_attn_Wv"], bv=inputs["L1_attn_bv"].reshape(1, -1),
            wo=inputs["L1_attn_Wo"], bo=inputs["L1_attn_bo"].reshape(1, -1),
        ))
    res = run_bass_kernel_spmd(nc, maps, core_ids=list(range(N_CORES)))
    return np.concatenate([res.results[c]["xout"] for c in range(N_CORES)], axis=0)




def apv(base, extra_off, dims):
    return bass.AP(base.tensor, base.offset + extra_off, [list(base.ap[0])] + dims)



TA = 2176
NW = 16


def build_stage_a(nc=None, pf="", ext=None, ctx=None, phases=4):
    ext = ext or {}
    if nc is None:
        nc = bass.Bass("TRN2", target_bir_lowering=False)

    def exio(name, shape, dt, kind):
        return ext[name] if name in ext else nc.dram_tensor(name, shape, dt, kind=kind).ap()
    xin = exio("xin", [TOK, E], F32, "ExternalInput")
    wq1 = exio("wq1", [E, E], F32, "ExternalInput")
    wq2 = exio("wq2", [E, E], F32, "ExternalInput")
    wk1 = exio("wk1", [E, 128], F32, "ExternalInput")
    wk2 = exio("wk2", [E, 128], F32, "ExternalInput")
    wv = exio("wv", [E, 128], F32, "ExternalInput")
    wout = exio("wout", [E, E], F32, "ExternalInput")
    bout = exio("bout", [1, E], F32, "ExternalInput")
    lqk = exio("lqk", [1, 256], F32, "ExternalInput")
    sintok = exio("sintok", [TA, 32], F32, "ExternalInput")
    costok = exio("costok", [TA, 32], F32, "ExternalInput")
    xout = exio("xout", [TOK, E], F32, "ExternalOutput")

    xlnd = nc.dram_tensor(pf + "xlnd", [TA, E], BF16)
    q1d = nc.dram_tensor(pf + "q1d", [TA, E], BF16)
    q2d = nc.dram_tensor(pf + "q2d", [TA, E], BF16)
    k1d = nc.dram_tensor(pf + "k1d", [TA, 128], BF16)
    k2d = nc.dram_tensor(pf + "k2d", [TA, 128], BF16)
    rrd = nc.dram_tensor(pf + "rrd", [2, 512], BF16)

    with ExitStack() as stk:
        En = stk.enter_context
        if ctx is None:
            SV = Eng(Sem(En(nc.semaphore(pf + "sv"))))
            SA = Eng(Sem(En(nc.semaphore(pf + "sa"))))
            SG = Eng(Sem(En(nc.semaphore(pf + "sg"))))
            SP = Eng(Sem(En(nc.semaphore(pf + "sp"))), serialize=False)
        else:
            SV, SA, SG, SP = ctx["SV"], ctx["SA"], ctx["SG"], ctx["SP"]
        QS = Q()
        QA = Q()
        QG = Q()
        dm = {k: Sem(En(nc.semaphore(pf + k))) for k in
              ["dwl", "dx", "dxs0", "dxs1", "dxt0", "dxt1", "dq0", "dq1", "dk", "dql",
               "dost", "dmisc", "dres0", "dres1", "dqb1", "dqb2"]}

        onesf = En(nc.sbuf_tensor(pf + "onesf", [1, 128], F32))
        onesb = En(nc.sbuf_tensor(pf + "onesb", [128, 1], BF16))
        epsbc = En(nc.sbuf_tensor(pf + "epsbc", [128, 1], F32))
        wq1b = En(nc.sbuf_tensor(pf + "wq1b", [128, 8 * E], BF16))
        wq2b = En(nc.sbuf_tensor(pf + "wq2b", [128, 8 * E], BF16))
        wkvb = En(nc.sbuf_tensor(pf + "wkvb", [128, 3 * 8 * 128], BF16))
        woutb = En(nc.sbuf_tensor(pf + "woutb", [128, 8 * E], BF16))
        boutbc = En(nc.sbuf_tensor(pf + "boutbc", [128, E], F32))
        sctok = En(nc.sbuf_tensor(pf + "sctok", [128, 17 * 64], F32))
        lambc = En(nc.sbuf_tensor(pf + "lamb", [128, 2], F32))
        vsb = En(nc.sbuf_tensor(pf + "vsb", [128, 17 * 128], BF16))
        q1t = En(nc.sbuf_tensor(pf + "q1t", [128, 8 * TA], BF16))
        q2t = En(nc.sbuf_tensor(pf + "q2t", [128, 8 * TA], BF16))
        k1t = En(nc.sbuf_tensor(pf + "k1t", [128, TA], BF16))
        k2t = En(nc.sbuf_tensor(pf + "k2t", [128, TA], BF16))
        xts = En(nc.sbuf_tensor(pf + "xts", [128, 2 * 8 * 128], BF16))
        xlnsb = En(nc.sbuf_tensor(pf + "xlnsb", [128, 2 * 1024], BF16))
        xc2 = En(nc.sbuf_tensor(pf + "xc2", [128, 1024], F32))
        sqs = En(nc.sbuf_tensor(pf + "sqs", [128, 1024], BF16))
        red = En(nc.sbuf_tensor(pf + "red", [128, 16], F32))
        xsb = En(nc.sbuf_tensor(pf + "xsb", [128, 2048], F32))
        rope = En(nc.sbuf_tensor(pf + "rope", [128, 1024], F32))
        qrb = En(nc.sbuf_tensor(pf + "qrb", [128, 2 * 1024], BF16))
        kvb = En(nc.sbuf_tensor(pf + "kvb", [128, 256], BF16))
        ptl = En(nc.sbuf_tensor(pf + "ptl", [128, 2 * 1024], BF16))
        rrow = En(nc.sbuf_tensor(pf + "rrow", [1, 512], F32))
        rrbf = En(nc.sbuf_tensor(pf + "rrbf", [128, 1024], BF16))
        rbsb = En(nc.sbuf_tensor(pf + "rbsb", [128, 1024], BF16))
        scrT = En(nc.sbuf_tensor(pf + "scrT", [128, 512], F32))
        atb = En(nc.sbuf_tensor(pf + "atb", [128, 2 * 2048], BF16))
        praw = En(nc.sbuf_tensor(pf + "praw", [128, 4 * 512], BF16))
        osb = En(nc.sbuf_tensor(pf + "osb", [128, 1024], F32))
        ps_s = En(nc.psum_tensor([128, 4 * 512], F32))
        ps_pv = En(nc.psum_tensor([128, 2 * 512], F32))
        ps_o = En(nc.psum_tensor([128, 2 * 512], F32))

        M = {}

        with nc.Block() as blk:
            @blk.gpsimd
            def _(g):
                SG.bind(g)
                SG.do("memset", onesf[:, :], 1.0)
                SG.do("memset", onesb[:, :], 1.0)
                SG.do("memset", epsbc[:, :], 1e-5)
                M["ms"] = SG.sem.val

            @blk.gpsimd
            def _(g):
                QG.bind(g)
                QG.dma(wq1b[:, :].rearrange("p (c e) -> p c e", c=8),
                       wq1.rearrange("(c p) e -> p c e", p=128), dm["dwl"])
                QG.dma(wq2b[:, :].rearrange("p (c e) -> p c e", c=8),
                       wq2.rearrange("(c p) e -> p c e", p=128), dm["dwl"])
                QG.dma(wkvb[:, 0:1024].rearrange("p (c e) -> p c e", c=8),
                       wk1.rearrange("(c p) e -> p c e", p=128), dm["dwl"])
                QG.dma(wkvb[:, 1024:2048].rearrange("p (c e) -> p c e", c=8),
                       wk2.rearrange("(c p) e -> p c e", p=128), dm["dwl"])
                QG.dma(wkvb[:, 2048:3072].rearrange("p (c e) -> p c e", c=8),
                       wv.rearrange("(c p) e -> p c e", p=128), dm["dwl"])
                QG.dma(woutb[:, :].rearrange("p (c e) -> p c e", c=8),
                       wout.rearrange("(c p) e -> p c e", p=128), dm["dwl"])
                M["wl"] = dm["dwl"].val
                QG.dma(sctok[:, :].rearrange("p (j f) -> p j f", f=64).rearrange("p j (h f) -> p j h f", h=2)[:, :, 0],
                       sintok.rearrange("(j p) f -> p j f", p=128), dm["dmisc"])
                QG.dma(sctok[:, :].rearrange("p (j f) -> p j f", f=64).rearrange("p j (h f) -> p j h f", h=2)[:, :, 1],
                       costok.rearrange("(j p) f -> p j f", p=128), dm["dmisc"])
                QG.dma(osb[0:1, 0:256], lqk, dm["dmisc"])
                M["misc"] = QG.dma(xc2[0:1, 0:1024], bout, dm["dmisc"])

            @blk.vector
            def _(v):
                SV.bind(v)
                SV.w(dm["dmisc"], M["misc"])
                SV.do("tensor_tensor", rrow[0:1, 0:64], osb[0:1, 0:64], osb[0:1, 64:128], ALU.mult)
                SV.do("tensor_tensor", rrow[0:1, 64:128], osb[0:1, 128:192], osb[0:1, 192:256], ALU.mult)
                SV.do("tensor_reduce", rrow[0:1, 128:129], rrow[0:1, 0:64], AX.X, ALU.add)
                SV.do("tensor_reduce", rrow[0:1, 129:130], rrow[0:1, 64:128], AX.X, ALU.add)
                M["lam1"] = SV.sem.val

            @blk.scalar
            def _(sc):
                SA.bind(sc)
                SA.w(SV.sem, M["lam1"])
                SA.do("activation", rrow[0:1, 130:131], rrow[0:1, 128:129], AF.Exp)
                SA.do("activation", rrow[0:1, 131:132], rrow[0:1, 129:130], AF.Exp)
                M["lam2"] = SA.sem.val

            @blk.vector
            def _(v):
                SV.bind(v)
                SV.w(SA.sem, M["lam2"])
                SV.do("tensor_tensor", rrow[0:1, 132:133], rrow[0:1, 130:131], rrow[0:1, 131:132], ALU.subtract)
                SV.do("tensor_scalar", rrow[0:1, 132:133], rrow[0:1, 132:133], 0.8, 0.9, ALU.add, ALU.min)
                SV.do("tensor_scalar", rrow[0:1, 132:133], rrow[0:1, 132:133], 0.1, -1.0, ALU.max, ALU.mult)
                M["lam3"] = SV.sem.val

            @blk.tensor
            def _(t):
                SP.bind(t)
                SP.w(SV.sem, M["lam3"])
                SP.w(SG.sem, M["ms"])
                SP.raw("matmul", ps_s[:, 0:512], onesf[:1, :], xc2[:1, 0:512], start=True, stop=True)
                SP.raw("matmul", ps_s[:, 512:1024], onesf[:1, :], xc2[:1, 512:1024], start=True, stop=True)
                SP.do("matmul", ps_o[:, 0:1], onesf[:1, :], rrow[:1, 132:133], start=True, stop=True)
                M["bmm"] = SP.sem.val

            @blk.vector
            def _(v):
                SV.bind(v)
                SV.w(SP.sem, M["bmm"])
                SV.do("tensor_copy", boutbc[:, :], ps_s[:, 0:1024])
                SV.do("tensor_copy", lambc[:, 0:1], ps_o[:, 0:1])
                M["bprep"] = SV.sem.val

        for b in range(BPC):
            B0 = b * L
            with nc.Block() as blk:
                for u in range(17):
                    sidx = u
                    ssl = u % 2
                    xs_sem = dm["dxs0"] if ssl == 0 else dm["dxs1"]
                    xl = u % 2

                    @blk.sync
                    def _(s, b=b, u=u, xl=xl, sidx=sidx):
                        QS.bind(s)
                        if u >= 2:
                            QS.w(SV.sem, M[("a1stt", b, u - 2)])
                        elif b > 0:
                            QS.w(SV.sem, M.get(("oevac", b - 1, NW - 1, 3), 0))
                        M[("a1x", b, u)] = QS.dma(xsb[:, xl * 1024:(xl + 1) * 1024],
                                                  xin[B0 + u * 128:B0 + (u + 1) * 128, :], dm["dx"])

                    @blk.vector
                    def _(v, b=b, u=u, xl=xl, sidx=sidx):
                        SV.bind(v)
                        x_ap = xsb[:, xl * 1024:(xl + 1) * 1024]
                        sl = sidx % 2
                        r = [red[:, sl * 8 + k:sl * 8 + k + 1] for k in range(5)]
                        SV.w(dm["dx"], M[("a1x", b, u)])
                        if b == 0 and u == 0:
                            SV.w(SP.sem, M["bmm"])
                        if sidx >= 1:
                            SV.w(SA.sem, M[("a1xln", b, sidx - 1)])
                        SV.do("tensor_reduce", r[0], x_ap, AX.X, ALU.add)
                        SV.do("tensor_scalar", r[1], r[0], -1.0 / E, None, ALU.mult)
                        SV.do("tensor_scalar", xc2[:, :], x_ap, r[1], None, ALU.add)
                        SV.do("scalar_tensor_tensor", sqs[:, :], xc2[:, :], 1.0, xc2[:, :],
                              ALU.mult, ALU.mult, accum_out=r[2])
                        M[("a1stt", b, sidx)] = SV.sem.val

                    @blk.scalar
                    def _(sc, b=b, sidx=sidx, ssl=ssl, xs_sem=xs_sem):
                        SA.bind(sc)
                        sl = sidx % 2
                        r = [red[:, sl * 8 + k:sl * 8 + k + 1] for k in range(5)]
                        SA.w(SV.sem, M[("a1stt", b, sidx)])
                        if sidx >= 2:
                            SA.w(xs_sem, M[("a1xst", b, sidx - 2)])
                        SA.do("activation", r[3], r[2], AF.Ln, bias=epsbc[:, 0:1], scale=1.0 / E)
                        SA.do("activation", r[4], r[3], AF.Exp, bias=0.0, scale=-0.5)
                        SA.do("activation", xlnsb[:, ssl * 1024:(ssl + 1) * 1024],
                              xc2[:, :], AF.Copy, bias=0.0, scale=r[4])
                        M[("a1xln", b, sidx)] = SA.sem.val

                    @blk.sync
                    def _(s, b=b, u=u, sidx=sidx, ssl=ssl, xs_sem=xs_sem):
                        QS.bind(s)
                        QS.w(SA.sem, M[("a1xln", b, sidx)])
                        M[("a1xst", b, sidx)] = QS.dma(
                            xlnd.ap()[u * 128:(u + 1) * 128, :],
                            xlnsb[:, ssl * 1024:(ssl + 1) * 1024], xs_sem)

            if phases < 2:
                continue
            with nc.Block() as blk:
                for j in range(17):
                    jsl = j % 2

                    @blk.scalar
                    def _(sc, b=b, j=j, jsl=jsl):
                        QA.bind(sc)
                        QA.w(dm["dxs0"], M[("a1xst", b, 16)])
                        QA.w(dm["dxs1"], M[("a1xst", b, 15)])
                        if j >= 2:
                            QA.w(SP.sem, M[("a2mm", b, j - 2)])
                        M[("a2xt", b, j)] = QA.dmat(
                            xts[:, jsl * 1024:(jsl + 1) * 1024].rearrange("p (c t) -> p c t", c=8),
                            xlnd.ap()[j * 128:(j + 1) * 128, :],
                            dm["dxt0"] if jsl == 0 else dm["dxt1"])

                    for iw in range(3):
                        @blk.tensor
                        def _(t, b=b, j=j, jsl=jsl, iw=iw):
                            SP.bind(t)
                            SP.w(dm["dxt0"] if jsl == 0 else dm["dxt1"], M[("a2xt", b, j)])
                            if b == 0 and j == 0 and iw == 0:
                                SP.w(dm["dwl"], M["wl"])
                                SP.w(SV.sem, M["bprep"])
                            if iw < 2:
                                wsb = wq1b if iw == 0 else wq2b
                                for dh in range(2):
                                    pr = M.get(("a2evac", b, j, 0, dh)) if iw == 1 else \
                                        M.get(("a2evac", b, j - 1, 1, dh))
                                    if pr:
                                        SP.w(SV.sem, pr)
                                    for c in range(8):
                                        mi = SP.raw("matmul", ps_s[:, dh * 512:(dh + 1) * 512],
                                                    xts[:, jsl * 1024 + c * 128:jsl * 1024 + (c + 1) * 128],
                                                    wsb[:, c * 1024 + dh * 512:c * 1024 + (dh + 1) * 512],
                                                    start=(c == 0), stop=(c == 7))
                                        if c == 7:
                                            mi.then_inc(SP.sem.inc(1), 1)
                                    M[("a2mmg", b, j, iw, dh)] = SP.sem.val
                            else:
                                for kk in range(3):
                                    pr = M.get(("a2evack", b, j - 1, kk))
                                    if pr:
                                        SP.w(SV.sem, pr)
                                    for c in range(8):
                                        mi = SP.raw("matmul", ps_pv[:, kk * 128:(kk + 1) * 128],
                                                    xts[:, jsl * 1024 + c * 128:jsl * 1024 + (c + 1) * 128],
                                                    wkvb[:, kk * 1024 + c * 128:kk * 1024 + (c + 1) * 128],
                                                    start=(kk == 0 and c == 0), stop=(kk == 2 and c == 7))
                                        if c == 7:
                                            mi.then_inc(SP.sem.inc(1), 1)
                                    M[("a2mmk", b, j, kk)] = SP.sem.val
                            if iw == 2:
                                M[("a2mm", b, j)] = SP.sem.val

                        if iw < 2:
                            @blk.vector
                            def _(v, b=b, j=j, iw=iw):
                                SV.bind(v)
                                for dh in range(2):
                                    SV.w(SP.sem, M[("a2mmg", b, j, iw, dh)])
                                    if (b, j) != (0, 0):
                                        SV.w(dm["dq0" if iw == 0 else "dq1"], M.get(("a2qst", b, j - 1, iw), 0))
                                    ps = ps_s[:, dh * 512:(dh + 1) * 512]
                                    qe = apv(ps, 0, [[64, 8], [2, 32]])
                                    qo = apv(ps, 1, [[64, 8], [2, 32]])
                                    sct = sctok[:, 0:1]
                                    sin8 = apv(sct, j * 64, [[0, 8], [1, 32]])
                                    cos8 = apv(sct, j * 64 + 32, [[0, 8], [1, 32]])
                                    out = qrb[:, iw * 1024 + dh * 512:iw * 1024 + (dh + 1) * 512]
                                    oute = apv(out, 0, [[64, 8], [1, 32]])
                                    outo = apv(out, 32, [[64, 8], [1, 32]])
                                    s1 = apv(rope[:, 0:1], 0, [[32, 8], [1, 32]])
                                    s2 = apv(rope[:, 0:1], 256, [[32, 8], [1, 32]])
                                    SV.do("tensor_tensor", s1, qe, cos8, ALU.mult)
                                    SV.do("tensor_tensor", s2, qo, sin8, ALU.mult)
                                    SV.do("tensor_tensor", oute, s1, s2, ALU.subtract)
                                    SV.do("tensor_tensor", s1, qe, sin8, ALU.mult)
                                    SV.do("tensor_tensor", s2, qo, cos8, ALU.mult)
                                    SV.do("tensor_tensor", outo, s1, s2, ALU.add)
                                    M[("a2evac", b, j, iw, dh)] = SV.sem.val
                                M[("a2rope", b, j, iw)] = SV.sem.val

                            @blk.sync
                            def _(s, b=b, j=j, iw=iw):
                                QS.bind(s)
                                QS.w(SV.sem, M[("a2rope", b, j, iw)])
                                qd = q1d if iw == 0 else q2d
                                M[("a2qst", b, j, iw)] = QS.dma(
                                    qd.ap()[j * 128:(j + 1) * 128, :],
                                    qrb[:, iw * 1024:(iw + 1) * 1024], dm["dq0" if iw == 0 else "dq1"])
                        else:
                            @blk.vector
                            def _(v, b=b, j=j):
                                SV.bind(v)
                                for kk in range(2):
                                    SV.w(SP.sem, M[("a2mmk", b, j, kk)])
                                    if (b, j) != (0, 0):
                                        SV.w(dm["dk"], M.get(("a2kst", b, j - 1), 0))
                                    ps = ps_pv[:, kk * 128:(kk + 1) * 128]
                                    qe = apv(ps, 0, [[64, 2], [2, 32]])
                                    qo = apv(ps, 1, [[64, 2], [2, 32]])
                                    sct = sctok[:, 0:1]
                                    sin2 = apv(sct, j * 64, [[0, 2], [1, 32]])
                                    cos2 = apv(sct, j * 64 + 32, [[0, 2], [1, 32]])
                                    out = kvb[:, kk * 128:(kk + 1) * 128]
                                    oute = apv(out, 0, [[64, 2], [1, 32]])
                                    outo = apv(out, 32, [[64, 2], [1, 32]])
                                    s1 = apv(rope[:, 0:1], 0, [[32, 2], [1, 32]])
                                    s2 = apv(rope[:, 0:1], 64, [[32, 2], [1, 32]])
                                    SV.do("tensor_tensor", s1, qe, cos2, ALU.mult)
                                    SV.do("tensor_tensor", s2, qo, sin2, ALU.mult)
                                    SV.do("tensor_tensor", oute, s1, s2, ALU.subtract)
                                    SV.do("tensor_tensor", s1, qe, sin2, ALU.mult)
                                    SV.do("tensor_tensor", s2, qo, cos2, ALU.mult)
                                    SV.do("tensor_tensor", outo, s1, s2, ALU.add)
                                SV.w(SP.sem, M[("a2mmk", b, j, 2)])
                                SV.do("tensor_copy", vsb[:, j * 128:(j + 1) * 128], ps_pv[:, 256:384])
                                M[("a2evack", b, j, 0)] = SV.sem.val
                                M[("a2evack", b, j, 1)] = SV.sem.val
                                M[("a2evack", b, j, 2)] = SV.sem.val
                                M[("a2kv", b, j)] = SV.sem.val

                            @blk.sync
                            def _(s, b=b, j=j):
                                QS.bind(s)
                                QS.w(SV.sem, M[("a2kv", b, j)])
                                QS.dma(k1d.ap()[j * 128:(j + 1) * 128, :], kvb[:, 0:128], dm["dk"])
                                M[("a2kst", b, j)] = QS.dma(k2d.ap()[j * 128:(j + 1) * 128, :],
                                                            kvb[:, 128:256], dm["dk"])

            if phases < 3:
                continue
            with nc.Block() as blk:
                @blk.scalar
                def _(sc, b=b):
                    QA.bind(sc)
                    QA.w(dm["dq0"], M[("a2qst", b, 16, 0)])
                    QA.w(dm["dq1"], M[("a2qst", b, 16, 1)])
                    QA.w(dm["dk"], M[("a2kst", b, 16)])
                    if b > 0:
                        QA.w(SP.sem, M.get(("wmm", b - 1, NW - 1, 3), 0))
                    QA.dmat(q1t[:, :].rearrange("p (c t) -> p c t", c=8), q1d.ap(), dm["dql"])
                    QA.dmat(q2t[:, :].rearrange("p (c t) -> p c t", c=8), q2d.ap(), dm["dql"])
                    QA.dmat(k1t[:, :].rearrange("p (c t) -> p c t", c=1), k1d.ap(), dm["dql"])
                    M[("a3", b)] = QA.dmat(k2t[:, :].rearrange("p (c t) -> p c t", c=1), k2d.ap(), dm["dql"])

            if phases < 4:
                continue
            with nc.Block() as blk:
                NU = NW * 16

                def pe_stile(u, b=b):
                    n, h = u // 16, u % 16
                    g = u % 2
                    if u == 0:
                        SP.w(dm["dql"], M[("a3", b)])
                        SP.w(SV.sem, M["bprep"])
                    if u >= 2:
                        SP.w(SA.sem, M[("exp", b, u - 2)])
                    hp = 64 * (h % 2)
                    hc = h // 2
                    for i in range(2):
                        kt = k1t if i == 0 else k2t
                        qt = q1t if i == 0 else q2t
                        for kc in range(2):
                            mi = SP.raw("matmul",
                                        ps_s[:, g * 1024 + i * 512 + kc * 256:
                                             g * 1024 + i * 512 + (kc + 1) * 256],
                                        kt[hp:hp + 64, n * 128 + kc * 128:n * 128 + (kc + 1) * 128],
                                        qt[hp:hp + 64, hc * TA + n * 128:hc * TA + n * 128 + 256],
                                        start=(kc == 0), stop=(kc == 1))
                            if i == 1 and kc == 1:
                                mi.then_inc(SP.sem.inc(1), 1)
                    M[("smm", b, u)] = SP.sem.val

                def pe_sm_pv(v, b=b):
                    gv = v % 2
                    pvg_ = v % 2
                    nv, hv = v // 16, v % 16
                    SP.w(SA.sem, M[("exp", b, v)])
                    if v >= 2:
                        SP.w(SA.sem, M[("rr", b, v - 2)])
                    for i in range(2):
                        for kc in range(2):
                            mi = SP.raw("matmul",
                                        ps_pv[64:65, pvg_ * 512 + i * 256:pvg_ * 512 + (i + 1) * 256],
                                        onesb[:, 0:1],
                                        ptl[:, gv * 1024 + i * 512 + kc * 256:
                                            gv * 1024 + i * 512 + (kc + 1) * 256],
                                        start=(i == 0 and kc == 0), stop=(i == 1 and kc == 1),
                                        tile_position=(0, 64))
                            if i == 1 and kc == 1:
                                mi.then_inc(SP.sem.inc(1), 1)
                    M[("summ", b, v)] = SP.sem.val
                    if v >= 2:
                        SP.w(SV.sem, M[("aev", b, v - 2)])
                    for i in range(2):
                        for kc in range(2):
                            mi = SP.raw("matmul",
                                        ps_pv[0:64, pvg_ * 512 + i * 256:pvg_ * 512 + (i + 1) * 256],
                                        vsb[:, (nv + kc) * 128 + 64 * (hv % 2):
                                            (nv + kc) * 128 + 64 * (hv % 2) + 64],
                                        ptl[:, gv * 1024 + i * 512 + kc * 256:
                                            gv * 1024 + i * 512 + (kc + 1) * 256],
                                        start=(i == 0 and kc == 0), stop=(i == 1 and kc == 1))
                            if i == 1 and kc == 1:
                                mi.then_inc(SP.sem.inc(1), 1)
                    M[("pvg", b, v)] = SP.sem.val

                def pe_wout(w, gidx, b=b):
                    ts, dh = gidx // 2, gidx % 2
                    if gidx == 0:
                        SP.w(SV.sem, M[("cb", b, w * 16 + 15)])
                    pr = M.get(("oevac", b, w, gidx - 2)) if gidx >= 2 else \
                        M.get(("oevac", b, w - 1, 2 + gidx))
                    if pr:
                        SP.w(SV.sem, pr)
                    ab = (w % 2) * 2048
                    for m in range(8):
                        mi = SP.raw("matmul", ps_o[:, (gidx % 2) * 512:(gidx % 2 + 1) * 512],
                                    atb[:, ab + m * 256 + ts * 128:ab + m * 256 + ts * 128 + 128],
                                    woutb[:, m * 1024 + dh * 512:m * 1024 + (dh + 1) * 512],
                                    start=(m == 0), stop=(m == 7))
                        if m == 7:
                            mi.then_inc(SP.sem.inc(1), 1)
                    M[("wmm", b, w, gidx)] = SP.sem.val

                def sa_e(u, b=b, fut=None):
                    g = u % 2
                    SA.w(SP.sem, fut if fut is not None else M[("smm", b, u)])
                    if u >= 2:
                        SA.w(SP.sem, M[("pvg", b, u - 2)])
                    SA.do("activation", ptl[:, g * 1024:(g + 1) * 1024],
                          ps_s[:, g * 1024:(g + 1) * 1024], AF.Exp, bias=0.0, scale=0.125)
                    M[("exp", b, u)] = SA.sem.val

                def sa_r(v, b=b, fut2=None):
                    gv = v % 2
                    SA.w(SP.sem, fut2 if fut2 is not None else M[("summ", b, v)])
                    if v >= 2:
                        SA.w(dm["dqb1"], M[("qb1", b, v - 2)])
                    SA.do("activation", xc2[64:65, 0:512],
                          ps_pv[64:65, (v % 2) * 512:(v % 2) * 512 + 512], AF.Ln)
                    SA.do("activation", rrbf[64:65, gv * 512:(gv + 1) * 512],
                          xc2[64:65, 0:512], AF.Exp, bias=0.0, scale=-1.0)
                    M[("rr", b, v)] = SA.sem.val

                def qs_qb1(v, b=b):
                    gv = v % 2
                    QS.w(SA.sem, M[("rr", b, v)])
                    if v >= 2:
                        QS.w(dm["dqb2"], M[("qb2", b, v - 2)])
                    M[("qb1", b, v)] = QS.dma(rrd.ap()[gv:gv + 1, :],
                                              rrbf[64:65, gv * 512:(gv + 1) * 512], dm["dqb1"])

                def qs_qb2(v, b=b):
                    gv = v % 2
                    QS.w(dm["dqb1"], M[("qb1", b, v)])
                    if v >= 2:
                        QS.w(SV.sem, M[("cb", b, v - 2)])
                    M[("qb2", b, v)] = QS.dma(
                        rbsb[0:64, gv * 512:(gv + 1) * 512],
                        bass.AP(rrd.ap().tensor, gv * 512, [[0, 64], [1, 512]]), dm["dqb2"])

                def qs_res(w, ts, b=b):
                    tsg = w * 2 + ts
                    rsem = dm["dres0"] if ts % 2 == 0 else dm["dres1"]
                    if tsg >= 2:
                        QS.w(SV.sem, M.get(("oevac", b, (tsg - 2) // 2, ((tsg - 2) % 2) * 2 + 1), 0))
                    M[("res", b, tsg)] = QS.dma(
                        xsb[:, (ts % 2) * 1024:(ts % 2 + 1) * 1024],
                        xin[B0 + w * 256 + ts * 128:B0 + w * 256 + (ts + 1) * 128, :], rsem)

                def qs_ost(w, ts, b=b):
                    QS.w(SV.sem, M[("oevac", b, w, ts * 2 + 1)])
                    M[("ost", b, w * 2 + ts)] = QS.dma(
                        xout[B0 + w * 256 + ts * 128:B0 + w * 256 + (ts + 1) * 128, :],
                        osb[:, :], dm["dost"])

                def dve_aev(v, b=b):
                    pvg_ = v % 2
                    SV.w(SP.sem, M[("pvg", b, v)])
                    if v >= 4:
                        SV.w(SV.sem, M[("cb", b, v - 4)])
                    SV.do("tensor_copy", praw[0:64, (v % 4) * 512:(v % 4 + 1) * 512],
                          ps_pv[0:64, pvg_ * 512:(pvg_ + 1) * 512])
                    M[("aev", b, v)] = SV.sem.val

                def dve_cb(v, b=b):
                    gv = v % 2
                    sl4 = v % 4
                    nv, hv = v // 16, v % 16
                    hp = 64 * (hv % 2)
                    hc = hv // 2
                    SV.w(dm["dqb2"], M[("qb2", b, v)])
                    if nv >= 2:
                        SV.w(SP.sem, M[("wmm", b, nv - 2, 3)])
                    ab = (nv % 2) * 2048
                    SV.do("tensor_tensor", scrT[0:64, 0:256],
                          praw[0:64, sl4 * 512:sl4 * 512 + 256],
                          rbsb[0:64, gv * 512:gv * 512 + 256], ALU.mult)
                    SV.do("tensor_tensor", scrT[0:64, 256:512],
                          praw[0:64, sl4 * 512 + 256:sl4 * 512 + 512],
                          rbsb[0:64, gv * 512 + 256:gv * 512 + 512], ALU.mult)
                    SV.do("scalar_tensor_tensor", atb[hp:hp + 64, ab + hc * 256:ab + (hc + 1) * 256],
                          scrT[0:64, 256:512], lambc[0:64, 0:1], scrT[0:64, 0:256],
                          ALU.mult, ALU.add)
                    M[("cb", b, v)] = SV.sem.val

                def dve_oevac(w, gidx, b=b):
                    ts, dh = gidx // 2, gidx % 2
                    tsg = w * 2 + ts
                    rsem = dm["dres0"] if ts % 2 == 0 else dm["dres1"]
                    SV.w(SP.sem, M[("wmm", b, w, gidx)])
                    SV.w(rsem, M[("res", b, tsg)])
                    if tsg >= 1:
                        SV.w(dm["dost"], M.get(("ost", b, tsg - 1), 0))
                    elif b > 0:
                        SV.w(dm["dost"], M.get(("ost", b - 1, 2 * NW - 1), 0))
                    SV.do("scalar_tensor_tensor", osb[:, dh * 512:(dh + 1) * 512],
                          ps_o[:, (gidx % 2) * 512:(gidx % 2 + 1) * 512], 1.0,
                          boutbc[:, dh * 512:(dh + 1) * 512], ALU.mult, ALU.add)
                    SV.do("tensor_tensor", osb[:, dh * 512:(dh + 1) * 512],
                          osb[:, dh * 512:(dh + 1) * 512],
                          xsb[:, (ts % 2) * 1024 + dh * 512:(ts % 2) * 1024 + (dh + 1) * 512],
                          ALU.add)
                    M[("oevac", b, w, gidx)] = SV.sem.val

                for u in range(NU):
                    n, h = u // 16, u % 16

                    @blk.tensor
                    def _(t, b=b, u=u, n=n, h=h):
                        SP.bind(t)
                        pe_stile(u)
                        if u >= 1:
                            pe_sm_pv(u - 1)
                        if n >= 1 and 8 <= h <= 11:
                            pe_wout(n - 1, h - 8)

                    if u % 2 == 0:
                        @blk.scalar
                        def _(sc, b=b, u=u):
                            SA.bind(sc)
                            spv = SP.sem.val
                            sa_e(u)
                            if u >= 1:
                                sa_r(u - 1)
                            if u + 1 < NU:
                                sa_e(u + 1, fut=spv + 1)
                                M[("exp", b, u + 1)] = SA.sem.val
                                sa_r(u, fut2=spv + 2)

                    @blk.sync
                    def _(s, b=b, u=u, n=n, h=h):
                        QS.bind(s)
                        if u >= 1:
                            qs_qb1(u - 1)
                        if u >= 2:
                            qs_qb2(u - 2)
                        if n >= 1 and h == 0:
                            qs_res(n - 1, 0)
                        if n >= 1 and h == 2:
                            qs_res(n - 1, 1)
                        if n >= 1 and h == 11:
                            qs_ost(n - 1, 0)
                        if n >= 1 and h == 13:
                            qs_ost(n - 1, 1)

                    @blk.vector
                    def _(v, b=b, u=u, n=n, h=h):
                        SV.bind(v)
                        if u >= 1:
                            dve_aev(u - 1)
                        if u >= 2:
                            dve_cb(u - 2)
                        if n >= 1 and 9 <= h <= 12:
                            dve_oevac(n - 1, h - 9)

                @blk.tensor
                def _(t, b=b):
                    SP.bind(t)
                    pe_sm_pv(NU - 1)

                @blk.scalar
                def _(sc, b=b):
                    SA.bind(sc)
                    sa_r(NU - 1)

                @blk.sync
                def _(s, b=b):
                    QS.bind(s)
                    qs_qb1(NU - 1)
                    qs_qb2(NU - 2)

                @blk.vector
                def _(v, b=b):
                    SV.bind(v)
                    dve_aev(NU - 1)
                    dve_cb(NU - 2)

                @blk.sync
                def _(s, b=b):
                    QS.bind(s)
                    qs_qb2(NU - 1)
                    qs_res(NW - 1, 0)
                    qs_res(NW - 1, 1)

                @blk.vector
                def _(v, b=b):
                    SV.bind(v)
                    dve_cb(NU - 1)

                for gidx in range(4):
                    @blk.tensor
                    def _(t, b=b, gidx=gidx):
                        SP.bind(t)
                        pe_wout(NW - 1, gidx)

                    @blk.vector
                    def _(v, b=b, gidx=gidx):
                        SV.bind(v)
                        dve_oevac(NW - 1, gidx)

                    if gidx == 1:
                        @blk.sync
                        def _(s, b=b):
                            QS.bind(s)
                            qs_ost(NW - 1, 0)

                @blk.sync
                def _(s, b=b):
                    QS.bind(s)
                    qs_ost(NW - 1, 1)

        with nc.Block() as blk:
            @blk.sync
            def _(s):
                for _ds in dm.values():
                    if _ds.val:
                        s.wait_ge(_ds.h, _ds.val)

    return nc

def _rope_tables():
    pos = np.arange(TA, dtype=np.float64)[:, None]
    div = np.exp(np.arange(0, 64, 2, dtype=np.float64) * (-np.log(10000.0) / 64))
    return (np.sin(pos * div).astype(np.float32), np.cos(pos * div).astype(np.float32))


def run_a(x_full, inputs):
    nc = build_stage_a()
    sintab, costab = _rope_tables()
    maps = []
    for c in range(N_CORES):
        maps.append(dict(
            xin=np.ascontiguousarray(x_full[c * TOK:(c + 1) * TOK]),
            wq1=inputs["L0_attn_Wq1"], wq2=inputs["L0_attn_Wq2"],
            wk1=inputs["L0_attn_Wk1"], wk2=inputs["L0_attn_Wk2"],
            wv=inputs["L0_attn_Wv"], wout=inputs["L0_attn_Wout"],
            bout=inputs["L0_attn_bout"].reshape(1, -1),
            lqk=np.concatenate([inputs["L0_attn_lq1"], inputs["L0_attn_lk1"],
                                inputs["L0_attn_lq2"], inputs["L0_attn_lk2"]]).reshape(1, -1),
            sintok=sintab, costok=costab,
        ))
    res = run_bass_kernel_spmd(nc, maps, core_ids=list(range(N_CORES)))
    return np.concatenate([res.results[c]["xout"] for c in range(N_CORES)], axis=0)





CEXP = 16384
NB = 512


def build_stage_d1(nc=None, pf="", ext=None, ctx=None, phases=4):
    ext = ext or {}
    if nc is None:
        nc = bass.Bass("TRN2", target_bir_lowering=False)

    def exio(name, shape, dt, kind):
        return ext[name] if name in ext else nc.dram_tensor(name, shape, dt, kind=kind).ap()
    xin = exio("xin", [TOK, E], F32, "ExternalInput")
    wg = exio("wg", [E, 8], F32, "ExternalInput")
    xln3 = exio("xln3", [TOK, E], BF16, "ExternalOutput")
    lgw = exio("lgw", [128, 512], F32, "ExternalOutput")
    xlnd = nc.dram_tensor(pf + "xlnd", [TOK, E], BF16)

    with ExitStack() as stk:
        En = stk.enter_context
        if ctx is None:
            SV = Eng(Sem(En(nc.semaphore(pf + "sv"))))
            SA = Eng(Sem(En(nc.semaphore(pf + "sa"))))
            SG = Eng(Sem(En(nc.semaphore(pf + "sg"))))
            SP = Eng(Sem(En(nc.semaphore(pf + "sp"))), serialize=False)
        else:
            SV, SA, SG, SP = ctx["SV"], ctx["SA"], ctx["SG"], ctx["SP"]
        QS = Q()
        QA = Q()
        QG = Q()
        dm = {k: Sem(En(nc.semaphore(pf + k))) for k in
              ["dx", "dxs0", "dxs1", "dxt0", "dxt1", "dwl", "dlg"]}

        epsbc = En(nc.sbuf_tensor(pf + "epsbc", [128, 1], F32))
        wgb = En(nc.sbuf_tensor(pf + "wgb", [128, 8 * 8], BF16))
        xlnsb = En(nc.sbuf_tensor(pf + "xlnsb", [128, 2 * 1024], BF16))
        xc2 = En(nc.sbuf_tensor(pf + "xc2", [128, 2 * 1024], F32))
        sqs = En(nc.sbuf_tensor(pf + "sqs", [128, 1024], BF16))
        red = En(nc.sbuf_tensor(pf + "red", [128, 16], F32))
        xsb = En(nc.sbuf_tensor(pf + "xsb", [128, 2048], F32))
        xts = En(nc.sbuf_tensor(pf + "xts", [128, 2 * 1024], BF16))
        lgsb = En(nc.sbuf_tensor(pf + "lgsb", [128, 512], F32))
        ps = En(nc.psum_tensor([128, 16], F32))

        M = {}
        with nc.Block() as blk:
            @blk.gpsimd
            def _(g):
                SG.bind(g)
                SG.do("memset", epsbc[:, :], 1e-5)
                M["ms"] = SG.sem.val

            @blk.gpsimd
            def _(g):
                QG.bind(g)
                M["wl"] = QG.dma(wgb[:, :].rearrange("p (c e) -> p c e", c=8),
                                 wg.rearrange("(c p) e -> p c e", p=128), dm["dwl"])

        with nc.Block() as blk:
            for u in range(TOK // 128):
                ssl = u % 2
                xs_sem = dm["dxs0"] if ssl == 0 else dm["dxs1"]

                @blk.sync
                def _(s, u=u, ssl=ssl):
                    QS.bind(s)
                    if u >= 2:
                        QS.w(SV.sem, M[("stt", u - 2)])
                    M[("x", u)] = QS.dma(xsb[:, ssl * 1024:(ssl + 1) * 1024],
                                         xin[u * 128:(u + 1) * 128, :], dm["dx"])

                @blk.vector
                def _(v, u=u, ssl=ssl):
                    SV.bind(v)
                    x_ap = xsb[:, ssl * 1024:(ssl + 1) * 1024]
                    r = [red[:, ssl * 8 + k:ssl * 8 + k + 1] for k in range(5)]
                    SV.w(dm["dx"], M[("x", u)])
                    if u >= 2:
                        SV.w(SA.sem, M[("xln", u - 2)])
                    xc = xc2[:, ssl * 1024:(ssl + 1) * 1024]
                    SV.do("tensor_reduce", r[0], x_ap, AX.X, ALU.add)
                    SV.do("tensor_scalar", r[1], r[0], -1.0 / E, None, ALU.mult)
                    SV.do("tensor_scalar", xc, x_ap, r[1], None, ALU.add)
                    SV.do("scalar_tensor_tensor", sqs[:, :], xc, 1.0, xc,
                          ALU.mult, ALU.mult, accum_out=r[2])
                    M[("stt", u)] = SV.sem.val

                @blk.scalar
                def _(sc, u=u, ssl=ssl, xs_sem=xs_sem):
                    SA.bind(sc)
                    r = [red[:, ssl * 8 + k:ssl * 8 + k + 1] for k in range(5)]
                    SA.w(SV.sem, M[("stt", u)])
                    SA.w(SG.sem, M["ms"])
                    if u >= 2:
                        SA.w(xs_sem, M[("xst", u - 2)])
                    SA.do("activation", r[3], r[2], AF.Ln, bias=epsbc[:, 0:1], scale=1.0 / E)
                    SA.do("activation", r[4], r[3], AF.Exp, bias=0.0, scale=-0.5)
                    SA.do("activation", xlnsb[:, ssl * 1024:(ssl + 1) * 1024],
                          xc2[:, ssl * 1024:(ssl + 1) * 1024], AF.Copy, bias=0.0, scale=r[4])
                    M[("xln", u)] = SA.sem.val

                @blk.sync
                def _(s, u=u, ssl=ssl, xs_sem=xs_sem):
                    QS.bind(s)
                    QS.w(SA.sem, M[("xln", u)])
                    QS.dma(xln3[u * 128:(u + 1) * 128, :], xlnsb[:, ssl * 1024:(ssl + 1) * 1024], xs_sem)
                    QS.dma(xlnd.ap()[u * 128:(u + 1) * 128, :],
                           xlnsb[:, ssl * 1024:(ssl + 1) * 1024], xs_sem)
                    M[("xst", u)] = xs_sem.val

                @blk.scalar
                def _(sc, u=u, ssl=ssl, xs_sem=xs_sem):
                    QA.bind(sc)
                    QA.w(xs_sem, M[("xst", u)])
                    if u >= 2:
                        QA.w(SP.sem, M.get(("lmm", u - 2), 0))
                    M[("xt", u)] = QA.dmat(
                        xts[:, ssl * 1024:(ssl + 1) * 1024].rearrange("p (c t) -> p c t", c=8),
                        xlnd.ap()[u * 128:(u + 1) * 128, :],
                        dm["dxt0"] if ssl == 0 else dm["dxt1"])

                @blk.tensor
                def _(t, u=u, ssl=ssl):
                    SP.bind(t)
                    SP.w(dm["dxt0"] if ssl == 0 else dm["dxt1"], M[("xt", u)])
                    if u == 0:
                        SP.w(dm["dwl"], M["wl"])
                    if u >= 2:
                        SP.w(SV.sem, M.get(("lev", u - 2), 0))
                    for c in range(8):
                        mi = SP.raw("matmul", ps[:, ssl * 8:ssl * 8 + 8],
                                    xts[:, ssl * 1024 + c * 128:ssl * 1024 + (c + 1) * 128],
                                    wgb[:, c * 8:(c + 1) * 8],
                                    start=(c == 0), stop=(c == 7))
                        if c == 7:
                            mi.then_inc(SP.sem.inc(1), 1)
                    M[("lmm", u)] = SP.sem.val

                @blk.vector
                def _(v, u=u, ssl=ssl):
                    SV.bind(v)
                    SV.w(SP.sem, M[("lmm", u)])
                    SV.do("tensor_copy", lgsb[:, u * 8:(u + 1) * 8], ps[:, ssl * 8:ssl * 8 + 8])
                    M[("lev", u)] = SV.sem.val

            @blk.sync
            def _(s):
                QS.bind(s)
                QS.w(SV.sem, M[("lev", TOK // 128 - 1)])
                QS.dma(lgw, lgsb[:, :], dm["dlg"])
                for _ds in dm.values():
                    if _ds.val:
                        s.wait_ge(_ds.h, _ds.val)

    return nc


def build_stage_d2a(nc=None, pf="", ext=None, ctx=None, phases=4):
    ext = ext or {}
    if nc is None:
        nc = bass.Bass("TRN2", target_bir_lowering=False)

    def exio(name, shape, dt, kind):
        return ext[name] if name in ext else nc.dram_tensor(name, shape, dt, kind=kind).ap()
    lg_in = exio("lg_in", [128, NB * 8], F32, "ExternalInput")
    tri = exio("tri", [128, 128], BF16, "ExternalInput")
    tid = exio("tid", [128, NB], I32, "ExternalInput")
    ioe = exio("ioe", [8], F32, "ExternalInput")
    expid = exio("expid", [1, 1], F32, "ExternalInput")
    w1o = exio("w1o", [128, NB], F32, "ExternalOutput")
    w2o = exio("w2o", [128, NB], F32, "ExternalOutput")
    f1o = exio("f1o", [128, NB], I32, "ExternalOutput")
    f2o = exio("f2o", [128, NB], I32, "ExternalOutput")
    stid = exio("stid", [CEXP], I32, "ExternalOutput")
    scrd = nc.dram_tensor(pf + "scrd", [8 * NB + 16], F32)

    TK = NB * 8
    with ExitStack() as stk:
        En = stk.enter_context
        if ctx is None:
            SV = Eng(Sem(En(nc.semaphore(pf + "sv"))))
            SA = Eng(Sem(En(nc.semaphore(pf + "sa"))))
            SG = Eng(Sem(En(nc.semaphore(pf + "sg"))))
            SP = Eng(Sem(En(nc.semaphore(pf + "sp"))), serialize=False)
        else:
            SV, SA, SG, SP = ctx["SV"], ctx["SA"], ctx["SG"], ctx["SP"]
        QS = Q()
        QG = Q()
        dm = {k: Sem(En(nc.semaphore(pf + k))) for k in ["dl", "dsc", "dso", "dout"]}

        lg = En(nc.sbuf_tensor(pf + "lg", [128, TK], F32))
        m1 = En(nc.sbuf_tensor(pf + "m1", [128, TK], BF16))
        m2 = En(nc.sbuf_tensor(pf + "m2", [128, TK], BF16))
        tmp = En(nc.sbuf_tensor(pf + "tmp", [128, TK], F32))
        cum1 = En(nc.sbuf_tensor(pf + "cum1", [128, TK], F32))
        cum2 = En(nc.sbuf_tensor(pf + "cum2", [128, TK], F32))
        trib = En(nc.sbuf_tensor(pf + "trib", [128, 128], BF16))
        onesb = En(nc.sbuf_tensor(pf + "onesb", [128, 1], BF16))
        tidb = En(nc.sbuf_tensor(pf + "tidb", [128, NB], I32))
        tok = En(nc.sbuf_tensor(pf + "tok", [128, 14 * NB], F32))
        rows = En(nc.sbuf_tensor(pf + "rows", [1, 16 * NB], F32))
        rows2 = En(nc.sbuf_tensor(pf + "rows2", [1, 16 * NB + 16], F32))
        ioeb = En(nc.sbuf_tensor(pf + "ioeb", [128, 8], F32))
        expb = En(nc.sbuf_tensor(pf + "expb", [128, 1], F32))
        ohb = En(nc.sbuf_tensor(pf + "ohb", [128, 8], F32))
        bct = En(nc.sbuf_tensor(pf + "bct", [128, 8 * NB + 16], F32))
        ints = En(nc.sbuf_tensor(pf + "ints", [128, 4 * NB], I32))
        stsb = En(nc.sbuf_tensor(pf + "stsb", [128, 128], I32))
        stf = En(nc.sbuf_tensor(pf + "stf", [128, 128], F32))
        ps1 = En(nc.psum_tensor([128, 1024], F32))
        ps2 = En(nc.psum_tensor([128, 1024], F32))

        def T(i):
            return tok[:, i * NB:(i + 1) * NB]
        MX, MX2, G1, G2, RDEN, GN1, GN2, IDX1, IDX2, LOC, KP, SCR, CE, ME = range(14)

        M = {}
        with nc.Block() as blk:
            @blk.sync
            def _(s):
                QS.bind(s)
                lgdst = lg[:, :].rearrange("p (r je) -> p r je", r=8) \
                    if len(lg_in.ap) == 3 else lg[:, :]
                QS.dma(lgdst, lg_in, dm["dl"])
                QS.dma(trib[:, :], tri, dm["dl"])
                QS.dma(tidb[:, :], tid, dm["dl"])
                QS.dma(ioeb[:, :], bass.AP(ioe.tensor, 0, [[0, 128], [1, 8]]), dm["dl"])
                M["ld"] = QS.dma(expb[:, :], bass.AP(expid.tensor, 0, [[0, 128], [1, 1]]), dm["dl"])

            @blk.gpsimd
            def _(g):
                SG.bind(g)
                SG.do("memset", onesb[:, :], 1.0)
                SG.do("memset", stf[:, :], float(ext.get("_sentinel", TOK * 8)))
                M["ms"] = SG.sem.val

            @blk.vector
            def _(v):
                SV.bind(v)
                SV.w(dm["dl"], M["ld"])
                SV.w(SG.sem, M["ms"])
                SV.do("tensor_reduce", T(MX), apv(lg[:, 0:1], 0, [[8, NB], [1, 8]]), AX.X, ALU.max)
                SV.do("tensor_tensor", m1[:, :], lg[:, :],
                      apv(tok[:, 0:1], MX * NB, [[1, NB], [0, 8]]), ALU.is_equal)
                SV.do("scalar_tensor_tensor", tmp[:, :], m1[:, :], -1e30, lg[:, :],
                      ALU.mult, ALU.add)
                SV.do("tensor_reduce", T(MX2), apv(tmp[:, 0:1], 0, [[8, NB], [1, 8]]), AX.X, ALU.max)
                SV.do("tensor_tensor", m2[:, :], tmp[:, :],
                      apv(tok[:, 0:1], MX2 * NB, [[1, NB], [0, 8]]), ALU.is_equal)
                SV.do("tensor_tensor", lg[:, :], lg[:, :],
                      apv(tok[:, 0:1], MX * NB, [[1, NB], [0, 8]]), ALU.subtract)
                M["pre"] = SV.sem.val

            @blk.scalar
            def _(sc):
                SA.bind(sc)
                SA.w(SV.sem, M["pre"])
                SA.do("activation", lg[:, :], lg[:, :], AF.Exp)
                M["exp"] = SA.sem.val

            @blk.vector
            def _(v):
                SV.bind(v)
                SV.w(SA.sem, M["exp"])
                SV.do("tensor_tensor", tmp[:, :], lg[:, :], m1[:, :], ALU.mult)
                SV.do("tensor_reduce", T(G1), apv(tmp[:, 0:1], 0, [[8, NB], [1, 8]]), AX.X, ALU.add)
                SV.do("tensor_tensor", tmp[:, :], lg[:, :], m2[:, :], ALU.mult)
                SV.do("tensor_reduce", T(G2), apv(tmp[:, 0:1], 0, [[8, NB], [1, 8]]), AX.X, ALU.add)
                SV.do("tensor_tensor", T(RDEN), T(G1), T(G2), ALU.add)
                SV.do("tensor_scalar", T(RDEN), T(RDEN), 1e-9, None, ALU.max)
                SV.do("reciprocal", T(RDEN), T(RDEN))
                SV.do("tensor_tensor", T(GN1), T(G1), T(RDEN), ALU.mult)
                SV.do("tensor_tensor", T(GN2), T(G2), T(RDEN), ALU.mult)
                SV.do("tensor_tensor", tmp[:, :], m1[:, :],
                      apv(ioeb[:, 0:1], 0, [[0, NB], [1, 8]]), ALU.mult)
                SV.do("tensor_reduce", T(IDX1), apv(tmp[:, 0:1], 0, [[8, NB], [1, 8]]), AX.X, ALU.add)
                SV.do("tensor_tensor", tmp[:, :], m2[:, :],
                      apv(ioeb[:, 0:1], 0, [[0, NB], [1, 8]]), ALU.mult)
                SV.do("tensor_reduce", T(IDX2), apv(tmp[:, 0:1], 0, [[8, NB], [1, 8]]), AX.X, ALU.add)
                SV.do("tensor_scalar", ohb[:, :], ioeb[:, :], expb[:, 0:1], None, ALU.is_equal)
                M["g"] = SV.sem.val

        with nc.Block() as blk:
            for un in range(16):
                ch, e = un // 8, un % 8
                msrc = m1 if ch == 0 else m2
                cdst = cum1 if ch == 0 else cum2

                @blk.tensor
                def _(t, un=un, e=e, msrc=msrc):
                    SP.bind(t)
                    if un == 0:
                        SP.w(SV.sem, M["g"])
                        SP.w(SG.sem, M["ms"])
                    pr = M.get(("scev", un - 2))
                    if pr:
                        SP.w(SV.sem, pr)
                    bank = (un % 2) * 512
                    SP.raw("matmul", ps1[:, bank:bank + 512], trib[:, :],
                           apv(msrc[:, 0:1], e, [[8, NB]]),
                           start=True, stop=True).then_inc(SP.sem.inc(1), 1)
                    SP.raw("matmul", ps2[0:1, bank:bank + 512], onesb[:, 0:1],
                           apv(msrc[:, 0:1], e, [[8, NB]]),
                           start=True, stop=True).then_inc(SP.sem.inc(1), 1)
                    M[("scmm", un)] = SP.sem.val

                @blk.vector
                def _(v, un=un, e=e, cdst=cdst):
                    SV.bind(v)
                    SV.w(SP.sem, M[("scmm", un)])
                    bank = (un % 2) * 512
                    SV.do("tensor_copy", apv(cdst[:, 0:1], e, [[8, NB]]), ps1[:, bank:bank + 512])
                    SV.do("tensor_copy", rows[0:1, un * NB:(un + 1) * NB], ps2[0:1, bank:bank + 512])
                    M[("scev", un)] = SV.sem.val

            @blk.vector
            def _(v):
                SV.bind(v)
                for un in range(16):
                    SV.do("tensor_tensor_scan", rows2[0:1, un * NB:(un + 1) * NB],
                          rows[0:1, un * NB:(un + 1) * NB],
                          rows[0:1, un * NB:(un + 1) * NB], 0.0, ALU.add, ALU.bypass)
                SV.do("tensor_copy", rows2[0:1, 16 * NB + 8:16 * NB + 16],
                      apv(rows2[0:1, 0:1], NB - 1, [[NB, 8]]))
                SV.do("tensor_tensor", rows2[0:1, 0:16 * NB],
                      rows2[0:1, 0:16 * NB], rows[0:1, 0:16 * NB], ALU.subtract)
                M["rowsv"] = SV.sem.val

            @blk.gpsimd
            def _(g):
                QG.bind(g)
                QG.w(SV.sem, M["rowsv"])
                QG.dma(scrd.ap().rearrange("(a b) -> a b", a=1)[0:1, 0:8 * NB],
                       rows2[0:1, 0:8 * NB], dm["dsc"])
                QG.w(dm["dsc"], dm["dsc"].val)
                M["bc1"] = QG.dma(bct[:, 0:8 * NB],
                                  bass.AP(scrd, 0, [[0, 128], [1, 8 * NB]]), dm["dsc"])

            @blk.vector
            def _(v):
                SV.bind(v)
                SV.w(dm["dsc"], M["bc1"])
                SV.do("tensor_tensor", cum1[:, :], cum1[:, :],
                      apv(bct[:, 0:1], 0, [[1, NB], [NB, 8]]), ALU.add)
                M["cum1v"] = SV.sem.val

            @blk.gpsimd
            def _(g):
                QG.bind(g)
                QG.w(SV.sem, M["cum1v"])
                QG.dma(scrd.ap().rearrange("(a b) -> a b", a=1)[0:1, 0:8 * NB + 16],
                       rows2[0:1, 8 * NB:16 * NB + 16], dm["dsc"])
                QG.w(dm["dsc"], dm["dsc"].val)
                M["bc2"] = QG.dma(bct[:, 0:8 * NB + 16],
                                  bass.AP(scrd, 0, [[0, 128], [1, 8 * NB + 16]]), dm["dsc"])

            @blk.vector
            def _(v):
                SV.bind(v)
                SV.w(dm["dsc"], M["bc2"])
                SV.do("tensor_tensor", cum2[:, :], cum2[:, :],
                      apv(bct[:, 0:1], 0, [[1, NB], [NB, 8]]), ALU.add)
                SV.do("tensor_tensor", cum2[:, :], cum2[:, :],
                      apv(bct[:, 0:1], 8 * NB + 8, [[0, NB], [1, 8]]), ALU.add)
                M["cum2v"] = SV.sem.val

        with nc.Block() as blk:
            @blk.vector
            def _(v):
                SV.bind(v)
                SV.w(SV.sem, M["cum2v"])
                ohv = apv(ohb[:, 0:1], 0, [[0, NB], [1, 8]])
                for ch in range(2):
                    mm = m1 if ch == 0 else m2
                    cc = cum1 if ch == 0 else cum2
                    GNx = GN1 if ch == 0 else GN2
                    IDXx = IDX1 if ch == 0 else IDX2
                    SV.do("tensor_tensor", tmp[:, :], mm[:, :], cc[:, :], ALU.mult)
                    SV.do("tensor_reduce", T(LOC), apv(tmp[:, 0:1], 0, [[8, NB], [1, 8]]),
                          AX.X, ALU.add)
                    SV.do("tensor_scalar", T(KP), T(LOC), float(CEXP), None, ALU.is_lt)
                    SV.do("tensor_tensor", T(GNx), T(GNx), T(KP), ALU.mult)
                    SV.do("tensor_scalar", T(SCR), T(LOC), float(CEXP - 1), None, ALU.min)
                    SV.do("tensor_scalar", T(CE), T(SCR), 4096.0, None, ALU.is_ge)
                    SV.do("scalar_tensor_tensor", T(CE), T(SCR), 8192.0, T(CE),
                          ALU.is_ge, ALU.add)
                    SV.do("scalar_tensor_tensor", T(CE), T(SCR), 12288.0, T(CE),
                          ALU.is_ge, ALU.add)
                    SV.do("scalar_tensor_tensor", T(SCR), T(CE), 28672.0, T(SCR),
                          ALU.mult, ALU.add)
                    SV.do("scalar_tensor_tensor", T(SCR), T(IDXx), 4096.0, T(SCR),
                          ALU.mult, ALU.add)
                    SV.do("tensor_copy", ints[:, (2 + ch) * NB:(3 + ch) * NB], T(SCR))
                    SV.do("tensor_tensor", tmp[:, :], tmp[:, :], ohv, ALU.mult)
                    SV.do("tensor_reduce", T(CE), apv(tmp[:, 0:1], 0, [[8, NB], [1, 8]]),
                          AX.X, ALU.add)
                    SV.do("tensor_tensor", tmp[:, :], mm[:, :], ohv, ALU.mult)
                    SV.do("tensor_reduce", T(ME), apv(tmp[:, 0:1], 0, [[8, NB], [1, 8]]),
                          AX.X, ALU.add)
                    SV.do("tensor_scalar", T(KP), T(CE), float(CEXP), None, ALU.is_lt)
                    SV.do("tensor_tensor", T(KP), T(ME), T(KP), ALU.mult)
                    SV.do("tensor_scalar", T(KP), T(KP), -1e9, 1e9, ALU.mult, ALU.add)
                    SV.do("tensor_tensor", T(SCR), T(CE), T(KP), ALU.add)
                    SV.do("tensor_copy", ints[:, ch * NB:(ch + 1) * NB], T(SCR))
                SV.do("tensor_copy", stsb[:, :], stf[:, :])
                M["tokv"] = SV.sem.val

            @blk.sync
            def _(s):
                QS.bind(s)
                QS.w(SV.sem, M["tokv"])
                QS.dma(w1o, T(GN1), dm["dout"])
                QS.dma(w2o, T(GN2), dm["dout"])
                QS.dma(f1o, ints[:, 2 * NB:3 * NB], dm["dout"])
                M["outs"] = QS.dma(f2o, ints[:, 3 * NB:4 * NB], dm["dout"])

            @blk.gpsimd
            def _(g):
                QG.bind(g)
                QG.w(SV.sem, M["tokv"])
                QG.dma(stid.rearrange("(p j) -> p j", p=128), stsb[:, :], dm["dso"])
                QG.w(dm["dso"], dm["dso"].val)
                with g.register("bnd") as rb:
                    g.reg_mov(rb, CEXP - 1)
                    for ch in range(2):
                        for b in range(NB):
                            g.indirect_dma_start(
                                stid.rearrange("(a b) -> a b", b=1),
                                IndirectOffsetOnAxis(ap=ints[:, ch * NB + b:ch * NB + b + 1], axis=0),
                                tidb[:, b:b + 1], None,
                                bounds_check=rb, oob_is_err=False,
                            ).then_inc(dm["dso"].inc(16), 16)
                for _ds in dm.values():
                    if _ds.val:
                        g.wait_ge(_ds.h, _ds.val)

    return nc


def build_stage_d2b(nc=None, pf="", ext=None, ctx=None, phases=4):
    ext = ext or {}
    if nc is None:
        nc = bass.Bass("TRN2", target_bir_lowering=False)

    def exio(name, shape, dt, kind):
        return ext[name] if name in ext else nc.dram_tensor(name, shape, dt, kind=kind).ap()
    xall = exio("xall", [TOK * 8 + 1, E], BF16, "ExternalInput")
    stid = exio("stid", [CEXP], I32, "ExternalInput")
    wgate = exio("wgate", [E, 4096], F32, "ExternalInput")
    wdown = exio("wdown", [4096, E], F32, "ExternalInput")
    eo = exio("eo", [CEXP, E], BF16, "ExternalOutput")
    dispd = nc.dram_tensor(pf + "dispd", [CEXP, E], BF16)

    with ExitStack() as stk:
        En = stk.enter_context
        if ctx is None:
            SV = Eng(Sem(En(nc.semaphore(pf + "sv"))))
            SA = Eng(Sem(En(nc.semaphore(pf + "sa"))))
            SP = Eng(Sem(En(nc.semaphore(pf + "sp"))), serialize=False)
        else:
            SV, SA, SP = ctx["SV"], ctx["SA"], ctx["SP"]
        QS = Q()
        QA = Q()
        QG = Q()
        dm = {k: Sem(En(nc.semaphore(pf + k))) for k in
              ["dst", "dg0", "dg1", "dds0", "dds1", "dwl", "dxt0", "dxt1", "dost"]}

        wgb = En(nc.sbuf_tensor(pf + "wgb", [128, 8 * 4096], BF16))
        wdb = En(nc.sbuf_tensor(pf + "wdb", [128, 32 * 1024], BF16))
        stsb = En(nc.sbuf_tensor(pf + "stsb", [128, 128], I32))
        gb = En(nc.sbuf_tensor(pf + "gb", [128, 2 * 1024], BF16))
        xts = En(nc.sbuf_tensor(pf + "xts", [128, 2 * 4096], BF16))
        ht = En(nc.sbuf_tensor(pf + "ht", [128, 32 * 512], BF16))
        osb = En(nc.sbuf_tensor(pf + "osb", [128, 1024], BF16))
        ph = En(nc.psum_tensor([128, 2048], F32))
        po = En(nc.psum_tensor([128, 1024], F32))

        M = {}
        with nc.Block() as blk:
            @blk.gpsimd
            def _(g):
                QG.bind(g)
                QG.dma(wgb[:, :].rearrange("p (c f) -> p c f", c=8),
                       wgate.rearrange("(c p) f -> p c f", p=128), dm["dwl"])
                M["wl"] = QG.dma(wdb[:, :].rearrange("p (c f) -> p c f", c=32),
                                 wdown.rearrange("(c p) f -> p c f", p=128), dm["dwl"])

            @blk.sync
            def _(s):
                QS.bind(s)
                M["st"] = QS.dma(stsb[:, :], stid.rearrange("(p j) -> p j", p=128), dm["dst"])

            @blk.gpsimd
            def _(g):
                QG.bind(g)
                QG.w(dm["dst"], M["st"])
                if "_ccs_x" in ext:
                    g.wait_ge(ext["_ccs_x"][0], ext["_ccs_x"][1])
                for j in range(128):
                    sl = j % 2
                    gsem = dm["dg0"] if sl == 0 else dm["dg1"]
                    ssem = dm["dds0"] if sl == 0 else dm["dds1"]
                    if j >= 2:
                        QG.w(ssem, M.get(("ds", j - 2), 0))
                    g.indirect_dma_start(
                        gb[:, sl * 1024:(sl + 1) * 1024], None,
                        xall, IndirectOffsetOnAxis(ap=stsb[:, j:j + 1], axis=0),
                    ).then_inc(gsem.inc(16), 16)
                    M[("g", j)] = gsem.val
                    QG.w(gsem, M[("g", j)])
                    g.dma_start(
                        bass.AP(dispd, j * 1024, [[128 * 1024, 128], [1, 1024]]),
                        gb[:, sl * 1024:(sl + 1) * 1024]).then_inc(ssem.inc(16), 16)
                    M[("ds", j)] = ssem.val

        NTD = CEXP // 512
        with nc.Block() as blk:
            for j in range(NTD):
                jsl = j % 2

                @blk.scalar
                def _(sc, j=j, jsl=jsl):
                    QA.bind(sc)
                    QA.w(dm["dds0"], M[("ds", 4 * j + 2)])
                    QA.w(dm["dds1"], M[("ds", 4 * j + 3)])
                    if j >= 2:
                        QA.w(SP.sem, M[("hmm", j - 2, 31)])
                    M[("xt", j)] = QA.dmat(
                        xts[:, jsl * 4096:(jsl + 1) * 4096].rearrange("p (c t) -> p c t", c=8),
                        dispd.ap()[j * 512:(j + 1) * 512, :],
                        dm["dxt0"] if jsl == 0 else dm["dxt1"])

                ga0 = SA.sem.val

                @blk.tensor
                def _(t, j=j, jsl=jsl, ga0=ga0):
                    SP.bind(t)
                    SP.w(dm["dxt0"] if jsl == 0 else dm["dxt1"], M[("xt", j)])
                    if j == 0:
                        SP.w(dm["dwl"], M["wl"])
                    for m in range(32):
                        if m < 4:
                            SP.w(SA.sem, ga0)
                        else:
                            SP.w(SA.sem, ga0 + (m - 4) // 2 + 1)
                        bank = m % 4
                        for c in range(8):
                            mi = SP.raw("matmul", ph[:, bank * 512:(bank + 1) * 512],
                                        wgb[:, c * 4096 + m * 128:c * 4096 + (m + 1) * 128],
                                        xts[:, jsl * 4096 + c * 512:jsl * 4096 + (c + 1) * 512],
                                        start=(c == 0), stop=(c == 7))
                            if c == 7:
                                mi.then_inc(SP.sem.inc(1), 1)
                        M[("hmm", j, m)] = SP.sem.val

                @blk.scalar
                def _(sc, j=j, ga0=ga0):
                    SA.bind(sc)
                    for P in range(16):
                        SA.w(SP.sem, M[("hmm", j, 2 * P + 1)])
                        sl2 = (2 * P) % 4
                        SA.do("activation", ht[:, 2 * P * 512:(2 * P + 2) * 512],
                              ph[:, sl2 * 512:(sl2 + 2) * 512], AF.Gelu)
                        M[("gelu", j, 2 * P)] = SA.sem.val
                        M[("gelu", j, 2 * P + 1)] = SA.sem.val
                    assert SA.sem.val == ga0 + 16

                for idx in range(8):
                    ts, dh = idx // 2, idx % 2

                    @blk.tensor
                    def _(t, j=j, idx=idx, ts=ts, dh=dh):
                        SP.bind(t)
                        if idx == 0:
                            SP.w(SA.sem, M[("gelu", j, 31)])
                        og = j * 8 + idx - 2
                        if og >= 0:
                            SP.w(SV.sem, M.get(("oev", og), 0))
                        bank = idx % 2
                        for m in range(32):
                            mi = SP.raw("matmul", po[:, bank * 512:(bank + 1) * 512],
                                        ht[:, m * 512 + ts * 128:m * 512 + (ts + 1) * 128],
                                        wdb[:, m * 1024 + dh * 512:m * 1024 + (dh + 1) * 512],
                                        start=(m == 0), stop=(m == 31))
                            if m == 31:
                                mi.then_inc(SP.sem.inc(1), 1)
                        M[("omm", j, idx)] = SP.sem.val

                    @blk.vector
                    def _(v, j=j, idx=idx, ts=ts, dh=dh):
                        SV.bind(v)
                        SV.w(SP.sem, M[("omm", j, idx)])
                        tsg = j * 4 + ts
                        if tsg >= 1:
                            SV.w(dm["dost"], M.get(("ost", tsg - 1), 0))
                        bank = idx % 2
                        SV.do("tensor_copy", osb[:, dh * 512:(dh + 1) * 512],
                              po[:, bank * 512:(bank + 1) * 512])
                        M[("oev", j * 8 + idx)] = SV.sem.val

                    if dh == 1:
                        @blk.sync
                        def _(s, j=j, ts=ts):
                            QS.bind(s)
                            QS.w(SV.sem, M[("oev", j * 8 + ts * 2 + 1)])
                            M[("ost", j * 4 + ts)] = QS.dma(
                                eo[j * 512 + ts * 128:j * 512 + (ts + 1) * 128, :],
                                osb[:, :], dm["dost"])

            if "eoall_cc" in ext:
                eoall_ap, ccs_h = ext["eoall_cc"]

                @blk.gpsimd
                def _(g):
                    QG.bind(g)
                    for k in range(4):
                        QG.w(dm["dost"], M[("ost", (8 * k + 7) * 4 + 3)])
                        g.collective_compute(
                            "AllGather", ALU.bypass, replica_groups=[list(range(8))],
                            ins=[eo[k * 4096:(k + 1) * 4096, :]],
                            outs=[bass.AP(eoall_ap.tensor, k * 32768 * E, [[E, 32768], [1, E]])],
                        ).then_inc(ccs_h, 1)

            @blk.sync
            def _(s):
                for _ds in dm.values():
                    if _ds.val:
                        s.wait_ge(_ds.h, _ds.val)

    return nc


def build_stage_d3(nc=None, pf="", ext=None, ctx=None, phases=4):
    ext = ext or {}
    if nc is None:
        nc = bass.Bass("TRN2", target_bir_lowering=False)

    def exio(name, shape, dt, kind):
        return ext[name] if name in ext else nc.dram_tensor(name, shape, dt, kind=kind).ap()
    NBL = TOK // 128
    xin = exio("xin", [TOK, E], F32, "ExternalInput")
    eoall = exio("eoall", [8 * CEXP, E], BF16, "ExternalInput")
    ext = ext or {}
    if "selidx" not in ext:
        f1s = exio("f1s", [128, NBL], I32, "ExternalInput")
        f2s = exio("f2s", [128, NBL], I32, "ExternalInput")
        w1s = exio("w1s", [128, NBL], F32, "ExternalInput")
        w2s = exio("w2s", [128, NBL], F32, "ExternalInput")
    else:
        f1s = f2s = w1s = w2s = None
    xout = exio("xout", [TOK, E], F32, "ExternalOutput")

    with ExitStack() as stk:
        En = stk.enter_context
        if ctx is None:
            SV = Eng(Sem(En(nc.semaphore(pf + "sv"))))
        else:
            SV = ctx["SV"]
        QS = Q()
        QG = Q()
        dm = {k: Sem(En(nc.semaphore(pf + k))) for k in
              ["dl", "dx0", "dx1", "dg10", "dg11", "dg20", "dg21", "dost"]}

        f1b = En(nc.sbuf_tensor(pf + "f1b", [128, NBL], I32))
        f2b = En(nc.sbuf_tensor(pf + "f2b", [128, NBL], I32))
        w1b = En(nc.sbuf_tensor(pf + "w1b", [128, NBL], F32))
        w2b = En(nc.sbuf_tensor(pf + "w2b", [128, NBL], F32))
        xsb = En(nc.sbuf_tensor(pf + "xsb", [128, 2 * 1024], F32))
        e1b = En(nc.sbuf_tensor(pf + "e1b", [128, 2 * 1024], BF16))
        e2b = En(nc.sbuf_tensor(pf + "e2b", [128, 2 * 1024], BF16))
        acc = En(nc.sbuf_tensor(pf + "acc", [128, 1024], F32))
        osb = En(nc.sbuf_tensor(pf + "osb", [128, 2 * 1024], F32))

        M = {}
        with nc.Block() as blk:
            if "selidx" in ext:
                selb = En(nc.sbuf_tensor(pf + "selb", [128, 1], I32))

                @blk.sync
                def _(s):
                    QS.bind(s)
                    M["sel"] = QS.dma(selb[:, :], ext["selidx"], dm["dl"])

                @blk.gpsimd
                def _(g):
                    QG.bind(g)
                    QG.w(dm["dl"], M["sel"])
                    g.indirect_dma_start(f1b[:, :], None, ext["f1v"],
                                         IndirectOffsetOnAxis(ap=selb[:, :], axis=0)
                                         ).then_inc(dm["dl"].inc(16), 16)
                    g.indirect_dma_start(f2b[:, :], None, ext["f2v"],
                                         IndirectOffsetOnAxis(ap=selb[:, :], axis=0)
                                         ).then_inc(dm["dl"].inc(16), 16)
                    g.indirect_dma_start(w1b[:, :], None, ext["w1v"],
                                         IndirectOffsetOnAxis(ap=selb[:, :], axis=0)
                                         ).then_inc(dm["dl"].inc(16), 16)
                    g.indirect_dma_start(w2b[:, :], None, ext["w2v"],
                                         IndirectOffsetOnAxis(ap=selb[:, :], axis=0)
                                         ).then_inc(dm["dl"].inc(16), 16)
                    M["ld"] = dm["dl"].val
            else:
                @blk.sync
                def _(s):
                    QS.bind(s)
                    QS.dma(f1b[:, :], f1s, dm["dl"])
                    QS.dma(f2b[:, :], f2s, dm["dl"])
                    QS.dma(w1b[:, :], w1s, dm["dl"])
                    M["ld"] = QS.dma(w2b[:, :], w2s, dm["dl"])

        with nc.Block() as blk:
            for j in range(NBL):
                sl = j % 2

                @blk.sync
                def _(s, j=j, sl=sl):
                    QS.bind(s)
                    if j >= 2:
                        QS.w(SV.sem, M[("cmb", j - 2)])
                    M[("x", j)] = QS.dma(xsb[:, sl * 1024:(sl + 1) * 1024],
                                         xin[j * 128:(j + 1) * 128, :],
                                         dm["dx0"] if sl == 0 else dm["dx1"])

                @blk.gpsimd
                def _(g, j=j, sl=sl):
                    QG.bind(g)
                    if j == 0:
                        QG.w(dm["dl"], M["ld"])
                        if "_ccs_eo" in ext:
                            g.wait_ge(ext["_ccs_eo"][0], ext["_ccs_eo"][1])
                    if j >= 2:
                        QG.w(SV.sem, M[("cmb", j - 2)])
                    g.indirect_dma_start(
                        e1b[:, sl * 1024:(sl + 1) * 1024], None,
                        eoall, IndirectOffsetOnAxis(ap=f1b[:, j:j + 1], axis=0),
                    ).then_inc((dm["dg10"] if sl == 0 else dm["dg11"]).inc(16), 16)
                    M[("g1", j)] = (dm["dg10"] if sl == 0 else dm["dg11"]).val
                    g.indirect_dma_start(
                        e2b[:, sl * 1024:(sl + 1) * 1024], None,
                        eoall, IndirectOffsetOnAxis(ap=f2b[:, j:j + 1], axis=0),
                    ).then_inc((dm["dg20"] if sl == 0 else dm["dg21"]).inc(16), 16)
                    M[("g2", j)] = (dm["dg20"] if sl == 0 else dm["dg21"]).val

                @blk.vector
                def _(v, j=j, sl=sl):
                    SV.bind(v)
                    SV.w(dm["dg10"] if sl == 0 else dm["dg11"], M[("g1", j)])
                    SV.w(dm["dg20"] if sl == 0 else dm["dg21"], M[("g2", j)])
                    SV.w(dm["dx0"] if sl == 0 else dm["dx1"], M[("x", j)])
                    if j >= 2:
                        SV.w(dm["dost"], M.get(("ost", j - 2), 0))
                    SV.do("tensor_scalar", acc[:, :], e1b[:, sl * 1024:(sl + 1) * 1024],
                          w1b[:, j:j + 1], None, ALU.mult)
                    SV.do("scalar_tensor_tensor", acc[:, :], e2b[:, sl * 1024:(sl + 1) * 1024],
                          w2b[:, j:j + 1], acc[:, :], ALU.mult, ALU.add)
                    SV.do("tensor_tensor", osb[:, sl * 1024:(sl + 1) * 1024],
                          acc[:, :], xsb[:, sl * 1024:(sl + 1) * 1024], ALU.add)
                    M[("cmb", j)] = SV.sem.val

                @blk.sync
                def _(s, j=j, sl=sl):
                    QS.bind(s)
                    QS.w(SV.sem, M[("cmb", j)])
                    M[("ost", j)] = QS.dma(xout[j * 128:(j + 1) * 128, :],
                                           osb[:, sl * 1024:(sl + 1) * 1024], dm["dost"])

            @blk.sync
            def _(s):
                for _ds in dm.values():
                    if _ds.val:
                        s.wait_ge(_ds.h, _ds.val)

    return nc


def _d_consts():
    tri = np.tril(np.ones((128, 128), dtype=np.float32), -1).T.astype(BF16NP)
    tid = (np.arange(NB)[None, :] * 128 + np.arange(128)[:, None]).astype(np.int32)
    ioe = np.arange(8, dtype=np.float32)
    return tri, tid, ioe


def run_d(x3_full, inputs):
    tri, tid, ioe = _d_consts()
    nc1 = build_stage_d1()
    maps = [dict(xin=np.ascontiguousarray(x3_full[c * TOK:(c + 1) * TOK]),
                 wg=inputs["L1_ffn_wg"]) for c in range(N_CORES)]
    r1 = run_bass_kernel_spmd(nc1, maps, core_ids=list(range(N_CORES)))
    xln_all = np.concatenate([r1.results[c]["xln3"] for c in range(N_CORES)], axis=0)
    xln_all = np.concatenate([xln_all, np.zeros((1, E), dtype=xln_all.dtype)], axis=0)
    lg_full = np.concatenate([r1.results[c]["lgw"] for c in range(N_CORES)], axis=1)
    nc2a = build_stage_d2a()
    maps = [dict(lg_in=lg_full, tri=tri, tid=tid, ioe=ioe,
                 expid=np.full((1, 1), float(c), np.float32)) for c in range(N_CORES)]
    r2a = run_bass_kernel_spmd(nc2a, maps, core_ids=list(range(N_CORES)))
    nc2b = build_stage_d2b()
    maps = [dict(xall=xln_all, stid=r2a.results[c]["stid"],
                 wgate=np.ascontiguousarray(inputs["L1_ffn_w_gate"][c]),
                 wdown=np.ascontiguousarray(inputs["L1_ffn_w_down"][c]))
            for c in range(N_CORES)]
    r2b = run_bass_kernel_spmd(nc2b, maps, core_ids=list(range(N_CORES)))
    eo_all = np.concatenate([r2b.results[c]["eo"] for c in range(N_CORES)], axis=0)
    f1 = r2a.results[0]["f1o"]
    f2 = r2a.results[0]["f2o"]
    w1 = r2a.results[0]["w1o"]
    w2 = r2a.results[0]["w2o"]
    nc3 = build_stage_d3()
    NBL = TOK // 128
    maps = []
    for c in range(N_CORES):
        sl = slice(c * NBL, (c + 1) * NBL)
        maps.append(dict(
            xin=np.ascontiguousarray(x3_full[c * TOK:(c + 1) * TOK]),
            eoall=eo_all,
            f1s=np.ascontiguousarray(f1[:, sl]), f2s=np.ascontiguousarray(f2[:, sl]),
            w1s=np.ascontiguousarray(w1[:, sl]), w2s=np.ascontiguousarray(w2[:, sl]),
        ))
    r3 = run_bass_kernel_spmd(nc3, maps, core_ids=list(range(N_CORES)))
    return np.concatenate([r3.results[c]["xout"] for c in range(N_CORES)], axis=0)



def emit_barrier(nc, bsem, idx, ctx):
    with nc.Block() as blk:
        def _s(e):
            for k in ("SV", "SA", "SG", "SP"):
                v = ctx[k].sem.val
                if v:
                    e.wait_ge(ctx[k].sem.h, v)
            e.nop().then_inc(bsem, 1)
            e.wait_ge(bsem, 5 * idx)
        blk.sync(_s)
        for engname in ["tensor", "vector", "scalar", "gpsimd"]:
            def _f(e):
                e.nop().then_inc(bsem, 1)
                e.wait_ge(bsem, 5 * idx)
            getattr(blk, engname)(_f)


def build_fused(dbg=False):
    nc = bass.Bass("TRN2", target_bir_lowering=False)
    DI = lambda n, sh, dt: nc.dram_tensor(n, sh, dt, kind="ExternalInput").ap()
    x = DI("x", [TOK, E], F32)
    out = nc.dram_tensor("out", [TOK, E], F32, kind="ExternalOutput").ap()
    wA = {n: DI(n, shp, F32) for n, shp in [
        ("wq1", [E, E]), ("wq2", [E, E]), ("wk1", [E, 128]), ("wk2", [E, 128]),
        ("wv", [E, 128]), ("wout", [E, E]), ("bout", [1, E]), ("lqk", [1, 256]),
        ("sintok", [TA, 32]), ("costok", [TA, 32])]}
    wB = {n: DI(n, shp, F32) for n, shp in [
        ("w1", [4096, 1024]), ("b1", [1, 4096]), ("w2", [1024, 4096]), ("b2", [1, 1024])]}
    wC = {n: DI("c" + n, shp, F32) for n, shp in [
        ("wq", [E, E]), ("wk", [E, E]), ("wv", [E, E]), ("wo", [E, E]),
        ("bq", [1, E]), ("bk", [1, E]), ("bv", [1, E]), ("bo", [1, E])]}
    wg = DI("wg", [E, 8], F32)
    tri = DI("tri", [128, 128], BF16)
    tid = DI("tid", [128, NB], I32)
    ioe = DI("ioe", [8], F32)
    expid = DI("expid", [1, 1], F32)
    wgate = DI("wgate", [E, 4096], F32)
    wdown = DI("wdown", [4096, E], F32)
    selidx = DI("selidx", [128, 1], I32)

    ko = "ExternalOutput" if dbg else "Internal"
    x1 = nc.dram_tensor("x1i", [TOK, E], F32, kind=ko)
    x2 = nc.dram_tensor("x2i", [TOK, E], F32, kind=ko)
    x3 = nc.dram_tensor("x3i", [TOK, E], F32, kind=ko)
    lgl = nc.dram_tensor("lgl", [128, 512], F32)
    lgall = nc.dram_tensor("lgall", [1024, 512], F32)
    xlnl = nc.dram_tensor("xlnl", [TOK + 1, E], BF16)
    xall = nc.dram_tensor("xallg", [(TOK + 1) * 8, E], BF16)
    f1i = nc.dram_tensor("f1i", [128, NB], I32)
    f2i = nc.dram_tensor("f2i", [128, NB], I32)
    w1i = nc.dram_tensor("w1i", [128, NB], F32)
    w2i = nc.dram_tensor("w2i", [128, NB], F32)
    stidi = nc.dram_tensor("stidi", [CEXP], I32)
    eol = nc.dram_tensor("eol", [CEXP, E], BF16)
    eoall = nc.dram_tensor("eoallg", [8 * CEXP, E], BF16)

    with ExitStack() as stk:
        En = stk.enter_context
        ctx = {
            "SV": Eng(Sem(En(nc.semaphore("zsv")))),
            "SA": Eng(Sem(En(nc.semaphore("zsa")))),
            "SG": Eng(Sem(En(nc.semaphore("zsg")))),
            "SP": Eng(Sem(En(nc.semaphore("zsp"))), serialize=False),
        }
        bsem = En(nc.semaphore("zbar"))
        ccs = En(nc.semaphore("zcc"))
        dgl = Sem(En(nc.semaphore("zgl")))
        nb = [0]

        def bar():
            nb[0] += 1
            emit_barrier(nc, bsem, nb[0], ctx)

        build_stage_a(nc, "a_", dict(xin=x, xout=x1.ap(), **wA), ctx)
        bar()
        build_stage_b(nc, "b_", dict(xin=x1.ap(), xout=x2.ap(), **wB), ctx)
        bar()
        build_stage_c(nc, "c_", dict(xin=x2.ap(), xout=x3.ap(), **wC), ctx)
        bar()
        build_stage_d1(nc, "e_", dict(xin=x3.ap(), xln3=xlnl.ap()[0:TOK, :],
                                      lgw=lgl.ap(), wg=wg), ctx)
        bar()
        with nc.sbuf_tensor("zrow", [1, 1024], BF16) as zrow, nc.Block() as blk:
            @blk.gpsimd
            def _(g):
                g.memset(zrow[0:1, :], 0.0)
                g.nop().then_inc(dgl.inc(16), 16)
                g.wait_ge(dgl.h, dgl.val)
                g.dma_start(xlnl.ap()[TOK:TOK + 1, :], zrow[0:1, :]).then_inc(dgl.inc(16), 16)
                g.wait_ge(dgl.h, dgl.val)
                g.collective_compute(
                    "AllGather", ALU.bypass, replica_groups=[list(range(8))],
                    ins=[lgl.ap()], outs=[lgall.ap()]).then_inc(ccs, 1)
                g.collective_compute(
                    "AllGather", ALU.bypass, replica_groups=[list(range(8))],
                    ins=[xlnl.ap()], outs=[xall.ap()]).then_inc(ccs, 1)
                g.wait_ge(ccs, 1)
        bar()
        build_stage_d2a(nc, "f_", dict(
            lg_in=lgall.ap().rearrange("(r p) je -> p r je", p=128),
            tri=tri, tid=tid, ioe=ioe, expid=expid,
            w1o=w1i.ap(), w2o=w2i.ap(), f1o=f1i.ap(), f2o=f2i.ap(), stid=stidi.ap(),
            _sentinel=float(TOK)), ctx)
        bar()
        build_stage_d2b(nc, "g_", dict(xall=xall.ap(), stid=stidi.ap(),
                                       wgate=wgate, wdown=wdown, eo=eol.ap(),
                                       _ccs_x=(ccs, 2), eoall_cc=(eoall.ap(), ccs)), ctx)
        bar()
        build_stage_d3(nc, "h_", dict(
            xin=x3.ap(), eoall=eoall.ap(), xout=out, selidx=selidx, _ccs_eo=(ccs, 6),
            f1v=f1i.ap().rearrange("p (h j) -> (p h) j", h=8),
            f2v=f2i.ap().rearrange("p (h j) -> (p h) j", h=8),
            w1v=w1i.ap().rearrange("p (h j) -> (p h) j", h=8),
            w2v=w2i.ap().rearrange("p (h j) -> (p h) j", h=8)), ctx)
    return nc


def kernel(**inputs):
    xfull = np.ascontiguousarray(np.asarray(inputs["L0_attn_x"], dtype=np.float32).reshape(-1, E))
    sintab, costab = _rope_tables()
    tri, tidc, ioe = _d_consts()
    tglob = tidc.astype(np.int64)
    tid_f = (tglob + tglob // TOK).astype(np.int32)
    nc = build_fused()
    maps = []
    for c in range(N_CORES):
        maps.append(dict(
            x=xfull[c * TOK:(c + 1) * TOK],
            wq1=inputs["L0_attn_Wq1"], wq2=inputs["L0_attn_Wq2"],
            wk1=inputs["L0_attn_Wk1"], wk2=inputs["L0_attn_Wk2"],
            wv=inputs["L0_attn_Wv"], wout=inputs["L0_attn_Wout"],
            bout=inputs["L0_attn_bout"].reshape(1, -1),
            lqk=np.concatenate([inputs["L0_attn_lq1"], inputs["L0_attn_lk1"],
                                inputs["L0_attn_lq2"], inputs["L0_attn_lk2"]]).reshape(1, -1),
            sintok=sintab, costok=costab,
            w1=inputs["L0_ffn_W1"], b1=inputs["L0_ffn_b1"].reshape(1, -1),
            w2=inputs["L0_ffn_W2"], b2=inputs["L0_ffn_b2"].reshape(1, -1),
            cwq=inputs["L1_attn_Wq"], cbq=inputs["L1_attn_bq"].reshape(1, -1),
            cwk=inputs["L1_attn_Wk"], cbk=inputs["L1_attn_bk"].reshape(1, -1),
            cwv=inputs["L1_attn_Wv"], cbv=inputs["L1_attn_bv"].reshape(1, -1),
            cwo=inputs["L1_attn_Wo"], cbo=inputs["L1_attn_bo"].reshape(1, -1),
            wg=inputs["L1_ffn_wg"],
            tri=tri, tid=tid_f, ioe=ioe,
            expid=np.full((1, 1), float(c), np.float32),
            wgate=np.ascontiguousarray(inputs["L1_ffn_w_gate"][c]),
            wdown=np.ascontiguousarray(inputs["L1_ffn_w_down"][c]),
            selidx=(np.arange(128, dtype=np.int32) * 8 + c).reshape(128, 1),
        ))
    res = run_bass_kernel_spmd(nc, maps, core_ids=list(range(N_CORES)))
    outf = np.concatenate([res.results[c]["out"] for c in range(N_CORES)], axis=0)
    return outf.reshape(16, L, E)


def kernel_unfused(**inputs):
    x = np.ascontiguousarray(np.asarray(inputs["L0_attn_x"], dtype=np.float32).reshape(-1, E))
    x1 = run_a(x, inputs)
    x2 = run_b(x1, inputs)
    x3 = run_c(x2, inputs)
    return run_d(x3, inputs).reshape(16, L, E)


# version 19
# speedup vs baseline: 1.1278x; 1.1278x over previous
import numpy as np
import ml_dtypes
import concourse.bass as bass
import concourse.mybir as mybir
from contextlib import ExitStack
from concourse.bass import IndirectOffsetOnAxis
from concourse.bass_utils import run_bass_kernel_spmd

F32 = mybir.dt.float32
BF16 = mybir.dt.bfloat16
I32 = mybir.dt.int32
AF = mybir.ActivationFunctionType
ALU = mybir.AluOpType
AX = mybir.AxisListType
BF16NP = ml_dtypes.bfloat16

N_CORES = 8
BPC = 2
L = 4096
E = 1024
TOK = BPC * L


class Sem:
    __slots__ = ("h", "val")

    def __init__(self, handle):
        self.h = handle
        self.val = 0

    def inc(self, n=1):
        self.val += n
        return self.h


SELF_SERIALIZE = True


class Eng:

    def __init__(self, sem, serialize=None):
        if serialize is None:
            serialize = SELF_SERIALIZE
        self.sem = sem
        self.serialize = serialize
        self.cur = None
        self.marks = {}

    def bind(self, e):
        self.cur = e
        return self

    def w(self, sem, val=None):
        val = sem.val if val is None else val
        if val <= 0 or self.marks.get(id(sem), -1) >= val:
            return
        self.cur.wait_ge(sem.h, val)
        self.marks[id(sem)] = val

    def do(self, name, *a, **k):
        if self.serialize:
            self.w(self.sem, self.sem.val)
        inst = getattr(self.cur, name)(*a, **k)
        inst.then_inc(self.sem.inc(1), 1)
        return inst

    def raw(self, name, *a, **k):
        return getattr(self.cur, name)(*a, **k)


class Q:

    def __init__(self):
        self.cur = None
        self.marks = {}

    def bind(self, e):
        self.cur = e
        return self

    def w(self, sem, val=None):
        val = sem.val if val is None else val
        if val <= 0 or self.marks.get(id(sem), -1) >= val:
            return
        self.cur.wait_ge(sem.h, val)
        self.marks[id(sem)] = val

    def dma(self, out, in_, sem, **kw):
        self.cur.dma_start(out, in_, **kw).then_inc(sem.inc(16), 16)
        return sem.val

    def dmat(self, out, in_, sem):
        self.cur.dma_start_transpose(out, in_).then_inc(sem.inc(16), 16)
        return sem.val



def build_stage_b(nc=None, pf="", ext=None, ctx=None, phases=4):
    ext = ext or {}
    if nc is None:
        nc = bass.Bass("TRN2", target_bir_lowering=False)

    def exio(name, shape, dt, kind):
        return ext[name] if name in ext else nc.dram_tensor(name, shape, dt, kind=kind).ap()
    xin = exio("xin", [TOK, E], F32, "ExternalInput")
    w1 = exio("w1", [4096, 1024], F32, "ExternalInput")
    b1 = exio("b1", [1, 4096], F32, "ExternalInput")
    w2 = exio("w2", [1024, 4096], F32, "ExternalInput")
    b2 = exio("b2", [1, 1024], F32, "ExternalInput")
    xout = exio("xout", [TOK, E], F32, "ExternalOutput")

    wq1d = nc.dram_tensor(pf + "wq1d", [4096, 1024], BF16)
    wq2d = nc.dram_tensor(pf + "wq2d", [1024, 4096], BF16)
    xlnd = nc.dram_tensor(pf + "xlnd", [TOK, E], BF16)
    gacc = nc.dram_tensor(pf + "gacc", [256], F32)

    NW1 = float(4096 * 1024)
    NW2 = float(1024 * 4096)

    with ExitStack() as stk:
        En = stk.enter_context
        SV = Eng(Sem(En(nc.semaphore(pf + "sv"))))
        SA = Eng(Sem(En(nc.semaphore(pf + "sa"))))
        SG = Eng(Sem(En(nc.semaphore(pf + "sg"))))
        SP = Eng(Sem(En(nc.semaphore(pf + "sp"))), serialize=False)
        QS = Q()
        QA = Q()
        QG = Q()
        dm = {k: Sem(En(nc.semaphore(pf + k))) for k in
              ["dw0", "dw1", "dx2", "dx3", "dg", "dq0", "dq1", "dx", "dxs0", "dxs1", "dxt", "dost", "dmisc"]}

        onesf = En(nc.sbuf_tensor(pf + "onesf", [1, 128], F32))
        epsbc = En(nc.sbuf_tensor(pf + "epsbc", [128, 1], F32))

        GROUPS = [("w1", g, 1024) for g in range(32)] + [("w2", g, 4096) for g in range(8)]
        w1g = w1.rearrange("(g p) d -> g p d", p=128)
        w2g = w2.rearrange("(g p) d -> g p d", p=128)
        wq1g = wq1d.rearrange("(g p) d -> g p d", p=128)
        wq2g = wq2d.rearrange("(g p) d -> g p d", p=128)
        wsrc = lambda i: (w1g if GROUPS[i][0] == "w1" else w2g)[GROUPS[i][1]]
        wdst = lambda i: (wq1g if GROUPS[i][0] == "w1" else wq2g)[GROUPS[i][1]]
        NG = len(GROUPS)

        with (
            nc.sbuf_tensor(pf + "wtile", [128, 2 * 4096], F32) as wtile,
            nc.sbuf_tensor(pf + "acc8", [128, 48], F32) as acc8,
            nc.sbuf_tensor(pf + "asum", [128, 2], F32) as asum,
            nc.sbuf_tensor(pf + "accrow", [1, 256], F32) as accrow,
            nc.sbuf_tensor(pf + "thr", [1, 2], F32) as thr,
            nc.sbuf_tensor(pf + "thrbc", [128, 2], F32) as thrbc,
            nc.sbuf_tensor(pf + "sgn", [128, 2 * 4096], F32) as sgn,
            nc.sbuf_tensor(pf + "qout", [128, 2 * 4096], BF16) as qout,
            nc.psum_tensor([128, 2], F32) as psb,
        ):
            ldv = {}
            redv = {}
            with nc.Block() as blk:
                @blk.gpsimd
                def _(g):
                    SG.bind(g)
                    SG.do("memset", onesf[:, :], 1.0)
                    SG.do("memset", epsbc[:, :], 1e-5)

                for i in range(NG):
                    W = GROUPS[i][2]
                    sl = i % 2
                    dsem = dm["dw0"] if sl == 0 else dm["dw1"]

                    @blk.sync
                    def _(s, i=i, W=W, sl=sl, dsem=dsem):
                        QS.bind(s)
                        if i >= 2:
                            QS.w(SV.sem, redv[i - 2])
                        ldv[i] = QS.dma(wtile[:, sl * 4096:sl * 4096 + W], wsrc(i), dsem)

                    @blk.vector
                    def _(v, i=i, W=W, sl=sl, dsem=dsem):
                        SV.bind(v)
                        SV.w(dsem, ldv[i])
                        SV.do("tensor_reduce", acc8[:, i:i + 1],
                              wtile[:, sl * 4096:sl * 4096 + W],
                              AX.X, ALU.add, apply_absolute_value=True)
                        redv[i] = SV.sem.val

                @blk.vector
                def _(v):
                    SV.bind(v)
                    SV.do("tensor_reduce", asum[:, 0:1], acc8[:, 0:32], AX.X, ALU.add)
                    SV.do("tensor_reduce", asum[:, 1:2], acc8[:, 32:40], AX.X, ALU.add)
                    ldv["fold"] = SV.sem.val

                @blk.gpsimd
                def _(g):
                    QG.bind(g)
                    QG.w(SV.sem, ldv["fold"])
                    QG.dma(gacc.rearrange("(c p) -> p c", p=128), asum[:, :], dm["dg"],
                           allow_slow_non_contiguous=True)
                    QG.w(dm["dg"], dm["dg"].val)
                    ldv["accrow"] = QG.dma(accrow[:, :], gacc.rearrange("(b a) -> b a", b=1), dm["dg"])

                @blk.vector
                def _(v):
                    SV.bind(v)
                    SV.w(dm["dg"], ldv["accrow"])
                    SV.do("tensor_reduce", thr[:, 0:1], accrow[:1, 0:128], AX.X, ALU.add)
                    SV.do("tensor_reduce", thr[:, 1:2], accrow[:1, 128:256], AX.X, ALU.add)
                    SV.do("tensor_scalar", thr[:, 0:1], thr[:, 0:1], 0.5 / NW1, 0.5e-5, ALU.mult, ALU.add)
                    SV.do("tensor_scalar", thr[:, 1:2], thr[:, 1:2], 0.5 / NW2, 0.5e-5, ALU.mult, ALU.add)
                    ldv["thr"] = SV.sem.val

                @blk.tensor
                def _(t):
                    SP.bind(t)
                    SP.w(SV.sem, ldv["thr"])
                    SP.w(SG.sem, 1)
                    SP.do("matmul", psb[:, :], onesf[:1, :], thr[:1, :], start=True, stop=True)
                    ldv["psb"] = SP.sem.val

                @blk.vector
                def _(v):
                    SV.bind(v)
                    SV.w(SP.sem, ldv["psb"])
                    SV.do("tensor_copy", thrbc[:, :], psb[:, :])
                    ldv["thrbc"] = SV.sem.val

            qst = {}
            with nc.Block() as blk:
                for i in range(NG):
                    tc = 0 if GROUPS[i][0] == "w1" else 1
                    sl = i % 2
                    W = GROUPS[i][2]
                    dsem = dm["dw0"] if sl == 0 else dm["dw1"]
                    stsem = dm["dq0"] if sl == 0 else dm["dq1"]

                    @blk.sync
                    def _(s, i=i, sl=sl, W=W, dsem=dsem):
                        QS.bind(s)
                        if i >= 2:
                            QS.w(SV.sem, qst[("q", i - 2)])
                        ldv[("q", i)] = QS.dma(wtile[:, sl * 4096:sl * 4096 + W], wsrc(i), dsem)

                    @blk.scalar
                    def _(sc, i=i, sl=sl, W=W, dsem=dsem):
                        SA.bind(sc)
                        SA.w(dsem, ldv[("q", i)])
                        if i >= 2:
                            SA.w(SV.sem, qst[("q", i - 2)])
                        SA.do("activation", sgn[:, sl * 4096:sl * 4096 + W],
                              wtile[:, sl * 4096:sl * 4096 + W], AF.Sign)
                        SA.do("activation", wtile[:, sl * 4096:sl * 4096 + W],
                              wtile[:, sl * 4096:sl * 4096 + W], AF.Abs)
                        qst[("sgn", i)] = SA.sem.val

                    @blk.vector
                    def _(v, i=i, sl=sl, tc=tc, W=W, stsem=stsem):
                        SV.bind(v)
                        SV.w(SA.sem, qst[("sgn", i)])
                        SV.w(SV.sem, ldv["thrbc"])
                        if i >= 2:
                            SV.w(stsem, qst[("st", i - 2)])
                        SV.do("tensor_scalar", wtile[:, sl * 4096:sl * 4096 + W],
                              wtile[:, sl * 4096:sl * 4096 + W],
                              thrbc[:, tc:tc + 1], None, ALU.is_ge)
                        SV.do("tensor_tensor", qout[:, sl * 4096:sl * 4096 + W],
                              sgn[:, sl * 4096:sl * 4096 + W],
                              wtile[:, sl * 4096:sl * 4096 + W], ALU.mult)
                        qst[("q", i)] = SV.sem.val

                    @blk.sync
                    def _(s, i=i, sl=sl, W=W, stsem=stsem):
                        QS.bind(s)
                        QS.w(SV.sem, qst[("q", i)])
                        qst[("st", i)] = QS.dma(wdst(i), qout[:, sl * 4096:sl * 4096 + W], stsem)

        w1t = En(nc.sbuf_tensor(pf + "w1t", [128, 8 * 4096], BF16))
        w2t = En(nc.sbuf_tensor(pf + "w2t", [128, 32 * 1024], BF16))
        b1col = En(nc.sbuf_tensor(pf + "b1col", [128, 32], F32))
        b2bc = En(nc.sbuf_tensor(pf + "b2bc", [128, 1024], F32))
        xsb = En(nc.sbuf_tensor(pf + "xsb", [128, 4096], F32))
        xlnb = En(nc.sbuf_tensor(pf + "xlnb", [128, 2 * 1024], F32))
        xlnsb = En(nc.sbuf_tensor(pf + "xlnsb", [128, 1024], BF16))
        xc2 = En(nc.sbuf_tensor(pf + "xc2", [128, 1024], F32))
        red = En(nc.sbuf_tensor(pf + "red", [128, 16], F32))
        xt2 = En(nc.sbuf_tensor(pf + "xt2", [128, 4096], BF16))
        ht = En(nc.sbuf_tensor(pf + "ht", [128, 32 * 512], BF16))
        osb = En(nc.sbuf_tensor(pf + "osb", [128, 1024], F32))
        ph = En(nc.psum_tensor([128, 2048], F32))
        po = En(nc.psum_tensor([128, 1024], F32))

        M = {}
        with (
            nc.psum_tensor([128, 1024], F32) as pb2,
            nc.Block() as blk,
        ):
            b2row = osb
            @blk.scalar
            def _(sc):
                QA.bind(sc)
                QA.w(dm["dq0"])
                QA.w(dm["dq1"])
                QA.dmat(w1t[:, :].rearrange("p (c f) -> p c f", c=8), wq1d.ap(), dm["dxt"])
                M["wt"] = QA.dmat(w2t[:, :].rearrange("p (c f) -> p c f", c=32), wq2d.ap(), dm["dxt"])

            @blk.gpsimd
            def _(g):
                QG.bind(g)
                QG.dma(b1col[:, :].rearrange("p (a c) -> p a c", a=1),
                       b1.rearrange("a (c p) -> p a c", p=128), dm["dmisc"],
                       allow_slow_non_contiguous=True)
                M["b"] = QG.dma(b2row[0:1, :], b2, dm["dmisc"])

            @blk.tensor
            def _(t):
                SP.bind(t)
                SP.w(dm["dmisc"], M["b"])
                SP.raw("matmul", pb2[:, 0:512], onesf[:1, :], b2row[:1, 0:512], start=True, stop=True)
                SP.do("matmul", pb2[:, 512:1024], onesf[:1, :], b2row[:1, 512:1024], start=True, stop=True)
                M["pb2"] = SP.sem.val

            @blk.vector
            def _(v):
                SV.bind(v)
                SV.w(SP.sem, M["pb2"])
                SV.do("tensor_copy", b2bc[:, :], pb2[:, :])
                M["b2bc"] = SV.sem.val

        NT = TOK // 512
        with nc.Block() as blk:
            for j in range(NT):
                off = j * 512

                @blk.sync
                def _(s, j=j, off=off):
                    QS.bind(s)
                    if j == 0:
                        for si in range(2):
                            M[("x2", si)] = QS.dma(
                                xlnb[:, (si % 2) * 1024:(si % 2 + 1) * 1024],
                                xin[si * 128:(si + 1) * 128, :],
                                dm["dx2"] if si % 2 == 0 else dm["dx3"])
                    if j >= 1:
                        QS.w(SV.sem, M[("evac", j - 1)])
                    for u in range(4):
                        QS.dma(xsb[:, u * 1024:(u + 1) * 1024],
                               xin[off + u * 128:off + (u + 1) * 128, :], dm["dx"])
                    M[("x", j)] = dm["dx"].val

                for u in range(4):
                    sidx = j * 4 + u
                    ssl = sidx % 2
                    xs_sem = dm["dxs0"] if ssl == 0 else dm["dxs1"]

                    @blk.vector
                    def _(v, j=j, u=u, sidx=sidx):
                        SV.bind(v)
                        x_ap = xlnb[:, (sidx % 2) * 1024:(sidx % 2 + 1) * 1024]
                        sl = sidx % 2
                        r = [red[:, sl * 8 + k:sl * 8 + k + 1] for k in range(5)]
                        SV.w(dm["dx2"] if sidx % 2 == 0 else dm["dx3"], M[("x2", sidx)])
                        if sidx >= 1:
                            SV.w(SA.sem, M[("xln", sidx - 1)])
                            SV.w(dm["dxs0"], M[("xst", sidx - 1)])
                        SV.do("tensor_reduce", r[0], x_ap, AX.X, ALU.add)
                        SV.do("tensor_scalar", r[1], r[0], -1.0 / E, None, ALU.mult)
                        SV.do("tensor_scalar", xc2[:, :], x_ap, r[1], None, ALU.add)
                        SV.do("scalar_tensor_tensor", xlnsb[:, :], xc2[:, :], 1.0, xc2[:, :],
                              ALU.mult, ALU.mult, accum_out=r[2])
                        M[("stt", sidx)] = SV.sem.val

                    @blk.scalar
                    def _(sc, sidx=sidx, ssl=ssl, xs_sem=xs_sem):
                        SA.bind(sc)
                        sl = sidx % 2
                        r = [red[:, sl * 8 + k:sl * 8 + k + 1] for k in range(5)]
                        SA.w(SV.sem, M[("stt", sidx)])
                        SA.do("activation", r[3], r[2], AF.Ln, bias=epsbc[:, 0:1], scale=1.0 / E)
                        SA.do("activation", r[4], r[3], AF.Exp, bias=0.0, scale=-0.5)
                        SA.do("activation", xlnsb[:, :],
                              xc2[:, :], AF.Copy, bias=0.0, scale=r[4])
                        M[("xln", sidx)] = SA.sem.val

                    @blk.sync
                    def _(s, sidx=sidx, ssl=ssl, off=off, u=u, xs_sem=xs_sem):
                        QS.bind(s)
                        QS.w(SA.sem, M[("xln", sidx)])
                        M[("xst", sidx)] = QS.dma(xlnd.ap()[off + u * 128:off + (u + 1) * 128, :],
                                                  xlnsb[:, :], dm["dxs0"])
                        if sidx + 2 < (TOK // 128):
                            QS.w(SV.sem, M[("stt", sidx)])
                            M[("x2", sidx + 2)] = QS.dma(
                                xlnb[:, (sidx % 2) * 1024:(sidx % 2 + 1) * 1024],
                                xin[(sidx + 2) * 128:(sidx + 3) * 128, :],
                                dm["dx2"] if sidx % 2 == 0 else dm["dx3"])

                @blk.scalar
                def _(sc, j=j, off=off):
                    QA.bind(sc)
                    QA.w(dm["dxs0"], M[("xst", j * 4 + 3)])
                    if j >= 1:
                        QA.w(SP.sem, M[("hmm", j - 1, 31)])
                    M[("xt", j)] = QA.dmat(
                        xt2[:, :].rearrange("p (c t) -> p c t", c=8),
                        xlnd.ap()[off:off + 512, :], dm["dxt"])

                for m in range(32):
                    @blk.tensor
                    def _(t, j=j, m=m):
                        SP.bind(t)
                        if m == 0:
                            SP.w(dm["dxt"], M[("xt", j)])
                        rp = j * 32 + m - 4
                        if rp >= 0:
                            SP.w(SA.sem, M[("relu", rp // 32, rp % 32)])
                        bank = m % 4
                        for c in range(8):
                            mi = SP.raw("matmul", ph[:, bank * 512:(bank + 1) * 512],
                                        w1t[:, c * 4096 + m * 128:c * 4096 + (m + 1) * 128],
                                        xt2[:, c * 512:(c + 1) * 512],
                                        start=(c == 0), stop=(c == 7))
                            if c == 7:
                                mi.then_inc(SP.sem.inc(1), 1)
                        M[("hmm", j, m)] = SP.sem.val

                    @blk.scalar
                    def _(sc, j=j, m=m):
                        SA.bind(sc)
                        SA.w(SP.sem, M[("hmm", j, m)])
                        bank = m % 4
                        SA.do("activation", ht[:, m * 512:(m + 1) * 512],
                              ph[:, bank * 512:(bank + 1) * 512],
                              AF.Relu, bias=b1col[:, m:m + 1], scale=1.0)
                        M[("relu", j, m)] = SA.sem.val

                @blk.vector
                def _(v, j=j):
                    SV.bind(v)
                    SV.w(SV.sem, M["b2bc"])
                    SV.w(SV.sem, M[("stt", j * 4 + 3)])
                    SV.w(dm["dx"], M[("x", j)])
                    for u in range(4):
                        SV.do("tensor_tensor", xsb[:, u * 1024:(u + 1) * 1024],
                              xsb[:, u * 1024:(u + 1) * 1024], b2bc[:, :], ALU.add)
                    M[("xb", j)] = SV.sem.val

                for idx in range(8):
                    ts, dh = idx // 2, idx % 2

                    @blk.tensor
                    def _(t, j=j, idx=idx, ts=ts, dh=dh):
                        SP.bind(t)
                        if idx == 0:
                            SP.w(SA.sem, M[("relu", j, 31)])
                        og = j * 8 + idx - 2
                        if og >= 0:
                            SP.w(SV.sem, M[("evacg", og)])
                        bank = idx % 2
                        for m in range(32):
                            mi = SP.raw("matmul", po[:, bank * 512:(bank + 1) * 512],
                                        ht[:, m * 512 + ts * 128:m * 512 + (ts + 1) * 128],
                                        w2t[:, m * 1024 + dh * 512:m * 1024 + (dh + 1) * 512],
                                        start=(m == 0), stop=(m == 31))
                            if m == 31:
                                mi.then_inc(SP.sem.inc(1), 1)
                        M[("omm", j, idx)] = SP.sem.val

                    @blk.vector
                    def _(v, j=j, idx=idx, ts=ts, dh=dh):
                        SV.bind(v)
                        SV.w(SP.sem, M[("omm", j, idx)])
                        SV.w(SV.sem, M[("xb", j)])
                        if j * 4 + ts >= 1:
                            SV.w(dm["dost"], M.get(("ost", j * 4 + ts - 1), 0))
                        bank = idx % 2
                        SV.do("tensor_tensor", osb[:, dh * 512:(dh + 1) * 512],
                              po[:, bank * 512:(bank + 1) * 512],
                              xsb[:, ts * 1024 + dh * 512:ts * 1024 + (dh + 1) * 512],
                              ALU.add)
                        M[("evacg", j * 8 + idx)] = SV.sem.val
                        M[("evac", j)] = SV.sem.val

                    if dh == 1:
                        @blk.sync
                        def _(s, j=j, ts=ts, off=off):
                            QS.bind(s)
                            QS.w(SV.sem, M[("evacg", j * 8 + ts * 2 + 1)])
                            M[("ost", j * 4 + ts)] = QS.dma(
                                xout[off + ts * 128:off + (ts + 1) * 128, :], osb[:, :], dm["dost"])

            @blk.sync
            def _(s):
                s.wait_ge(dm["dost"].h, dm["dost"].val)

    return nc



def run_b(x1_full, inputs):
    nc = build_stage_b()
    maps = []
    for c in range(N_CORES):
        maps.append(dict(
            xin=np.ascontiguousarray(x1_full[c * TOK:(c + 1) * TOK]),
            w1=inputs["L0_ffn_W1"], b1=inputs["L0_ffn_b1"].reshape(1, -1),
            w2=inputs["L0_ffn_W2"], b2=inputs["L0_ffn_b2"].reshape(1, -1),
        ))
    res = run_bass_kernel_spmd(nc, maps, core_ids=list(range(N_CORES)))
    return np.concatenate([res.results[c]["xout"] for c in range(N_CORES)], axis=0)





def build_stage_c(nc=None, pf="", ext=None, ctx=None, phases=4):
    ext = ext or {}
    if nc is None:
        nc = bass.Bass("TRN2", target_bir_lowering=False)

    def exio(name, shape, dt, kind):
        return ext[name] if name in ext else nc.dram_tensor(name, shape, dt, kind=kind).ap()
    xin = exio("xin", [TOK, E], F32, "ExternalInput")
    wq = exio("wq", [E, E], F32, "ExternalInput")
    wk = exio("wk", [E, E], F32, "ExternalInput")
    wv = exio("wv", [E, E], F32, "ExternalInput")
    wo = exio("wo", [E, E], F32, "ExternalInput")
    bq = exio("bq", [1, E], F32, "ExternalInput")
    bk = exio("bk", [1, E], F32, "ExternalInput")
    bv = exio("bv", [1, E], F32, "ExternalInput")
    bo = exio("bo", [1, E], F32, "ExternalInput")
    xout = exio("xout", [TOK, E], F32, "ExternalOutput")

    wbd = {n: nc.dram_tensor(pf + f"w{n}bd", [E, E], BF16) for n in "qkvo"}
    xlnd = nc.dram_tensor(pf + "xlnd", [L, E], BF16)
    qtd = nc.dram_tensor(pf + "qtd", [8, 128, L], BF16)

    with ExitStack() as stk:
        En = stk.enter_context
        if ctx is None:
            SV = Eng(Sem(En(nc.semaphore(pf + "sv"))))
            SA = Eng(Sem(En(nc.semaphore(pf + "sa"))))
            SG = Eng(Sem(En(nc.semaphore(pf + "sg"))))
            SP = Eng(Sem(En(nc.semaphore(pf + "sp"))), serialize=False)
        else:
            SV, SA, SG, SP = ctx["SV"], ctx["SA"], ctx["SG"], ctx["SP"]
        QS = Q()
        QA = Q()
        QG = Q()
        dm = {k: Sem(En(nc.semaphore(pf + k))) for k in
              ["dwc", "dx", "dxs0", "dxs1", "dxt", "dwt", "dqs0", "dqs1", "dql", "dost", "dmisc", "dres0", "dres1"]}

        onesf = En(nc.sbuf_tensor(pf + "onesf", [1, 128], F32))
        epsbc = En(nc.sbuf_tensor(pf + "epsbc", [128, 1], F32))
        bcol = En(nc.sbuf_tensor(pf + "bcol", [128, 16], F32))
        bvbc = En(nc.sbuf_tensor(pf + "bvbc", [128, 1024], F32))
        bobc = En(nc.sbuf_tensor(pf + "bobc", [128, 1024], F32))
        ksb = En(nc.sbuf_tensor(pf + "ksb", [128, 8 * L], BF16))
        vsb = En(nc.sbuf_tensor(pf + "vsb", [128, 32 * E], BF16))
        wtb = En(nc.sbuf_tensor(pf + "wtb", [128, 8 * 1024], BF16))
        xts = En(nc.sbuf_tensor(pf + "xts", [128, 4096], BF16))
        xlnsb = En(nc.sbuf_tensor(pf + "xlnsb", [128, 2 * 1024], BF16))
        xc2 = En(nc.sbuf_tensor(pf + "xc2", [128, 2 * 1024], F32))
        red = En(nc.sbuf_tensor(pf + "red", [128, 16], F32))
        xsb = En(nc.sbuf_tensor(pf + "xsb", [128, 2048], F32))
        qts = En(nc.sbuf_tensor(pf + "qts", [128, 4096], BF16))
        ptl = En(nc.sbuf_tensor(pf + "ptl", [128, 2 * 1024], BF16))
        aout = En(nc.sbuf_tensor(pf + "aout", [128, 8 * 512], BF16))
        qbuf = En(nc.sbuf_tensor(pf + "qbuf", [128, 2 * 512], BF16))
        osb = En(nc.sbuf_tensor(pf + "osb", [128, 1024], F32))
        ps_s = En(nc.psum_tensor([128, 4 * 512], F32))
        ps_pv = En(nc.psum_tensor([128, 2 * 512], F32))
        ps_o = En(nc.psum_tensor([128, 2 * 512], F32))

        M = {}

        with nc.Block() as blk:
            @blk.gpsimd
            def _(g):
                SG.bind(g)
                SG.do("memset", onesf[:, :], 1.0)
                SG.do("memset", epsbc[:, :], 1e-5)

            @blk.gpsimd
            def _(g):
                QG.bind(g)
                for i, (nm, w) in enumerate(zip("qkvo", [wq, wk, wv, wo])):
                    stg = ksb[:, (i % 2) * 8192:(i % 2) * 8192 + 8192]
                    QG.dma(stg.rearrange("p (c d) -> p c d", c=8),
                           w.rearrange("(c p) d -> p c d", p=128), dm["dwc"])
                    QG.w(dm["dwc"], dm["dwc"].val)
                    M[("wc", i)] = QG.dma(wbd[nm].ap().rearrange("(c p) d -> p c d", p=128),
                                          stg.rearrange("p (c d) -> p c d", c=8), dm["dwc"])
                    QG.w(dm["dwc"], dm["dwc"].val)
                QG.dma(bcol[:, 0:8].rearrange("p (a c) -> p a c", a=1),
                       bq.rearrange("a (c p) -> p a c", p=128), dm["dmisc"],
                       allow_slow_non_contiguous=True)
                QG.dma(bcol[:, 8:16].rearrange("p (a c) -> p a c", a=1),
                       bk.rearrange("a (c p) -> p a c", p=128), dm["dmisc"],
                       allow_slow_non_contiguous=True)
                QG.dma(osb[0:1, 0:1024], bv, dm["dmisc"])
                M["bst"] = QG.dma(xc2[0:1, 0:1024], bo, dm["dmisc"])

            @blk.tensor
            def _(t):
                SP.bind(t)
                SP.w(dm["dmisc"], M["bst"])
                SP.w(SG.sem, 1)
                SP.raw("matmul", ps_s[:, 0:512], onesf[:1, :], osb[:1, 0:512], start=True, stop=True)
                SP.raw("matmul", ps_s[:, 512:1024], onesf[:1, :], osb[:1, 512:1024], start=True, stop=True)
                SP.raw("matmul", ps_pv[:, 0:512], onesf[:1, :], xc2[:1, 0:512], start=True, stop=True)
                SP.do("matmul", ps_pv[:, 512:1024], onesf[:1, :], xc2[:1, 512:1024], start=True, stop=True)
                M["bmm"] = SP.sem.val

            @blk.vector
            def _(v):
                SV.bind(v)
                SV.w(SP.sem, M["bmm"])
                SV.do("tensor_copy", bvbc[:, :], ps_s[:, 0:1024])
                SV.do("tensor_copy", bobc[:, :], ps_pv[:, 0:1024])
                M["bprep"] = SV.sem.val

        for b in range(BPC):
            B0 = b * L
            with nc.Block() as blk:
                for u in range(L // 128):
                    sidx = u
                    ssl = u % 2
                    xs_sem = dm["dxs0"] if ssl == 0 else dm["dxs1"]
                    xl = u % 2

                    @blk.sync
                    def _(s, b=b, u=u, xl=xl, sidx=sidx):
                        QS.bind(s)
                        if u >= 2:
                            QS.w(SV.sem, M[("c1stt", b, u - 2)])
                        elif b > 0:
                            QS.w(SV.sem, M[("oevac", b - 1, 7, 7)])
                        M[("c1x", b, u)] = QS.dma(xsb[:, xl * 1024:(xl + 1) * 1024],
                                                  xin[B0 + u * 128:B0 + (u + 1) * 128, :], dm["dx"])

                    @blk.vector
                    def _(v, b=b, u=u, xl=xl, sidx=sidx):
                        SV.bind(v)
                        x_ap = xsb[:, xl * 1024:(xl + 1) * 1024]
                        sl = sidx % 2
                        r = [red[:, sl * 8 + k:sl * 8 + k + 1] for k in range(5)]
                        SV.w(dm["dx"], M[("c1x", b, u)])
                        if b == 0 and u == 0:
                            SV.w(SP.sem, M["bmm"])
                        if sidx >= 2:
                            SV.w(SA.sem, M[("c1xln", b, sidx - 2)])
                        elif b > 0 and sidx == 0:
                            SV.w(SA.sem, M[("c1xln", b - 1, 16)])
                        xc = xc2[:, sl * 1024:(sl + 1) * 1024]
                        if sidx >= 2:
                            SV.w(xs_sem, M[("c1xst", b, sidx - 2)])
                        SV.do("tensor_reduce", r[0], x_ap, AX.X, ALU.add)
                        SV.do("tensor_scalar", r[1], r[0], -1.0 / E, None, ALU.mult)
                        SV.do("tensor_scalar", xc, x_ap, r[1], None, ALU.add)
                        SV.do("scalar_tensor_tensor", xlnsb[:, ssl * 1024:(ssl + 1) * 1024],
                              xc, 1.0, xc, ALU.mult, ALU.mult, accum_out=r[2])
                        M[("c1stt", b, sidx)] = SV.sem.val

                    @blk.scalar
                    def _(sc, b=b, sidx=sidx, ssl=ssl, xs_sem=xs_sem):
                        SA.bind(sc)
                        sl = sidx % 2
                        r = [red[:, sl * 8 + k:sl * 8 + k + 1] for k in range(5)]
                        SA.w(SV.sem, M[("c1stt", b, sidx)])
                        if sidx >= 2:
                            SA.w(xs_sem, M[("c1xst", b, sidx - 2)])
                        SA.do("activation", r[3], r[2], AF.Ln, bias=epsbc[:, 0:1], scale=1.0 / E)
                        SA.do("activation", r[4], r[3], AF.Exp, bias=0.0, scale=-0.5)
                        SA.do("activation", xlnsb[:, ssl * 1024:(ssl + 1) * 1024],
                              xc2[:, sidx % 2 * 1024:(sidx % 2 + 1) * 1024],
                              AF.Copy, bias=0.0, scale=r[4])
                        M[("c1xln", b, sidx)] = SA.sem.val

                    @blk.sync
                    def _(s, b=b, u=u, sidx=sidx, ssl=ssl, xs_sem=xs_sem):
                        QS.bind(s)
                        QS.w(SA.sem, M[("c1xln", b, sidx)])
                        M[("c1xst", b, sidx)] = QS.dma(
                            xlnd.ap()[u * 128:(u + 1) * 128, :],
                            xlnsb[:, ssl * 1024:(ssl + 1) * 1024], xs_sem)

            for iw, nm in enumerate(["k", "v", "q"]):
                with nc.Block() as blk:
                    @blk.scalar
                    def _(sc, b=b, iw=iw, nm=nm):
                        QA.bind(sc)
                        QA.w(dm["dwc"], M[("wc", {"k": 1, "v": 2, "q": 0}[nm])])
                        if iw > 0:
                            QA.w(SP.sem, M[("pmm", b, iw - 1, 7)])
                        elif b > 0:
                            QA.w(SP.sem, M[("omm", b - 1, 7, 7)])
                        M[("wt", b, iw)] = QA.dmat(wtb[:, :].rearrange("p (c e) -> p c e", c=8),
                                                   wbd[nm].ap(), dm["dwt"])

                    for j in range(8):
                        @blk.scalar
                        def _(sc, b=b, iw=iw, j=j, nm=nm):
                            QA.bind(sc)
                            pj = M.get(("pmm", b, iw, j - 1)) if j > 0 else \
                                (M.get(("pmm", b, iw - 1, 7)) if iw > 0 else M.get(("pmm", b - 1, 2, 7)))
                            if pj:
                                QA.w(SP.sem, pj)
                            QA.w(dm["dxs0"], M[("c1xst", b, (L // 128) - 2)])
                            QA.w(dm["dxs1"], M[("c1xst", b, (L // 128) - 1)])
                            M[("xt", b, iw, j)] = QA.dmat(
                                xts[:, :].rearrange("p (c t) -> p c t", c=8),
                                xlnd.ap()[j * 512:(j + 1) * 512, :], dm["dxt"])

                        @blk.tensor
                        def _(t, b=b, iw=iw, nm=nm, j=j):
                            SP.bind(t)
                            SP.w(dm["dwt"], M[("wt", b, iw)])
                            SP.w(dm["dxt"], M[("xt", b, iw, j)])

                        for m in range(8):
                            @blk.tensor
                            def _(t, b=b, iw=iw, nm=nm, j=j, m=m):
                                SP.bind(t)
                                SP.w(SV.sem, M["bprep"])
                                pr = M.get(("pevac", b, iw, j, m - 2)) if m >= 2 else \
                                    (M.get(("pevac", b, iw, j - 1, 6 + m)) if j > 0 else
                                     (M.get(("pevac", b, iw - 1, 7, 6 + m)) if iw > 0 else
                                      M.get(("pevac", b - 1, 2, 7, 6 + m))))
                                if pr:
                                    SP.w(SV.sem, pr)
                                if nm in ("k", "q"):
                                    for c in range(8):
                                        mi = SP.raw("matmul", ps_s[:, (m % 2) * 512:(m % 2 + 1) * 512],
                                                    wtb[:, c * 1024 + m * 128:c * 1024 + (m + 1) * 128],
                                                    xts[:, c * 512:(c + 1) * 512],
                                                    start=(c == 0), stop=(c == 7))
                                        if c == 7:
                                            mi.then_inc(SP.sem.inc(1), 1)
                                else:
                                    ts, dh = m // 2, m % 2
                                    for c in range(8):
                                        mi = SP.raw("matmul", ps_s[:, (m % 2) * 512:(m % 2 + 1) * 512],
                                                    xts[:, c * 512 + ts * 128:c * 512 + (ts + 1) * 128],
                                                    wtb[:, c * 1024 + dh * 512:c * 1024 + (dh + 1) * 512],
                                                    start=(c == 0), stop=(c == 7))
                                        if c == 7:
                                            mi.then_inc(SP.sem.inc(1), 1)
                                M[("pmmg", b, iw, j, m)] = SP.sem.val
                                if m == 7:
                                    M[("pmm", b, iw, j)] = SP.sem.val

                            if nm == "k":
                                @blk.vector
                                def _(v, b=b, iw=iw, j=j, m=m):
                                    SV.bind(v)
                                    SV.w(dm["dmisc"], M["bst"])
                                    SV.w(dm["dwc"], M[("wc", 3)])
                                    SV.w(SP.sem, M[("pmmg", b, iw, j, m)])
                                    SV.do("tensor_scalar", ksb[:, m * L + j * 512:m * L + (j + 1) * 512],
                                          ps_s[:, (m % 2) * 512:(m % 2 + 1) * 512],
                                          bcol[:, 8 + m:9 + m], None, ALU.add)
                                    M[("pevac", b, iw, j, m)] = SV.sem.val
                            elif nm == "q":
                                @blk.vector
                                def _(v, b=b, iw=iw, j=j, m=m):
                                    SV.bind(v)
                                    SV.w(SV.sem, M["bprep"])
                                    SV.w(SP.sem, M[("pmmg", b, iw, j, m)])
                                    qsl = m % 2
                                    if (j * 8 + m) >= 2:
                                        SV.w(dm["dqs0"] if qsl == 0 else dm["dqs1"],
                                             M.get(("qst", b, j * 8 + m - 2), 0))
                                    SV.do("tensor_scalar", qbuf[:, qsl * 512:(qsl + 1) * 512],
                                          ps_s[:, (m % 2) * 512:(m % 2 + 1) * 512],
                                          bcol[:, m:m + 1], 30.0 / 8.0, ALU.add, ALU.mult)
                                    M[("pevac", b, iw, j, m)] = SV.sem.val
                                    M[("qact", b, j * 8 + m)] = SV.sem.val

                                @blk.sync
                                def _(s, b=b, j=j, m=m):
                                    QS.bind(s)
                                    qsl = m % 2
                                    QS.w(SV.sem, M[("qact", b, j * 8 + m)])
                                    M[("qst", b, j * 8 + m)] = QS.dma(
                                        qtd.ap()[m, :, j * 512:(j + 1) * 512],
                                        qbuf[:, qsl * 512:(qsl + 1) * 512],
                                        dm["dqs0"] if qsl == 0 else dm["dqs1"])
                            else:
                                @blk.vector
                                def _(v, b=b, iw=iw, j=j, m=m):
                                    SV.bind(v)
                                    ts, dh = m // 2, m % 2
                                    SV.w(SP.sem, M[("pmmg", b, iw, j, m)])
                                    SV.do("tensor_tensor",
                                          vsb[:, (j * 4 + ts) * 1024 + dh * 512:(j * 4 + ts) * 1024 + (dh + 1) * 512],
                                          ps_s[:, (m % 2) * 512:(m % 2 + 1) * 512],
                                          bvbc[:, dh * 512:(dh + 1) * 512], ALU.add)
                                    M[("pevac", b, iw, j, m)] = SV.sem.val

            with nc.Block() as blk:
                for jq in range(8):
                    @blk.scalar
                    def _(sc, b=b, jq=jq):
                        QA.bind(sc)
                        if jq > 0:
                            QA.w(SP.sem, M[("smm", b, jq - 1, 7, 31)])
                        QA.w(dm["dqs0"], M[("qst", b, 62)])
                        QA.w(dm["dqs1"], M[("qst", b, 63)])
                        M[("ql", b, jq)] = QA.dma(
                            qts[:, :].rearrange("p (c t) -> p c t", c=8),
                            qtd.ap().rearrange("c p t -> p c t")[:, :, jq * 512:(jq + 1) * 512], dm["dql"])


                    for hh in range(8):
                        hA, hB = 2 * hh, 2 * hh + 1
                        sa0 = SA.sem.val
                        pv_prev = M.get(("pvg_last", b))

                        @blk.tensor
                        def _(t, b=b, jq=jq, hh=hh, hA=hA, hB=hB, sa0=sa0):
                            SP.bind(t)
                            if hh == 0:
                                SP.w(dm["dql"], M[("ql", b, jq)])

                            def emit_s(kc):
                                for hi, h in enumerate((hA, hB)):
                                    hp = 64 * hi
                                    bank = (2 * kc + hi) % 4
                                    if kc < 2:
                                        SP.w(SA.sem, sa0)
                                    else:
                                        SP.w(SA.sem, sa0 + kc - 1)
                                    SP.raw("matmul", ps_s[:, bank * 512:(bank + 1) * 512],
                                           ksb[hp:hp + 64, hh * L + kc * 128:hh * L + (kc + 1) * 128],
                                           qts[hp:hp + 64, hh * 512:(hh + 1) * 512],
                                           start=True, stop=True).then_inc(SP.sem.inc(1), 1)
                                M[("smm", b, jq, hh, kc)] = SP.sem.val

                            def emit_pv(kc):
                                SP.w(SA.sem, sa0 + kc + 1)
                                if kc == 0:
                                    pr = (M.get(("aevac", b, jq, hh - 1)) if hh > 0
                                          else M.get(("aevac", b, jq - 1, 7)))
                                    if pr:
                                        SP.w(SV.sem, pr)
                                sl = (kc % 2) * 1024
                                SP.raw("matmul", ps_pv[0:64, 0:512],
                                       vsb[:, kc * 1024 + 64 * hA:kc * 1024 + 64 * hA + 64],
                                       ptl[:, sl:sl + 512],
                                       start=(kc == 0), stop=(kc == 31)).then_inc(SP.sem.inc(1), 1)
                                SP.raw("matmul", ps_pv[64:128, 512:1024],
                                       vsb[:, kc * 1024 + 64 * hB:kc * 1024 + 64 * hB + 64],
                                       ptl[:, sl + 512:sl + 1024],
                                       start=(kc == 0), stop=(kc == 31),
                                       tile_position=(0, 64)).then_inc(SP.sem.inc(1), 1)
                                M[("pvmm", b, jq, hh, kc)] = SP.sem.val

                            for kc in range(32):
                                emit_s(kc)
                                if kc >= 2:
                                    emit_pv(kc - 2)
                            for kc in range(30, 32):
                                emit_pv(kc)
                            M[("pvg", b, jq, hh)] = SP.sem.val
                            M[("pvg_last", b)] = SP.sem.val

                        @blk.scalar
                        def _(sc, b=b, jq=jq, hh=hh, sa0=sa0, pv_prev=pv_prev):
                            SA.bind(sc)
                            for kc in range(32):
                                SA.w(SP.sem, M[("smm", b, jq, hh, kc)])
                                if kc >= 2:
                                    SA.w(SP.sem, M[("pvmm", b, jq, hh, kc - 2)])
                                elif pv_prev:
                                    SA.w(SP.sem, pv_prev)
                                sl2 = (2 * kc) % 4
                                SA.do("activation", ptl[:, (kc % 2) * 1024:(kc % 2 + 1) * 1024],
                                      ps_s[:, sl2 * 512:(sl2 + 2) * 512], AF.Sin)
                            assert SA.sem.val == sa0 + 32

                        @blk.vector
                        def _(v, b=b, jq=jq, hh=hh, hA=hA, hB=hB):
                            SV.bind(v)
                            SV.w(SP.sem, M[("pvg", b, jq, hh)])
                            SV.do("tensor_copy", aout[0:64, hh * 512:(hh + 1) * 512],
                                  ps_pv[0:64, 0:512])
                            SV.do("tensor_copy", aout[64:128, hh * 512:(hh + 1) * 512],
                                  ps_pv[64:128, 512:1024])
                            M[("aevac", b, jq, hh)] = SV.sem.val

                    @blk.scalar
                    def _(sc, b=b, jq=jq):
                        QA.bind(sc)
                        if jq == 0:
                            QA.w(dm["dwc"], M[("wc", 3)])
                            QA.w(SP.sem, M[("pmm", b, 2, 7)])
                            M[("wo_t", b)] = QA.dmat(wtb[:, :].rearrange("p (c e) -> p c e", c=8),
                                                     wbd["o"].ap(), dm["dwt"])
                    for g in range(8):
                        ts, dh = g // 2, g % 2

                        if dh == 0:
                            @blk.sync
                            def _(s, b=b, jq=jq, ts=ts):
                                QS.bind(s)
                                tsg = jq * 4 + ts
                                rsem = dm["dres0"] if ts % 2 == 0 else dm["dres1"]
                                if tsg >= 2:
                                    QS.w(SV.sem, M.get(("resfree", b, tsg - 2), 0))
                                M[("res", b, tsg)] = QS.dma(
                                    xsb[:, (ts % 2) * 1024:(ts % 2 + 1) * 1024],
                                    xin[B0 + jq * 512 + ts * 128:B0 + jq * 512 + (ts + 1) * 128, :], rsem)

                        @blk.tensor
                        def _(t, b=b, jq=jq, g=g, ts=ts, dh=dh):
                            SP.bind(t)
                            if g == 0:
                                SP.w(dm["dwt"], M[("wo_t", b)])
                                SP.w(SV.sem, M[("aevac", b, jq, 7)])
                            og = M.get(("oevac", b, jq, g - 2)) if g >= 2 else M.get(("oevac", b, jq - 1, 6 + g))
                            if og:
                                SP.w(SV.sem, og)
                            for m in range(8):
                                mi = SP.raw("matmul", ps_o[:, (g % 2) * 512:(g % 2 + 1) * 512],
                                            aout[:, m * 512 + ts * 128:m * 512 + (ts + 1) * 128],
                                            wtb[:, m * 1024 + dh * 512:m * 1024 + (dh + 1) * 512],
                                            start=(m == 0), stop=(m == 7))
                                if m == 7:
                                    mi.then_inc(SP.sem.inc(1), 1)
                            M[("omm", b, jq, g)] = SP.sem.val

                        @blk.vector
                        def _(v, b=b, jq=jq, g=g, ts=ts, dh=dh):
                            SV.bind(v)
                            SV.w(SP.sem, M[("omm", b, jq, g)])
                            tsg = jq * 4 + ts
                            rsem = dm["dres0"] if ts % 2 == 0 else dm["dres1"]
                            SV.w(rsem, M[("res", b, tsg)])
                            if tsg >= 1:
                                SV.w(dm["dost"], M.get(("ost", b, tsg - 1), 0))
                            elif b > 0:
                                SV.w(dm["dost"], M.get(("ost", b - 1, 31), 0))
                            SV.do("scalar_tensor_tensor", osb[:, dh * 512:(dh + 1) * 512],
                                  ps_o[:, (g % 2) * 512:(g % 2 + 1) * 512], 1.0,
                                  bobc[:, dh * 512:(dh + 1) * 512], ALU.mult, ALU.add)
                            SV.do("tensor_tensor", osb[:, dh * 512:(dh + 1) * 512],
                                  osb[:, dh * 512:(dh + 1) * 512],
                                  xsb[:, (ts % 2) * 1024 + dh * 512:(ts % 2) * 1024 + (dh + 1) * 512], ALU.add)
                            M[("oevac", b, jq, g)] = SV.sem.val
                            if dh == 1:
                                M[("resfree", b, tsg)] = SV.sem.val

                        if dh == 1:
                            @blk.sync
                            def _(s, b=b, jq=jq, ts=ts):
                                QS.bind(s)
                                QS.w(SV.sem, M[("oevac", b, jq, ts * 2 + 1)])
                                M[("ost", b, jq * 4 + ts)] = QS.dma(
                                    xout[B0 + jq * 512 + ts * 128:B0 + jq * 512 + (ts + 1) * 128, :],
                                    osb[:, :], dm["dost"])

        with nc.Block() as blk:
            @blk.sync
            def _(s):
                for _ds in dm.values():
                    if _ds.val:
                        s.wait_ge(_ds.h, _ds.val)

    return nc


def run_c(x2_full, inputs):
    nc = build_stage_c()
    maps = []
    for c in range(N_CORES):
        maps.append(dict(
            xin=np.ascontiguousarray(x2_full[c * TOK:(c + 1) * TOK]),
            wq=inputs["L1_attn_Wq"], bq=inputs["L1_attn_bq"].reshape(1, -1),
            wk=inputs["L1_attn_Wk"], bk=inputs["L1_attn_bk"].reshape(1, -1),
            wv=inputs["L1_attn_Wv"], bv=inputs["L1_attn_bv"].reshape(1, -1),
            wo=inputs["L1_attn_Wo"], bo=inputs["L1_attn_bo"].reshape(1, -1),
        ))
    res = run_bass_kernel_spmd(nc, maps, core_ids=list(range(N_CORES)))
    return np.concatenate([res.results[c]["xout"] for c in range(N_CORES)], axis=0)




def apv(base, extra_off, dims):
    return bass.AP(base.tensor, base.offset + extra_off, [list(base.ap[0])] + dims)



TA = 2176
NW = 16


def build_stage_a(nc=None, pf="", ext=None, ctx=None, phases=4):
    ext = ext or {}
    if nc is None:
        nc = bass.Bass("TRN2", target_bir_lowering=False)

    def exio(name, shape, dt, kind):
        return ext[name] if name in ext else nc.dram_tensor(name, shape, dt, kind=kind).ap()
    xin = exio("xin", [TOK, E], F32, "ExternalInput")
    wq1 = exio("wq1", [E, E], F32, "ExternalInput")
    wq2 = exio("wq2", [E, E], F32, "ExternalInput")
    wk1 = exio("wk1", [E, 128], F32, "ExternalInput")
    wk2 = exio("wk2", [E, 128], F32, "ExternalInput")
    wv = exio("wv", [E, 128], F32, "ExternalInput")
    wout = exio("wout", [E, E], F32, "ExternalInput")
    bout = exio("bout", [1, E], F32, "ExternalInput")
    lqk = exio("lqk", [1, 256], F32, "ExternalInput")
    sintok = exio("sintok", [TA, 32], F32, "ExternalInput")
    costok = exio("costok", [TA, 32], F32, "ExternalInput")
    xout = exio("xout", [TOK, E], F32, "ExternalOutput")

    xlnd = nc.dram_tensor(pf + "xlnd", [TA, E], BF16)
    q1d = nc.dram_tensor(pf + "q1d", [TA, E], BF16)
    q2d = nc.dram_tensor(pf + "q2d", [TA, E], BF16)
    k1d = nc.dram_tensor(pf + "k1d", [TA, 128], BF16)
    k2d = nc.dram_tensor(pf + "k2d", [TA, 128], BF16)
    rrd = nc.dram_tensor(pf + "rrd", [2, 512], BF16)

    with ExitStack() as stk:
        En = stk.enter_context
        if ctx is None:
            SV = Eng(Sem(En(nc.semaphore(pf + "sv"))))
            SA = Eng(Sem(En(nc.semaphore(pf + "sa"))))
            SG = Eng(Sem(En(nc.semaphore(pf + "sg"))))
            SP = Eng(Sem(En(nc.semaphore(pf + "sp"))), serialize=False)
        else:
            SV, SA, SG, SP = ctx["SV"], ctx["SA"], ctx["SG"], ctx["SP"]
        QS = Q()
        QA = Q()
        QG = Q()
        dm = {k: Sem(En(nc.semaphore(pf + k))) for k in
              ["dwl", "dx", "dxs0", "dxs1", "dxt0", "dxt1", "dq0", "dq1", "dk", "dql",
               "dost", "dmisc", "dres0", "dres1", "dqb1", "dqb2"]}

        onesf = En(nc.sbuf_tensor(pf + "onesf", [1, 128], F32))
        onesb = En(nc.sbuf_tensor(pf + "onesb", [128, 1], BF16))
        epsbc = En(nc.sbuf_tensor(pf + "epsbc", [128, 1], F32))
        wq1b = En(nc.sbuf_tensor(pf + "wq1b", [128, 8 * E], BF16))
        wq2b = En(nc.sbuf_tensor(pf + "wq2b", [128, 8 * E], BF16))
        wkvb = En(nc.sbuf_tensor(pf + "wkvb", [128, 3 * 8 * 128], BF16))
        woutb = En(nc.sbuf_tensor(pf + "woutb", [128, 8 * E], BF16))
        boutbc = En(nc.sbuf_tensor(pf + "boutbc", [128, E], F32))
        sctok = En(nc.sbuf_tensor(pf + "sctok", [128, 17 * 64], F32))
        lambc = En(nc.sbuf_tensor(pf + "lamb", [128, 2], F32))
        vsb = En(nc.sbuf_tensor(pf + "vsb", [128, 17 * 128], BF16))
        q1t = En(nc.sbuf_tensor(pf + "q1t", [128, 8 * TA], BF16))
        q2t = En(nc.sbuf_tensor(pf + "q2t", [128, 8 * TA], BF16))
        k1t = En(nc.sbuf_tensor(pf + "k1t", [128, TA], BF16))
        k2t = En(nc.sbuf_tensor(pf + "k2t", [128, TA], BF16))
        xts = En(nc.sbuf_tensor(pf + "xts", [128, 2 * 8 * 128], BF16))
        xlnsb = En(nc.sbuf_tensor(pf + "xlnsb", [128, 2 * 1024], BF16))
        xc2 = En(nc.sbuf_tensor(pf + "xc2", [128, 1024], F32))
        sqs = En(nc.sbuf_tensor(pf + "sqs", [128, 1024], BF16))
        red = En(nc.sbuf_tensor(pf + "red", [128, 16], F32))
        xsb = En(nc.sbuf_tensor(pf + "xsb", [128, 2048], F32))
        rope = En(nc.sbuf_tensor(pf + "rope", [128, 1024], F32))
        qrb = En(nc.sbuf_tensor(pf + "qrb", [128, 2 * 1024], BF16))
        kvb = En(nc.sbuf_tensor(pf + "kvb", [128, 256], BF16))
        ptl = En(nc.sbuf_tensor(pf + "ptl", [128, 2 * 1024], BF16))
        rrow = En(nc.sbuf_tensor(pf + "rrow", [1, 512], F32))
        rrbf = En(nc.sbuf_tensor(pf + "rrbf", [128, 1024], BF16))
        rbsb = En(nc.sbuf_tensor(pf + "rbsb", [128, 1024], BF16))
        scrT = En(nc.sbuf_tensor(pf + "scrT", [128, 512], F32))
        atb = En(nc.sbuf_tensor(pf + "atb", [128, 2 * 2048], BF16))
        praw = En(nc.sbuf_tensor(pf + "praw", [128, 4 * 512], BF16))
        osb = En(nc.sbuf_tensor(pf + "osb", [128, 1024], F32))
        ps_s = En(nc.psum_tensor([128, 4 * 512], F32))
        ps_pv = En(nc.psum_tensor([128, 2 * 512], F32))
        ps_o = En(nc.psum_tensor([128, 2 * 512], F32))

        M = {}

        with nc.Block() as blk:
            @blk.gpsimd
            def _(g):
                SG.bind(g)
                SG.do("memset", onesf[:, :], 1.0)
                SG.do("memset", onesb[:, :], 1.0)
                SG.do("memset", epsbc[:, :], 1e-5)
                M["ms"] = SG.sem.val

            @blk.gpsimd
            def _(g):
                QG.bind(g)
                QG.dma(wq1b[:, :].rearrange("p (c e) -> p c e", c=8),
                       wq1.rearrange("(c p) e -> p c e", p=128), dm["dwl"])
                QG.dma(wq2b[:, :].rearrange("p (c e) -> p c e", c=8),
                       wq2.rearrange("(c p) e -> p c e", p=128), dm["dwl"])
                QG.dma(wkvb[:, 0:1024].rearrange("p (c e) -> p c e", c=8),
                       wk1.rearrange("(c p) e -> p c e", p=128), dm["dwl"])
                QG.dma(wkvb[:, 1024:2048].rearrange("p (c e) -> p c e", c=8),
                       wk2.rearrange("(c p) e -> p c e", p=128), dm["dwl"])
                QG.dma(wkvb[:, 2048:3072].rearrange("p (c e) -> p c e", c=8),
                       wv.rearrange("(c p) e -> p c e", p=128), dm["dwl"])
                QG.dma(woutb[:, :].rearrange("p (c e) -> p c e", c=8),
                       wout.rearrange("(c p) e -> p c e", p=128), dm["dwl"])
                M["wl"] = dm["dwl"].val
                QG.dma(sctok[:, :].rearrange("p (j f) -> p j f", f=64).rearrange("p j (h f) -> p j h f", h=2)[:, :, 0],
                       sintok.rearrange("(j p) f -> p j f", p=128), dm["dmisc"])
                QG.dma(sctok[:, :].rearrange("p (j f) -> p j f", f=64).rearrange("p j (h f) -> p j h f", h=2)[:, :, 1],
                       costok.rearrange("(j p) f -> p j f", p=128), dm["dmisc"])
                QG.dma(osb[0:1, 0:256], lqk, dm["dmisc"])
                M["misc"] = QG.dma(xc2[0:1, 0:1024], bout, dm["dmisc"])

            @blk.vector
            def _(v):
                SV.bind(v)
                SV.w(dm["dmisc"], M["misc"])
                SV.do("tensor_tensor", rrow[0:1, 0:64], osb[0:1, 0:64], osb[0:1, 64:128], ALU.mult)
                SV.do("tensor_tensor", rrow[0:1, 64:128], osb[0:1, 128:192], osb[0:1, 192:256], ALU.mult)
                SV.do("tensor_reduce", rrow[0:1, 128:129], rrow[0:1, 0:64], AX.X, ALU.add)
                SV.do("tensor_reduce", rrow[0:1, 129:130], rrow[0:1, 64:128], AX.X, ALU.add)
                M["lam1"] = SV.sem.val

            @blk.scalar
            def _(sc):
                SA.bind(sc)
                SA.w(SV.sem, M["lam1"])
                SA.do("activation", rrow[0:1, 130:131], rrow[0:1, 128:129], AF.Exp)
                SA.do("activation", rrow[0:1, 131:132], rrow[0:1, 129:130], AF.Exp)
                M["lam2"] = SA.sem.val

            @blk.vector
            def _(v):
                SV.bind(v)
                SV.w(SA.sem, M["lam2"])
                SV.do("tensor_tensor", rrow[0:1, 132:133], rrow[0:1, 130:131], rrow[0:1, 131:132], ALU.subtract)
                SV.do("tensor_scalar", rrow[0:1, 132:133], rrow[0:1, 132:133], 0.8, 0.9, ALU.add, ALU.min)
                SV.do("tensor_scalar", rrow[0:1, 132:133], rrow[0:1, 132:133], 0.1, -1.0, ALU.max, ALU.mult)
                M["lam3"] = SV.sem.val

            @blk.tensor
            def _(t):
                SP.bind(t)
                SP.w(SV.sem, M["lam3"])
                SP.w(SG.sem, M["ms"])
                SP.raw("matmul", ps_s[:, 0:512], onesf[:1, :], xc2[:1, 0:512], start=True, stop=True)
                SP.raw("matmul", ps_s[:, 512:1024], onesf[:1, :], xc2[:1, 512:1024], start=True, stop=True)
                SP.do("matmul", ps_o[:, 0:1], onesf[:1, :], rrow[:1, 132:133], start=True, stop=True)
                M["bmm"] = SP.sem.val

            @blk.vector
            def _(v):
                SV.bind(v)
                SV.w(SP.sem, M["bmm"])
                SV.do("tensor_copy", boutbc[:, :], ps_s[:, 0:1024])
                SV.do("tensor_copy", lambc[:, 0:1], ps_o[:, 0:1])
                M["bprep"] = SV.sem.val

        for b in range(BPC):
            B0 = b * L
            with nc.Block() as blk:
                for u in range(17):
                    sidx = u
                    ssl = u % 2
                    xs_sem = dm["dxs0"] if ssl == 0 else dm["dxs1"]
                    xl = u % 2

                    @blk.sync
                    def _(s, b=b, u=u, xl=xl, sidx=sidx):
                        QS.bind(s)
                        if u >= 2:
                            QS.w(SV.sem, M[("a1stt", b, u - 2)])
                        elif b > 0:
                            QS.w(SV.sem, M.get(("oevac", b - 1, NW - 1, 3), 0))
                        M[("a1x", b, u)] = QS.dma(xsb[:, xl * 1024:(xl + 1) * 1024],
                                                  xin[B0 + u * 128:B0 + (u + 1) * 128, :], dm["dx"])

                    @blk.vector
                    def _(v, b=b, u=u, xl=xl, sidx=sidx):
                        SV.bind(v)
                        x_ap = xsb[:, xl * 1024:(xl + 1) * 1024]
                        sl = sidx % 2
                        r = [red[:, sl * 8 + k:sl * 8 + k + 1] for k in range(5)]
                        SV.w(dm["dx"], M[("a1x", b, u)])
                        if b == 0 and u == 0:
                            SV.w(SP.sem, M["bmm"])
                        if sidx >= 1:
                            SV.w(SA.sem, M[("a1xln", b, sidx - 1)])
                        SV.do("tensor_reduce", r[0], x_ap, AX.X, ALU.add)
                        SV.do("tensor_scalar", r[1], r[0], -1.0 / E, None, ALU.mult)
                        SV.do("tensor_scalar", xc2[:, :], x_ap, r[1], None, ALU.add)
                        SV.do("scalar_tensor_tensor", sqs[:, :], xc2[:, :], 1.0, xc2[:, :],
                              ALU.mult, ALU.mult, accum_out=r[2])
                        M[("a1stt", b, sidx)] = SV.sem.val

                    @blk.scalar
                    def _(sc, b=b, sidx=sidx, ssl=ssl, xs_sem=xs_sem):
                        SA.bind(sc)
                        sl = sidx % 2
                        r = [red[:, sl * 8 + k:sl * 8 + k + 1] for k in range(5)]
                        SA.w(SV.sem, M[("a1stt", b, sidx)])
                        if sidx >= 2:
                            SA.w(xs_sem, M[("a1xst", b, sidx - 2)])
                        SA.do("activation", r[3], r[2], AF.Ln, bias=epsbc[:, 0:1], scale=1.0 / E)
                        SA.do("activation", r[4], r[3], AF.Exp, bias=0.0, scale=-0.5)
                        SA.do("activation", xlnsb[:, ssl * 1024:(ssl + 1) * 1024],
                              xc2[:, :], AF.Copy, bias=0.0, scale=r[4])
                        M[("a1xln", b, sidx)] = SA.sem.val

                    @blk.sync
                    def _(s, b=b, u=u, sidx=sidx, ssl=ssl, xs_sem=xs_sem):
                        QS.bind(s)
                        QS.w(SA.sem, M[("a1xln", b, sidx)])
                        M[("a1xst", b, sidx)] = QS.dma(
                            xlnd.ap()[u * 128:(u + 1) * 128, :],
                            xlnsb[:, ssl * 1024:(ssl + 1) * 1024], xs_sem)

            if phases < 2:
                continue
            with nc.Block() as blk:
                for j in range(17):
                    jsl = j % 2

                    @blk.scalar
                    def _(sc, b=b, j=j, jsl=jsl):
                        QA.bind(sc)
                        QA.w(dm["dxs0"], M[("a1xst", b, 16)])
                        QA.w(dm["dxs1"], M[("a1xst", b, 15)])
                        if j >= 2:
                            QA.w(SP.sem, M[("a2mm", b, j - 2)])
                        M[("a2xt", b, j)] = QA.dmat(
                            xts[:, jsl * 1024:(jsl + 1) * 1024].rearrange("p (c t) -> p c t", c=8),
                            xlnd.ap()[j * 128:(j + 1) * 128, :],
                            dm["dxt0"] if jsl == 0 else dm["dxt1"])

                    for iw in range(3):
                        @blk.tensor
                        def _(t, b=b, j=j, jsl=jsl, iw=iw):
                            SP.bind(t)
                            SP.w(dm["dxt0"] if jsl == 0 else dm["dxt1"], M[("a2xt", b, j)])
                            if b == 0 and j == 0 and iw == 0:
                                SP.w(dm["dwl"], M["wl"])
                                SP.w(SV.sem, M["bprep"])
                            if iw < 2:
                                wsb = wq1b if iw == 0 else wq2b
                                for dh in range(2):
                                    pr = M.get(("a2evac", b, j, 0, dh)) if iw == 1 else \
                                        M.get(("a2evac", b, j - 1, 1, dh))
                                    if pr:
                                        SP.w(SV.sem, pr)
                                    for c in range(8):
                                        mi = SP.raw("matmul", ps_s[:, dh * 512:(dh + 1) * 512],
                                                    xts[:, jsl * 1024 + c * 128:jsl * 1024 + (c + 1) * 128],
                                                    wsb[:, c * 1024 + dh * 512:c * 1024 + (dh + 1) * 512],
                                                    start=(c == 0), stop=(c == 7))
                                        if c == 7:
                                            mi.then_inc(SP.sem.inc(1), 1)
                                    M[("a2mmg", b, j, iw, dh)] = SP.sem.val
                            else:
                                for kk in range(3):
                                    pr = M.get(("a2evack", b, j - 1, kk))
                                    if pr:
                                        SP.w(SV.sem, pr)
                                    for c in range(8):
                                        mi = SP.raw("matmul", ps_pv[:, kk * 128:(kk + 1) * 128],
                                                    xts[:, jsl * 1024 + c * 128:jsl * 1024 + (c + 1) * 128],
                                                    wkvb[:, kk * 1024 + c * 128:kk * 1024 + (c + 1) * 128],
                                                    start=(kk == 0 and c == 0), stop=(kk == 2 and c == 7))
                                        if c == 7:
                                            mi.then_inc(SP.sem.inc(1), 1)
                                    M[("a2mmk", b, j, kk)] = SP.sem.val
                            if iw == 2:
                                M[("a2mm", b, j)] = SP.sem.val

                        if iw < 2:
                            @blk.vector
                            def _(v, b=b, j=j, iw=iw):
                                SV.bind(v)
                                for dh in range(2):
                                    SV.w(SP.sem, M[("a2mmg", b, j, iw, dh)])
                                    if (b, j) != (0, 0):
                                        SV.w(dm["dq0" if iw == 0 else "dq1"], M.get(("a2qst", b, j - 1, iw), 0))
                                    ps = ps_s[:, dh * 512:(dh + 1) * 512]
                                    qe = apv(ps, 0, [[64, 8], [2, 32]])
                                    qo = apv(ps, 1, [[64, 8], [2, 32]])
                                    sct = sctok[:, 0:1]
                                    sin8 = apv(sct, j * 64, [[0, 8], [1, 32]])
                                    cos8 = apv(sct, j * 64 + 32, [[0, 8], [1, 32]])
                                    out = qrb[:, iw * 1024 + dh * 512:iw * 1024 + (dh + 1) * 512]
                                    oute = apv(out, 0, [[64, 8], [1, 32]])
                                    outo = apv(out, 32, [[64, 8], [1, 32]])
                                    s1 = apv(rope[:, 0:1], 0, [[32, 8], [1, 32]])
                                    s2 = apv(rope[:, 0:1], 256, [[32, 8], [1, 32]])
                                    SV.do("tensor_tensor", s1, qe, cos8, ALU.mult)
                                    SV.do("tensor_tensor", s2, qo, sin8, ALU.mult)
                                    SV.do("tensor_tensor", oute, s1, s2, ALU.subtract)
                                    SV.do("tensor_tensor", s1, qe, sin8, ALU.mult)
                                    SV.do("tensor_tensor", s2, qo, cos8, ALU.mult)
                                    SV.do("tensor_tensor", outo, s1, s2, ALU.add)
                                    M[("a2evac", b, j, iw, dh)] = SV.sem.val
                                M[("a2rope", b, j, iw)] = SV.sem.val

                            @blk.sync
                            def _(s, b=b, j=j, iw=iw):
                                QS.bind(s)
                                QS.w(SV.sem, M[("a2rope", b, j, iw)])
                                qd = q1d if iw == 0 else q2d
                                M[("a2qst", b, j, iw)] = QS.dma(
                                    qd.ap()[j * 128:(j + 1) * 128, :],
                                    qrb[:, iw * 1024:(iw + 1) * 1024], dm["dq0" if iw == 0 else "dq1"])
                        else:
                            @blk.vector
                            def _(v, b=b, j=j):
                                SV.bind(v)
                                for kk in range(2):
                                    SV.w(SP.sem, M[("a2mmk", b, j, kk)])
                                    if (b, j) != (0, 0):
                                        SV.w(dm["dk"], M.get(("a2kst", b, j - 1), 0))
                                    ps = ps_pv[:, kk * 128:(kk + 1) * 128]
                                    qe = apv(ps, 0, [[64, 2], [2, 32]])
                                    qo = apv(ps, 1, [[64, 2], [2, 32]])
                                    sct = sctok[:, 0:1]
                                    sin2 = apv(sct, j * 64, [[0, 2], [1, 32]])
                                    cos2 = apv(sct, j * 64 + 32, [[0, 2], [1, 32]])
                                    out = kvb[:, kk * 128:(kk + 1) * 128]
                                    oute = apv(out, 0, [[64, 2], [1, 32]])
                                    outo = apv(out, 32, [[64, 2], [1, 32]])
                                    s1 = apv(rope[:, 0:1], 0, [[32, 2], [1, 32]])
                                    s2 = apv(rope[:, 0:1], 64, [[32, 2], [1, 32]])
                                    SV.do("tensor_tensor", s1, qe, cos2, ALU.mult)
                                    SV.do("tensor_tensor", s2, qo, sin2, ALU.mult)
                                    SV.do("tensor_tensor", oute, s1, s2, ALU.subtract)
                                    SV.do("tensor_tensor", s1, qe, sin2, ALU.mult)
                                    SV.do("tensor_tensor", s2, qo, cos2, ALU.mult)
                                    SV.do("tensor_tensor", outo, s1, s2, ALU.add)
                                SV.w(SP.sem, M[("a2mmk", b, j, 2)])
                                SV.do("tensor_copy", vsb[:, j * 128:(j + 1) * 128], ps_pv[:, 256:384])
                                M[("a2evack", b, j, 0)] = SV.sem.val
                                M[("a2evack", b, j, 1)] = SV.sem.val
                                M[("a2evack", b, j, 2)] = SV.sem.val
                                M[("a2kv", b, j)] = SV.sem.val

                            @blk.sync
                            def _(s, b=b, j=j):
                                QS.bind(s)
                                QS.w(SV.sem, M[("a2kv", b, j)])
                                QS.dma(k1d.ap()[j * 128:(j + 1) * 128, :], kvb[:, 0:128], dm["dk"])
                                M[("a2kst", b, j)] = QS.dma(k2d.ap()[j * 128:(j + 1) * 128, :],
                                                            kvb[:, 128:256], dm["dk"])

            if phases < 3:
                continue
            with nc.Block() as blk:
                @blk.scalar
                def _(sc, b=b):
                    QA.bind(sc)
                    QA.w(dm["dq0"], M[("a2qst", b, 16, 0)])
                    QA.w(dm["dq1"], M[("a2qst", b, 16, 1)])
                    QA.w(dm["dk"], M[("a2kst", b, 16)])
                    if b > 0:
                        QA.w(SP.sem, M.get(("wmm", b - 1, NW - 1, 3), 0))
                    QA.dmat(q1t[:, :].rearrange("p (c t) -> p c t", c=8), q1d.ap(), dm["dql"])
                    QA.dmat(q2t[:, :].rearrange("p (c t) -> p c t", c=8), q2d.ap(), dm["dql"])
                    QA.dmat(k1t[:, :].rearrange("p (c t) -> p c t", c=1), k1d.ap(), dm["dql"])
                    M[("a3", b)] = QA.dmat(k2t[:, :].rearrange("p (c t) -> p c t", c=1), k2d.ap(), dm["dql"])

            if phases < 4:
                continue
            with nc.Block() as blk:
                NU = NW * 16

                def pe_stile(u, b=b):
                    n, h = u // 16, u % 16
                    g = u % 2
                    if u == 0:
                        SP.w(dm["dql"], M[("a3", b)])
                        SP.w(SV.sem, M["bprep"])
                    if u >= 2:
                        SP.w(SA.sem, M[("exp", b, u - 2)])
                    hp = 64 * (h % 2)
                    hc = h // 2
                    for i in range(2):
                        kt = k1t if i == 0 else k2t
                        qt = q1t if i == 0 else q2t
                        for kc in range(2):
                            mi = SP.raw("matmul",
                                        ps_s[:, g * 1024 + i * 512 + kc * 256:
                                             g * 1024 + i * 512 + (kc + 1) * 256],
                                        kt[hp:hp + 64, n * 128 + kc * 128:n * 128 + (kc + 1) * 128],
                                        qt[hp:hp + 64, hc * TA + n * 128:hc * TA + n * 128 + 256],
                                        start=(kc == 0), stop=(kc == 1))
                            if i == 1 and kc == 1:
                                mi.then_inc(SP.sem.inc(1), 1)
                    M[("smm", b, u)] = SP.sem.val

                def pe_sm_pv(v, b=b):
                    gv = v % 2
                    pvg_ = v % 2
                    nv, hv = v // 16, v % 16
                    SP.w(SA.sem, M[("exp", b, v)])
                    if v >= 2:
                        SP.w(SA.sem, M[("rr", b, v - 2)])
                    for i in range(2):
                        for kc in range(2):
                            mi = SP.raw("matmul",
                                        ps_pv[64:65, pvg_ * 512 + i * 256:pvg_ * 512 + (i + 1) * 256],
                                        onesb[:, 0:1],
                                        ptl[:, gv * 1024 + i * 512 + kc * 256:
                                            gv * 1024 + i * 512 + (kc + 1) * 256],
                                        start=(i == 0 and kc == 0), stop=(i == 1 and kc == 1),
                                        tile_position=(0, 64))
                            if i == 1 and kc == 1:
                                mi.then_inc(SP.sem.inc(1), 1)
                    M[("summ", b, v)] = SP.sem.val
                    if v >= 2:
                        SP.w(SV.sem, M[("aev", b, v - 2)])
                    for i in range(2):
                        for kc in range(2):
                            mi = SP.raw("matmul",
                                        ps_pv[0:64, pvg_ * 512 + i * 256:pvg_ * 512 + (i + 1) * 256],
                                        vsb[:, (nv + kc) * 128 + 64 * (hv % 2):
                                            (nv + kc) * 128 + 64 * (hv % 2) + 64],
                                        ptl[:, gv * 1024 + i * 512 + kc * 256:
                                            gv * 1024 + i * 512 + (kc + 1) * 256],
                                        start=(i == 0 and kc == 0), stop=(i == 1 and kc == 1))
                            if i == 1 and kc == 1:
                                mi.then_inc(SP.sem.inc(1), 1)
                    M[("pvg", b, v)] = SP.sem.val

                def pe_wout(w, gidx, b=b):
                    ts, dh = gidx // 2, gidx % 2
                    if gidx == 0:
                        SP.w(SV.sem, M[("cb", b, w * 16 + 15)])
                    pr = M.get(("oevac", b, w, gidx - 2)) if gidx >= 2 else \
                        M.get(("oevac", b, w - 1, 2 + gidx))
                    if pr:
                        SP.w(SV.sem, pr)
                    ab = (w % 2) * 2048
                    for m in range(8):
                        mi = SP.raw("matmul", ps_o[:, (gidx % 2) * 512:(gidx % 2 + 1) * 512],
                                    atb[:, ab + m * 256 + ts * 128:ab + m * 256 + ts * 128 + 128],
                                    woutb[:, m * 1024 + dh * 512:m * 1024 + (dh + 1) * 512],
                                    start=(m == 0), stop=(m == 7))
                        if m == 7:
                            mi.then_inc(SP.sem.inc(1), 1)
                    M[("wmm", b, w, gidx)] = SP.sem.val

                def sa_e(u, b=b, fut=None):
                    g = u % 2
                    SA.w(SP.sem, fut if fut is not None else M[("smm", b, u)])
                    if u >= 2:
                        SA.w(SP.sem, M[("pvg", b, u - 2)])
                    SA.do("activation", ptl[:, g * 1024:(g + 1) * 1024],
                          ps_s[:, g * 1024:(g + 1) * 1024], AF.Exp, bias=0.0, scale=0.125)
                    M[("exp", b, u)] = SA.sem.val

                def sa_r(v, b=b, fut2=None):
                    gv = v % 2
                    SA.w(SP.sem, fut2 if fut2 is not None else M[("summ", b, v)])
                    if v >= 2:
                        SA.w(dm["dqb1"], M[("qb1", b, v - 2)])
                    SA.do("activation", xc2[64:65, 0:512],
                          ps_pv[64:65, (v % 2) * 512:(v % 2) * 512 + 512], AF.Ln)
                    SA.do("activation", rrbf[64:65, gv * 512:(gv + 1) * 512],
                          xc2[64:65, 0:512], AF.Exp, bias=0.0, scale=-1.0)
                    M[("rr", b, v)] = SA.sem.val

                def qs_qb1(v, b=b):
                    gv = v % 2
                    QS.w(SA.sem, M[("rr", b, v)])
                    if v >= 2:
                        QS.w(dm["dqb2"], M[("qb2", b, v - 2)])
                    M[("qb1", b, v)] = QS.dma(rrd.ap()[gv:gv + 1, :],
                                              rrbf[64:65, gv * 512:(gv + 1) * 512], dm["dqb1"])

                def qs_qb2(v, b=b):
                    gv = v % 2
                    QS.w(dm["dqb1"], M[("qb1", b, v)])
                    if v >= 2:
                        QS.w(SV.sem, M[("cb", b, v - 2)])
                    M[("qb2", b, v)] = QS.dma(
                        rbsb[0:64, gv * 512:(gv + 1) * 512],
                        bass.AP(rrd.ap().tensor, gv * 512, [[0, 64], [1, 512]]), dm["dqb2"])

                def qs_res(w, ts, b=b):
                    tsg = w * 2 + ts
                    rsem = dm["dres0"] if ts % 2 == 0 else dm["dres1"]
                    if tsg >= 2:
                        QS.w(SV.sem, M.get(("oevac", b, (tsg - 2) // 2, ((tsg - 2) % 2) * 2 + 1), 0))
                    M[("res", b, tsg)] = QS.dma(
                        xsb[:, (ts % 2) * 1024:(ts % 2 + 1) * 1024],
                        xin[B0 + w * 256 + ts * 128:B0 + w * 256 + (ts + 1) * 128, :], rsem)

                def qs_ost(w, ts, b=b):
                    QS.w(SV.sem, M[("oevac", b, w, ts * 2 + 1)])
                    M[("ost", b, w * 2 + ts)] = QS.dma(
                        xout[B0 + w * 256 + ts * 128:B0 + w * 256 + (ts + 1) * 128, :],
                        osb[:, :], dm["dost"])

                def dve_aev(v, b=b):
                    pvg_ = v % 2
                    SV.w(SP.sem, M[("pvg", b, v)])
                    if v >= 4:
                        SV.w(SV.sem, M[("cb", b, v - 4)])
                    SV.do("tensor_copy", praw[0:64, (v % 4) * 512:(v % 4 + 1) * 512],
                          ps_pv[0:64, pvg_ * 512:(pvg_ + 1) * 512])
                    M[("aev", b, v)] = SV.sem.val

                def dve_cb(v, b=b):
                    gv = v % 2
                    sl4 = v % 4
                    nv, hv = v // 16, v % 16
                    hp = 64 * (hv % 2)
                    hc = hv // 2
                    SV.w(dm["dqb2"], M[("qb2", b, v)])
                    if nv >= 2:
                        SV.w(SP.sem, M[("wmm", b, nv - 2, 3)])
                    ab = (nv % 2) * 2048
                    SV.do("tensor_tensor", scrT[0:64, 0:256],
                          praw[0:64, sl4 * 512:sl4 * 512 + 256],
                          rbsb[0:64, gv * 512:gv * 512 + 256], ALU.mult)
                    SV.do("tensor_tensor", scrT[0:64, 256:512],
                          praw[0:64, sl4 * 512 + 256:sl4 * 512 + 512],
                          rbsb[0:64, gv * 512 + 256:gv * 512 + 512], ALU.mult)
                    SV.do("scalar_tensor_tensor", atb[hp:hp + 64, ab + hc * 256:ab + (hc + 1) * 256],
                          scrT[0:64, 256:512], lambc[0:64, 0:1], scrT[0:64, 0:256],
                          ALU.mult, ALU.add)
                    M[("cb", b, v)] = SV.sem.val

                def dve_oevac(w, gidx, b=b):
                    ts, dh = gidx // 2, gidx % 2
                    tsg = w * 2 + ts
                    rsem = dm["dres0"] if ts % 2 == 0 else dm["dres1"]
                    SV.w(SP.sem, M[("wmm", b, w, gidx)])
                    SV.w(rsem, M[("res", b, tsg)])
                    if tsg >= 1:
                        SV.w(dm["dost"], M.get(("ost", b, tsg - 1), 0))
                    elif b > 0:
                        SV.w(dm["dost"], M.get(("ost", b - 1, 2 * NW - 1), 0))
                    SV.do("scalar_tensor_tensor", osb[:, dh * 512:(dh + 1) * 512],
                          ps_o[:, (gidx % 2) * 512:(gidx % 2 + 1) * 512], 1.0,
                          boutbc[:, dh * 512:(dh + 1) * 512], ALU.mult, ALU.add)
                    SV.do("tensor_tensor", osb[:, dh * 512:(dh + 1) * 512],
                          osb[:, dh * 512:(dh + 1) * 512],
                          xsb[:, (ts % 2) * 1024 + dh * 512:(ts % 2) * 1024 + (dh + 1) * 512],
                          ALU.add)
                    M[("oevac", b, w, gidx)] = SV.sem.val

                for u in range(NU):
                    n, h = u // 16, u % 16

                    @blk.tensor
                    def _(t, b=b, u=u, n=n, h=h):
                        SP.bind(t)
                        pe_stile(u)
                        if u >= 1:
                            pe_sm_pv(u - 1)
                        if n >= 1 and 8 <= h <= 11:
                            pe_wout(n - 1, h - 8)

                    if u % 2 == 0:
                        @blk.scalar
                        def _(sc, b=b, u=u):
                            SA.bind(sc)
                            spv = SP.sem.val
                            sa_e(u)
                            if u >= 1:
                                sa_r(u - 1)
                            if u + 1 < NU:
                                sa_e(u + 1, fut=spv + 1)
                                M[("exp", b, u + 1)] = SA.sem.val
                                sa_r(u, fut2=spv + 2)

                    @blk.sync
                    def _(s, b=b, u=u, n=n, h=h):
                        QS.bind(s)
                        if u >= 1:
                            qs_qb1(u - 1)
                        if u >= 2:
                            qs_qb2(u - 2)
                        if n >= 1 and h == 0:
                            qs_res(n - 1, 0)
                        if n >= 1 and h == 2:
                            qs_res(n - 1, 1)
                        if n >= 1 and h == 11:
                            qs_ost(n - 1, 0)
                        if n >= 1 and h == 13:
                            qs_ost(n - 1, 1)

                    @blk.vector
                    def _(v, b=b, u=u, n=n, h=h):
                        SV.bind(v)
                        if u >= 1:
                            dve_aev(u - 1)
                        if u >= 2:
                            dve_cb(u - 2)
                        if n >= 1 and 9 <= h <= 12:
                            dve_oevac(n - 1, h - 9)

                @blk.tensor
                def _(t, b=b):
                    SP.bind(t)
                    pe_sm_pv(NU - 1)

                @blk.scalar
                def _(sc, b=b):
                    SA.bind(sc)
                    sa_r(NU - 1)

                @blk.sync
                def _(s, b=b):
                    QS.bind(s)
                    qs_qb1(NU - 1)
                    qs_qb2(NU - 2)

                @blk.vector
                def _(v, b=b):
                    SV.bind(v)
                    dve_aev(NU - 1)
                    dve_cb(NU - 2)

                @blk.sync
                def _(s, b=b):
                    QS.bind(s)
                    qs_qb2(NU - 1)
                    qs_res(NW - 1, 0)
                    qs_res(NW - 1, 1)

                @blk.vector
                def _(v, b=b):
                    SV.bind(v)
                    dve_cb(NU - 1)

                for gidx in range(4):
                    @blk.tensor
                    def _(t, b=b, gidx=gidx):
                        SP.bind(t)
                        pe_wout(NW - 1, gidx)

                    @blk.vector
                    def _(v, b=b, gidx=gidx):
                        SV.bind(v)
                        dve_oevac(NW - 1, gidx)

                    if gidx == 1:
                        @blk.sync
                        def _(s, b=b):
                            QS.bind(s)
                            qs_ost(NW - 1, 0)

                @blk.sync
                def _(s, b=b):
                    QS.bind(s)
                    qs_ost(NW - 1, 1)

        with nc.Block() as blk:
            @blk.sync
            def _(s):
                for _ds in dm.values():
                    if _ds.val:
                        s.wait_ge(_ds.h, _ds.val)

    return nc

def _rope_tables():
    pos = np.arange(TA, dtype=np.float64)[:, None]
    div = np.exp(np.arange(0, 64, 2, dtype=np.float64) * (-np.log(10000.0) / 64))
    return (np.sin(pos * div).astype(np.float32), np.cos(pos * div).astype(np.float32))


def run_a(x_full, inputs):
    nc = build_stage_a()
    sintab, costab = _rope_tables()
    maps = []
    for c in range(N_CORES):
        maps.append(dict(
            xin=np.ascontiguousarray(x_full[c * TOK:(c + 1) * TOK]),
            wq1=inputs["L0_attn_Wq1"], wq2=inputs["L0_attn_Wq2"],
            wk1=inputs["L0_attn_Wk1"], wk2=inputs["L0_attn_Wk2"],
            wv=inputs["L0_attn_Wv"], wout=inputs["L0_attn_Wout"],
            bout=inputs["L0_attn_bout"].reshape(1, -1),
            lqk=np.concatenate([inputs["L0_attn_lq1"], inputs["L0_attn_lk1"],
                                inputs["L0_attn_lq2"], inputs["L0_attn_lk2"]]).reshape(1, -1),
            sintok=sintab, costok=costab,
        ))
    res = run_bass_kernel_spmd(nc, maps, core_ids=list(range(N_CORES)))
    return np.concatenate([res.results[c]["xout"] for c in range(N_CORES)], axis=0)





CEXP = 16384
NB = 512


def build_stage_d1(nc=None, pf="", ext=None, ctx=None, phases=4):
    ext = ext or {}
    if nc is None:
        nc = bass.Bass("TRN2", target_bir_lowering=False)

    def exio(name, shape, dt, kind):
        return ext[name] if name in ext else nc.dram_tensor(name, shape, dt, kind=kind).ap()
    xin = exio("xin", [TOK, E], F32, "ExternalInput")
    wg = exio("wg", [E, 8], F32, "ExternalInput")
    xln3 = exio("xln3", [TOK, E], BF16, "ExternalOutput")
    lgw = exio("lgw", [128, 512], F32, "ExternalOutput")
    xlnd = nc.dram_tensor(pf + "xlnd", [TOK, E], BF16)

    with ExitStack() as stk:
        En = stk.enter_context
        if ctx is None:
            SV = Eng(Sem(En(nc.semaphore(pf + "sv"))))
            SA = Eng(Sem(En(nc.semaphore(pf + "sa"))))
            SG = Eng(Sem(En(nc.semaphore(pf + "sg"))))
            SP = Eng(Sem(En(nc.semaphore(pf + "sp"))), serialize=False)
        else:
            SV, SA, SG, SP = ctx["SV"], ctx["SA"], ctx["SG"], ctx["SP"]
        QS = Q()
        QA = Q()
        QG = Q()
        dm = {k: Sem(En(nc.semaphore(pf + k))) for k in
              ["dx", "dxs0", "dxs1", "dxt0", "dxt1", "dwl", "dlg"]}

        epsbc = En(nc.sbuf_tensor(pf + "epsbc", [128, 1], F32))
        wgb = En(nc.sbuf_tensor(pf + "wgb", [128, 8 * 8], BF16))
        xlnsb = En(nc.sbuf_tensor(pf + "xlnsb", [128, 2 * 1024], BF16))
        xc2 = En(nc.sbuf_tensor(pf + "xc2", [128, 2 * 1024], F32))
        sqs = En(nc.sbuf_tensor(pf + "sqs", [128, 1024], BF16))
        red = En(nc.sbuf_tensor(pf + "red", [128, 16], F32))
        xsb = En(nc.sbuf_tensor(pf + "xsb", [128, 2048], F32))
        xts = En(nc.sbuf_tensor(pf + "xts", [128, 2 * 1024], BF16))
        lgsb = En(nc.sbuf_tensor(pf + "lgsb", [128, 512], F32))
        ps = En(nc.psum_tensor([128, 16], F32))

        M = {}
        with nc.Block() as blk:
            @blk.gpsimd
            def _(g):
                SG.bind(g)
                SG.do("memset", epsbc[:, :], 1e-5)
                M["ms"] = SG.sem.val

            @blk.gpsimd
            def _(g):
                QG.bind(g)
                M["wl"] = QG.dma(wgb[:, :].rearrange("p (c e) -> p c e", c=8),
                                 wg.rearrange("(c p) e -> p c e", p=128), dm["dwl"])

        with nc.Block() as blk:
            for u in range(TOK // 128):
                ssl = u % 2
                xs_sem = dm["dxs0"] if ssl == 0 else dm["dxs1"]

                @blk.sync
                def _(s, u=u, ssl=ssl):
                    QS.bind(s)
                    if u >= 2:
                        QS.w(SV.sem, M[("stt", u - 2)])
                    M[("x", u)] = QS.dma(xsb[:, ssl * 1024:(ssl + 1) * 1024],
                                         xin[u * 128:(u + 1) * 128, :], dm["dx"])

                @blk.vector
                def _(v, u=u, ssl=ssl):
                    SV.bind(v)
                    x_ap = xsb[:, ssl * 1024:(ssl + 1) * 1024]
                    r = [red[:, ssl * 8 + k:ssl * 8 + k + 1] for k in range(5)]
                    SV.w(dm["dx"], M[("x", u)])
                    if u >= 2:
                        SV.w(SA.sem, M[("xln", u - 2)])
                    xc = xc2[:, ssl * 1024:(ssl + 1) * 1024]
                    SV.do("tensor_reduce", r[0], x_ap, AX.X, ALU.add)
                    SV.do("tensor_scalar", r[1], r[0], -1.0 / E, None, ALU.mult)
                    SV.do("tensor_scalar", xc, x_ap, r[1], None, ALU.add)
                    SV.do("scalar_tensor_tensor", sqs[:, :], xc, 1.0, xc,
                          ALU.mult, ALU.mult, accum_out=r[2])
                    M[("stt", u)] = SV.sem.val

                @blk.scalar
                def _(sc, u=u, ssl=ssl, xs_sem=xs_sem):
                    SA.bind(sc)
                    r = [red[:, ssl * 8 + k:ssl * 8 + k + 1] for k in range(5)]
                    SA.w(SV.sem, M[("stt", u)])
                    SA.w(SG.sem, M["ms"])
                    if u >= 2:
                        SA.w(xs_sem, M[("xst", u - 2)])
                    SA.do("activation", r[3], r[2], AF.Ln, bias=epsbc[:, 0:1], scale=1.0 / E)
                    SA.do("activation", r[4], r[3], AF.Exp, bias=0.0, scale=-0.5)
                    SA.do("activation", xlnsb[:, ssl * 1024:(ssl + 1) * 1024],
                          xc2[:, ssl * 1024:(ssl + 1) * 1024], AF.Copy, bias=0.0, scale=r[4])
                    M[("xln", u)] = SA.sem.val

                @blk.sync
                def _(s, u=u, ssl=ssl, xs_sem=xs_sem):
                    QS.bind(s)
                    QS.w(SA.sem, M[("xln", u)])
                    QS.dma(xln3[u * 128:(u + 1) * 128, :], xlnsb[:, ssl * 1024:(ssl + 1) * 1024], xs_sem)
                    QS.dma(xlnd.ap()[u * 128:(u + 1) * 128, :],
                           xlnsb[:, ssl * 1024:(ssl + 1) * 1024], xs_sem)
                    M[("xst", u)] = xs_sem.val

                @blk.scalar
                def _(sc, u=u, ssl=ssl, xs_sem=xs_sem):
                    QA.bind(sc)
                    QA.w(xs_sem, M[("xst", u)])
                    if u >= 2:
                        QA.w(SP.sem, M.get(("lmm", u - 2), 0))
                    M[("xt", u)] = QA.dmat(
                        xts[:, ssl * 1024:(ssl + 1) * 1024].rearrange("p (c t) -> p c t", c=8),
                        xlnd.ap()[u * 128:(u + 1) * 128, :],
                        dm["dxt0"] if ssl == 0 else dm["dxt1"])

                @blk.tensor
                def _(t, u=u, ssl=ssl):
                    SP.bind(t)
                    SP.w(dm["dxt0"] if ssl == 0 else dm["dxt1"], M[("xt", u)])
                    if u == 0:
                        SP.w(dm["dwl"], M["wl"])
                    if u >= 2:
                        SP.w(SV.sem, M.get(("lev", u - 2), 0))
                    for c in range(8):
                        mi = SP.raw("matmul", ps[:, ssl * 8:ssl * 8 + 8],
                                    xts[:, ssl * 1024 + c * 128:ssl * 1024 + (c + 1) * 128],
                                    wgb[:, c * 8:(c + 1) * 8],
                                    start=(c == 0), stop=(c == 7))
                        if c == 7:
                            mi.then_inc(SP.sem.inc(1), 1)
                    M[("lmm", u)] = SP.sem.val

                @blk.vector
                def _(v, u=u, ssl=ssl):
                    SV.bind(v)
                    SV.w(SP.sem, M[("lmm", u)])
                    SV.do("tensor_copy", lgsb[:, u * 8:(u + 1) * 8], ps[:, ssl * 8:ssl * 8 + 8])
                    M[("lev", u)] = SV.sem.val

            @blk.sync
            def _(s):
                QS.bind(s)
                QS.w(SV.sem, M[("lev", TOK // 128 - 1)])
                QS.dma(lgw, lgsb[:, :], dm["dlg"])
                for _ds in dm.values():
                    if _ds.val:
                        s.wait_ge(_ds.h, _ds.val)

    return nc


def build_stage_d2a(nc=None, pf="", ext=None, ctx=None, phases=4):
    ext = ext or {}
    if nc is None:
        nc = bass.Bass("TRN2", target_bir_lowering=False)

    def exio(name, shape, dt, kind):
        return ext[name] if name in ext else nc.dram_tensor(name, shape, dt, kind=kind).ap()
    lg_in = exio("lg_in", [128, NB * 8], F32, "ExternalInput")
    tri = exio("tri", [128, 128], BF16, "ExternalInput")
    tid = exio("tid", [128, NB], I32, "ExternalInput")
    ioe = exio("ioe", [8], F32, "ExternalInput")
    expid = exio("expid", [1, 1], F32, "ExternalInput")
    w1o = exio("w1o", [128, NB], F32, "ExternalOutput")
    w2o = exio("w2o", [128, NB], F32, "ExternalOutput")
    f1o = exio("f1o", [128, NB], I32, "ExternalOutput")
    f2o = exio("f2o", [128, NB], I32, "ExternalOutput")
    stid = exio("stid", [CEXP], I32, "ExternalOutput")
    scrd = nc.dram_tensor(pf + "scrd", [8 * NB + 16], F32)

    TK = NB * 8
    with ExitStack() as stk:
        En = stk.enter_context
        if ctx is None:
            SV = Eng(Sem(En(nc.semaphore(pf + "sv"))))
            SA = Eng(Sem(En(nc.semaphore(pf + "sa"))))
            SG = Eng(Sem(En(nc.semaphore(pf + "sg"))))
            SP = Eng(Sem(En(nc.semaphore(pf + "sp"))), serialize=False)
        else:
            SV, SA, SG, SP = ctx["SV"], ctx["SA"], ctx["SG"], ctx["SP"]
        QS = Q()
        QG = Q()
        dm = {k: Sem(En(nc.semaphore(pf + k))) for k in ["dl", "dsc", "dso", "dout"]}

        lg = En(nc.sbuf_tensor(pf + "lg", [128, TK], F32))
        m1 = En(nc.sbuf_tensor(pf + "m1", [128, TK], BF16))
        m2 = En(nc.sbuf_tensor(pf + "m2", [128, TK], BF16))
        tmp = En(nc.sbuf_tensor(pf + "tmp", [128, TK], F32))
        cum1 = En(nc.sbuf_tensor(pf + "cum1", [128, TK], F32))
        cum2 = En(nc.sbuf_tensor(pf + "cum2", [128, TK], F32))
        trib = En(nc.sbuf_tensor(pf + "trib", [128, 128], BF16))
        onesb = En(nc.sbuf_tensor(pf + "onesb", [128, 1], BF16))
        tidb = En(nc.sbuf_tensor(pf + "tidb", [128, NB], I32))
        tok = En(nc.sbuf_tensor(pf + "tok", [128, 14 * NB], F32))
        rows = En(nc.sbuf_tensor(pf + "rows", [1, 16 * NB], F32))
        rows2 = En(nc.sbuf_tensor(pf + "rows2", [1, 16 * NB + 16], F32))
        ioeb = En(nc.sbuf_tensor(pf + "ioeb", [128, 8], F32))
        expb = En(nc.sbuf_tensor(pf + "expb", [128, 1], F32))
        ohb = En(nc.sbuf_tensor(pf + "ohb", [128, 8], F32))
        bct = En(nc.sbuf_tensor(pf + "bct", [128, 8 * NB + 16], F32))
        ints = En(nc.sbuf_tensor(pf + "ints", [128, 4 * NB], I32))
        stsb = En(nc.sbuf_tensor(pf + "stsb", [128, 128], I32))
        stf = En(nc.sbuf_tensor(pf + "stf", [128, 128], F32))
        ps1 = En(nc.psum_tensor([128, 1024], F32))
        ps2 = En(nc.psum_tensor([128, 1024], F32))

        def T(i):
            return tok[:, i * NB:(i + 1) * NB]
        MX, MX2, G1, G2, RDEN, GN1, GN2, IDX1, IDX2, LOC, KP, SCR, CE, ME = range(14)

        M = {}
        with nc.Block() as blk:
            @blk.sync
            def _(s):
                QS.bind(s)
                lgdst = lg[:, :].rearrange("p (r je) -> p r je", r=8) \
                    if len(lg_in.ap) == 3 else lg[:, :]
                QS.dma(lgdst, lg_in, dm["dl"])
                QS.dma(trib[:, :], tri, dm["dl"])
                QS.dma(tidb[:, :], tid, dm["dl"])
                QS.dma(ioeb[:, :], bass.AP(ioe.tensor, 0, [[0, 128], [1, 8]]), dm["dl"])
                M["ld"] = QS.dma(expb[:, :], bass.AP(expid.tensor, 0, [[0, 128], [1, 1]]), dm["dl"])

            @blk.gpsimd
            def _(g):
                SG.bind(g)
                SG.do("memset", onesb[:, :], 1.0)
                SG.do("memset", stf[:, :], float(ext.get("_sentinel", TOK * 8)))
                M["ms"] = SG.sem.val

            @blk.vector
            def _(v):
                SV.bind(v)
                SV.w(dm["dl"], M["ld"])
                SV.w(SG.sem, M["ms"])
                SV.do("tensor_reduce", T(MX), apv(lg[:, 0:1], 0, [[8, NB], [1, 8]]), AX.X, ALU.max)
                SV.do("tensor_tensor", m1[:, :], lg[:, :],
                      apv(tok[:, 0:1], MX * NB, [[1, NB], [0, 8]]), ALU.is_equal)
                SV.do("scalar_tensor_tensor", tmp[:, :], m1[:, :], -1e30, lg[:, :],
                      ALU.mult, ALU.add)
                SV.do("tensor_reduce", T(MX2), apv(tmp[:, 0:1], 0, [[8, NB], [1, 8]]), AX.X, ALU.max)
                SV.do("tensor_tensor", m2[:, :], tmp[:, :],
                      apv(tok[:, 0:1], MX2 * NB, [[1, NB], [0, 8]]), ALU.is_equal)
                SV.do("tensor_tensor", lg[:, :], lg[:, :],
                      apv(tok[:, 0:1], MX * NB, [[1, NB], [0, 8]]), ALU.subtract)
                M["pre"] = SV.sem.val

            @blk.scalar
            def _(sc):
                SA.bind(sc)
                SA.w(SV.sem, M["pre"])
                SA.do("activation", lg[:, :], lg[:, :], AF.Exp)
                M["exp"] = SA.sem.val

            @blk.vector
            def _(v):
                SV.bind(v)
                SV.w(SA.sem, M["exp"])
                SV.do("tensor_tensor", tmp[:, :], lg[:, :], m1[:, :], ALU.mult)
                SV.do("tensor_reduce", T(G1), apv(tmp[:, 0:1], 0, [[8, NB], [1, 8]]), AX.X, ALU.add)
                SV.do("tensor_tensor", tmp[:, :], lg[:, :], m2[:, :], ALU.mult)
                SV.do("tensor_reduce", T(G2), apv(tmp[:, 0:1], 0, [[8, NB], [1, 8]]), AX.X, ALU.add)
                SV.do("tensor_tensor", T(RDEN), T(G1), T(G2), ALU.add)
                SV.do("tensor_scalar", T(RDEN), T(RDEN), 1e-9, None, ALU.max)
                SV.do("reciprocal", T(RDEN), T(RDEN))
                SV.do("tensor_tensor", T(GN1), T(G1), T(RDEN), ALU.mult)
                SV.do("tensor_tensor", T(GN2), T(G2), T(RDEN), ALU.mult)
                SV.do("tensor_tensor", tmp[:, :], m1[:, :],
                      apv(ioeb[:, 0:1], 0, [[0, NB], [1, 8]]), ALU.mult)
                SV.do("tensor_reduce", T(IDX1), apv(tmp[:, 0:1], 0, [[8, NB], [1, 8]]), AX.X, ALU.add)
                SV.do("tensor_tensor", tmp[:, :], m2[:, :],
                      apv(ioeb[:, 0:1], 0, [[0, NB], [1, 8]]), ALU.mult)
                SV.do("tensor_reduce", T(IDX2), apv(tmp[:, 0:1], 0, [[8, NB], [1, 8]]), AX.X, ALU.add)
                SV.do("tensor_scalar", ohb[:, :], ioeb[:, :], expb[:, 0:1], None, ALU.is_equal)
                M["g"] = SV.sem.val

        with nc.Block() as blk:
            for un in range(16):
                ch, e = un // 8, un % 8
                msrc = m1 if ch == 0 else m2
                cdst = cum1 if ch == 0 else cum2

                @blk.tensor
                def _(t, un=un, e=e, msrc=msrc):
                    SP.bind(t)
                    if un == 0:
                        SP.w(SV.sem, M["g"])
                        SP.w(SG.sem, M["ms"])
                    pr = M.get(("scev", un - 2))
                    if pr:
                        SP.w(SV.sem, pr)
                    bank = (un % 2) * 512
                    SP.raw("matmul", ps1[:, bank:bank + 512], trib[:, :],
                           apv(msrc[:, 0:1], e, [[8, NB]]),
                           start=True, stop=True).then_inc(SP.sem.inc(1), 1)
                    SP.raw("matmul", ps2[0:1, bank:bank + 512], onesb[:, 0:1],
                           apv(msrc[:, 0:1], e, [[8, NB]]),
                           start=True, stop=True).then_inc(SP.sem.inc(1), 1)
                    M[("scmm", un)] = SP.sem.val

                @blk.vector
                def _(v, un=un, e=e, cdst=cdst):
                    SV.bind(v)
                    SV.w(SP.sem, M[("scmm", un)])
                    bank = (un % 2) * 512
                    SV.do("tensor_copy", apv(cdst[:, 0:1], e, [[8, NB]]), ps1[:, bank:bank + 512])
                    SV.do("tensor_copy", rows[0:1, un * NB:(un + 1) * NB], ps2[0:1, bank:bank + 512])
                    M[("scev", un)] = SV.sem.val

            @blk.vector
            def _(v):
                SV.bind(v)
                for un in range(16):
                    SV.do("tensor_tensor_scan", rows2[0:1, un * NB:(un + 1) * NB],
                          rows[0:1, un * NB:(un + 1) * NB],
                          rows[0:1, un * NB:(un + 1) * NB], 0.0, ALU.add, ALU.bypass)
                SV.do("tensor_copy", rows2[0:1, 16 * NB + 8:16 * NB + 16],
                      apv(rows2[0:1, 0:1], NB - 1, [[NB, 8]]))
                SV.do("tensor_tensor", rows2[0:1, 0:16 * NB],
                      rows2[0:1, 0:16 * NB], rows[0:1, 0:16 * NB], ALU.subtract)
                M["rowsv"] = SV.sem.val

            @blk.gpsimd
            def _(g):
                QG.bind(g)
                QG.w(SV.sem, M["rowsv"])
                QG.dma(scrd.ap().rearrange("(a b) -> a b", a=1)[0:1, 0:8 * NB],
                       rows2[0:1, 0:8 * NB], dm["dsc"])
                QG.w(dm["dsc"], dm["dsc"].val)
                M["bc1"] = QG.dma(bct[:, 0:8 * NB],
                                  bass.AP(scrd, 0, [[0, 128], [1, 8 * NB]]), dm["dsc"])

            @blk.vector
            def _(v):
                SV.bind(v)
                SV.w(dm["dsc"], M["bc1"])
                SV.do("tensor_tensor", cum1[:, :], cum1[:, :],
                      apv(bct[:, 0:1], 0, [[1, NB], [NB, 8]]), ALU.add)
                M["cum1v"] = SV.sem.val

            @blk.gpsimd
            def _(g):
                QG.bind(g)
                QG.w(SV.sem, M["cum1v"])
                QG.dma(scrd.ap().rearrange("(a b) -> a b", a=1)[0:1, 0:8 * NB + 16],
                       rows2[0:1, 8 * NB:16 * NB + 16], dm["dsc"])
                QG.w(dm["dsc"], dm["dsc"].val)
                M["bc2"] = QG.dma(bct[:, 0:8 * NB + 16],
                                  bass.AP(scrd, 0, [[0, 128], [1, 8 * NB + 16]]), dm["dsc"])

            @blk.vector
            def _(v):
                SV.bind(v)
                SV.w(dm["dsc"], M["bc2"])
                SV.do("tensor_tensor", cum2[:, :], cum2[:, :],
                      apv(bct[:, 0:1], 0, [[1, NB], [NB, 8]]), ALU.add)
                SV.do("tensor_tensor", cum2[:, :], cum2[:, :],
                      apv(bct[:, 0:1], 8 * NB + 8, [[0, NB], [1, 8]]), ALU.add)
                M["cum2v"] = SV.sem.val

        with nc.Block() as blk:
            @blk.vector
            def _(v):
                SV.bind(v)
                SV.w(SV.sem, M["cum2v"])
                ohv = apv(ohb[:, 0:1], 0, [[0, NB], [1, 8]])
                for ch in range(2):
                    mm = m1 if ch == 0 else m2
                    cc = cum1 if ch == 0 else cum2
                    GNx = GN1 if ch == 0 else GN2
                    IDXx = IDX1 if ch == 0 else IDX2
                    SV.do("tensor_tensor", tmp[:, :], mm[:, :], cc[:, :], ALU.mult)
                    SV.do("tensor_reduce", T(LOC), apv(tmp[:, 0:1], 0, [[8, NB], [1, 8]]),
                          AX.X, ALU.add)
                    SV.do("tensor_scalar", T(KP), T(LOC), float(CEXP), None, ALU.is_lt)
                    SV.do("tensor_tensor", T(GNx), T(GNx), T(KP), ALU.mult)
                    SV.do("tensor_scalar", T(SCR), T(LOC), float(CEXP - 1), None, ALU.min)
                    SV.do("tensor_scalar", T(CE), T(SCR), 4096.0, None, ALU.is_ge)
                    SV.do("scalar_tensor_tensor", T(CE), T(SCR), 8192.0, T(CE),
                          ALU.is_ge, ALU.add)
                    SV.do("scalar_tensor_tensor", T(CE), T(SCR), 12288.0, T(CE),
                          ALU.is_ge, ALU.add)
                    SV.do("scalar_tensor_tensor", T(SCR), T(CE), 28672.0, T(SCR),
                          ALU.mult, ALU.add)
                    SV.do("scalar_tensor_tensor", T(SCR), T(IDXx), 4096.0, T(SCR),
                          ALU.mult, ALU.add)
                    SV.do("tensor_copy", ints[:, (2 + ch) * NB:(3 + ch) * NB], T(SCR))
                    SV.do("tensor_tensor", tmp[:, :], tmp[:, :], ohv, ALU.mult)
                    SV.do("tensor_reduce", T(CE), apv(tmp[:, 0:1], 0, [[8, NB], [1, 8]]),
                          AX.X, ALU.add)
                    SV.do("tensor_tensor", tmp[:, :], mm[:, :], ohv, ALU.mult)
                    SV.do("tensor_reduce", T(ME), apv(tmp[:, 0:1], 0, [[8, NB], [1, 8]]),
                          AX.X, ALU.add)
                    SV.do("tensor_scalar", T(KP), T(CE), float(CEXP), None, ALU.is_lt)
                    SV.do("tensor_tensor", T(KP), T(ME), T(KP), ALU.mult)
                    SV.do("tensor_scalar", T(KP), T(KP), -1e9, 1e9, ALU.mult, ALU.add)
                    SV.do("tensor_tensor", T(SCR), T(CE), T(KP), ALU.add)
                    SV.do("tensor_copy", ints[:, ch * NB:(ch + 1) * NB], T(SCR))
                SV.do("tensor_copy", stsb[:, :], stf[:, :])
                M["tokv"] = SV.sem.val

            @blk.sync
            def _(s):
                QS.bind(s)
                QS.w(SV.sem, M["tokv"])
                QS.dma(w1o, T(GN1), dm["dout"])
                QS.dma(w2o, T(GN2), dm["dout"])
                QS.dma(f1o, ints[:, 2 * NB:3 * NB], dm["dout"])
                M["outs"] = QS.dma(f2o, ints[:, 3 * NB:4 * NB], dm["dout"])

            @blk.gpsimd
            def _(g):
                QG.bind(g)
                QG.w(SV.sem, M["tokv"])
                QG.dma(stid.rearrange("(p j) -> p j", p=128), stsb[:, :], dm["dso"])
                QG.w(dm["dso"], dm["dso"].val)
                with g.register("bnd") as rb:
                    g.reg_mov(rb, CEXP - 1)
                    for ch in range(2):
                        for b in range(NB):
                            g.indirect_dma_start(
                                stid.rearrange("(a b) -> a b", b=1),
                                IndirectOffsetOnAxis(ap=ints[:, ch * NB + b:ch * NB + b + 1], axis=0),
                                tidb[:, b:b + 1], None,
                                bounds_check=rb, oob_is_err=False,
                            ).then_inc(dm["dso"].inc(16), 16)
                for _ds in dm.values():
                    if _ds.val:
                        g.wait_ge(_ds.h, _ds.val)

    return nc


def build_stage_d2b(nc=None, pf="", ext=None, ctx=None, phases=4):
    ext = ext or {}
    if nc is None:
        nc = bass.Bass("TRN2", target_bir_lowering=False)

    def exio(name, shape, dt, kind):
        return ext[name] if name in ext else nc.dram_tensor(name, shape, dt, kind=kind).ap()
    xall = exio("xall", [TOK * 8 + 1, E], BF16, "ExternalInput")
    stid = exio("stid", [CEXP], I32, "ExternalInput")
    wgate = exio("wgate", [E, 4096], F32, "ExternalInput")
    wdown = exio("wdown", [4096, E], F32, "ExternalInput")
    eo = exio("eo", [CEXP, E], BF16, "ExternalOutput")
    dispd = nc.dram_tensor(pf + "dispd", [CEXP, E], BF16)

    with ExitStack() as stk:
        En = stk.enter_context
        if ctx is None:
            SV = Eng(Sem(En(nc.semaphore(pf + "sv"))))
            SA = Eng(Sem(En(nc.semaphore(pf + "sa"))))
            SP = Eng(Sem(En(nc.semaphore(pf + "sp"))), serialize=False)
        else:
            SV, SA, SP = ctx["SV"], ctx["SA"], ctx["SP"]
        QS = Q()
        QA = Q()
        QG = Q()
        dm = {k: Sem(En(nc.semaphore(pf + k))) for k in
              ["dst", "dg0", "dg1", "dds0", "dds1", "dwl", "dxt0", "dxt1", "dost"]}

        wgb = En(nc.sbuf_tensor(pf + "wgb", [128, 8 * 4096], BF16))
        wdb = En(nc.sbuf_tensor(pf + "wdb", [128, 32 * 1024], BF16))
        stsb = En(nc.sbuf_tensor(pf + "stsb", [128, 128], I32))
        gb = En(nc.sbuf_tensor(pf + "gb", [128, 2 * 1024], BF16))
        xts = En(nc.sbuf_tensor(pf + "xts", [128, 2 * 4096], BF16))
        ht = En(nc.sbuf_tensor(pf + "ht", [128, 32 * 512], BF16))
        osb = En(nc.sbuf_tensor(pf + "osb", [128, 1024], BF16))
        ph = En(nc.psum_tensor([128, 2048], F32))
        po = En(nc.psum_tensor([128, 1024], F32))

        M = {}
        with nc.Block() as blk:
            @blk.gpsimd
            def _(g):
                QG.bind(g)
                QG.dma(wgb[:, :].rearrange("p (c f) -> p c f", c=8),
                       wgate.rearrange("(c p) f -> p c f", p=128), dm["dwl"])
                M["wl"] = QG.dma(wdb[:, :].rearrange("p (c f) -> p c f", c=32),
                                 wdown.rearrange("(c p) f -> p c f", p=128), dm["dwl"])

            @blk.sync
            def _(s):
                QS.bind(s)
                M["st"] = QS.dma(stsb[:, :], stid.rearrange("(p j) -> p j", p=128), dm["dst"])

            @blk.gpsimd
            def _(g):
                QG.bind(g)
                QG.w(dm["dst"], M["st"])
                if "_ccs_x" in ext:
                    g.wait_ge(ext["_ccs_x"][0], ext["_ccs_x"][1])
                for j in range(128):
                    sl = j % 2
                    gsem = dm["dg0"] if sl == 0 else dm["dg1"]
                    ssem = dm["dds0"] if sl == 0 else dm["dds1"]
                    if j >= 2:
                        QG.w(ssem, M.get(("ds", j - 2), 0))
                    g.indirect_dma_start(
                        gb[:, sl * 1024:(sl + 1) * 1024], None,
                        xall, IndirectOffsetOnAxis(ap=stsb[:, j:j + 1], axis=0),
                    ).then_inc(gsem.inc(16), 16)
                    M[("g", j)] = gsem.val
                    QG.w(gsem, M[("g", j)])
                    g.dma_start(
                        bass.AP(dispd, j * 1024, [[128 * 1024, 128], [1, 1024]]),
                        gb[:, sl * 1024:(sl + 1) * 1024]).then_inc(ssem.inc(16), 16)
                    M[("ds", j)] = ssem.val

        NTD = CEXP // 512
        with nc.Block() as blk:
            for j in range(NTD):
                jsl = j % 2

                @blk.scalar
                def _(sc, j=j, jsl=jsl):
                    QA.bind(sc)
                    QA.w(dm["dds0"], M[("ds", 4 * j + 2)])
                    QA.w(dm["dds1"], M[("ds", 4 * j + 3)])
                    if j >= 2:
                        QA.w(SP.sem, M[("hmm", j - 2, 31)])
                    M[("xt", j)] = QA.dmat(
                        xts[:, jsl * 4096:(jsl + 1) * 4096].rearrange("p (c t) -> p c t", c=8),
                        dispd.ap()[j * 512:(j + 1) * 512, :],
                        dm["dxt0"] if jsl == 0 else dm["dxt1"])

                ga0 = SA.sem.val

                @blk.tensor
                def _(t, j=j, jsl=jsl, ga0=ga0):
                    SP.bind(t)
                    SP.w(dm["dxt0"] if jsl == 0 else dm["dxt1"], M[("xt", j)])
                    if j == 0:
                        SP.w(dm["dwl"], M["wl"])
                    for m in range(32):
                        if m < 4:
                            SP.w(SA.sem, ga0)
                        else:
                            SP.w(SA.sem, ga0 + (m - 4) // 2 + 1)
                        bank = m % 4
                        for c in range(8):
                            mi = SP.raw("matmul", ph[:, bank * 512:(bank + 1) * 512],
                                        wgb[:, c * 4096 + m * 128:c * 4096 + (m + 1) * 128],
                                        xts[:, jsl * 4096 + c * 512:jsl * 4096 + (c + 1) * 512],
                                        start=(c == 0), stop=(c == 7))
                            if c == 7:
                                mi.then_inc(SP.sem.inc(1), 1)
                        M[("hmm", j, m)] = SP.sem.val

                @blk.scalar
                def _(sc, j=j, ga0=ga0):
                    SA.bind(sc)
                    for P in range(16):
                        SA.w(SP.sem, M[("hmm", j, 2 * P + 1)])
                        sl2 = (2 * P) % 4
                        SA.do("activation", ht[:, 2 * P * 512:(2 * P + 2) * 512],
                              ph[:, sl2 * 512:(sl2 + 2) * 512], AF.Gelu)
                        M[("gelu", j, 2 * P)] = SA.sem.val
                        M[("gelu", j, 2 * P + 1)] = SA.sem.val
                    assert SA.sem.val == ga0 + 16

                for idx in range(8):
                    ts, dh = idx // 2, idx % 2

                    @blk.tensor
                    def _(t, j=j, idx=idx, ts=ts, dh=dh):
                        SP.bind(t)
                        if idx == 0:
                            SP.w(SA.sem, M[("gelu", j, 31)])
                        og = j * 8 + idx - 2
                        if og >= 0:
                            SP.w(SV.sem, M.get(("oev", og), 0))
                        bank = idx % 2
                        for m in range(32):
                            mi = SP.raw("matmul", po[:, bank * 512:(bank + 1) * 512],
                                        ht[:, m * 512 + ts * 128:m * 512 + (ts + 1) * 128],
                                        wdb[:, m * 1024 + dh * 512:m * 1024 + (dh + 1) * 512],
                                        start=(m == 0), stop=(m == 31))
                            if m == 31:
                                mi.then_inc(SP.sem.inc(1), 1)
                        M[("omm", j, idx)] = SP.sem.val

                    @blk.vector
                    def _(v, j=j, idx=idx, ts=ts, dh=dh):
                        SV.bind(v)
                        SV.w(SP.sem, M[("omm", j, idx)])
                        tsg = j * 4 + ts
                        if tsg >= 1:
                            SV.w(dm["dost"], M.get(("ost", tsg - 1), 0))
                        bank = idx % 2
                        SV.do("tensor_copy", osb[:, dh * 512:(dh + 1) * 512],
                              po[:, bank * 512:(bank + 1) * 512])
                        M[("oev", j * 8 + idx)] = SV.sem.val

                    if dh == 1:
                        @blk.sync
                        def _(s, j=j, ts=ts):
                            QS.bind(s)
                            QS.w(SV.sem, M[("oev", j * 8 + ts * 2 + 1)])
                            M[("ost", j * 4 + ts)] = QS.dma(
                                eo[j * 512 + ts * 128:j * 512 + (ts + 1) * 128, :],
                                osb[:, :], dm["dost"])

            if "eoall_cc" in ext:
                eoall_ap, ccs_h = ext["eoall_cc"]

                @blk.gpsimd
                def _(g):
                    QG.bind(g)
                    for k in range(4):
                        QG.w(dm["dost"], M[("ost", (8 * k + 7) * 4 + 3)])
                        g.collective_compute(
                            "AllGather", ALU.bypass, replica_groups=[list(range(8))],
                            ins=[eo[k * 4096:(k + 1) * 4096, :]],
                            outs=[bass.AP(eoall_ap.tensor, k * 32768 * E, [[E, 32768], [1, E]])],
                        ).then_inc(ccs_h, 1)

            @blk.sync
            def _(s):
                for _ds in dm.values():
                    if _ds.val:
                        s.wait_ge(_ds.h, _ds.val)

    return nc


def build_stage_d3(nc=None, pf="", ext=None, ctx=None, phases=4):
    ext = ext or {}
    if nc is None:
        nc = bass.Bass("TRN2", target_bir_lowering=False)

    def exio(name, shape, dt, kind):
        return ext[name] if name in ext else nc.dram_tensor(name, shape, dt, kind=kind).ap()
    NBL = TOK // 128
    xin = exio("xin", [TOK, E], F32, "ExternalInput")
    eoall = exio("eoall", [8 * CEXP, E], BF16, "ExternalInput")
    ext = ext or {}
    if "selidx" not in ext:
        f1s = exio("f1s", [128, NBL], I32, "ExternalInput")
        f2s = exio("f2s", [128, NBL], I32, "ExternalInput")
        w1s = exio("w1s", [128, NBL], F32, "ExternalInput")
        w2s = exio("w2s", [128, NBL], F32, "ExternalInput")
    else:
        f1s = f2s = w1s = w2s = None
    xout = exio("xout", [TOK, E], F32, "ExternalOutput")

    with ExitStack() as stk:
        En = stk.enter_context
        if ctx is None:
            SV = Eng(Sem(En(nc.semaphore(pf + "sv"))))
        else:
            SV = ctx["SV"]
        QS = Q()
        QG = Q()
        dm = {k: Sem(En(nc.semaphore(pf + k))) for k in
              ["dl", "dx0", "dx1", "dg10", "dg11", "dg20", "dg21", "dost"]}

        f1b = En(nc.sbuf_tensor(pf + "f1b", [128, NBL], I32))
        f2b = En(nc.sbuf_tensor(pf + "f2b", [128, NBL], I32))
        w1b = En(nc.sbuf_tensor(pf + "w1b", [128, NBL], F32))
        w2b = En(nc.sbuf_tensor(pf + "w2b", [128, NBL], F32))
        xsb = En(nc.sbuf_tensor(pf + "xsb", [128, 2 * 1024], F32))
        e1b = En(nc.sbuf_tensor(pf + "e1b", [128, 2 * 1024], BF16))
        e2b = En(nc.sbuf_tensor(pf + "e2b", [128, 2 * 1024], BF16))
        acc = En(nc.sbuf_tensor(pf + "acc", [128, 1024], F32))
        osb = En(nc.sbuf_tensor(pf + "osb", [128, 2 * 1024], F32))

        M = {}
        with nc.Block() as blk:
            if "selidx" in ext:
                selb = En(nc.sbuf_tensor(pf + "selb", [128, 1], I32))

                @blk.sync
                def _(s):
                    QS.bind(s)
                    M["sel"] = QS.dma(selb[:, :], ext["selidx"], dm["dl"])

                @blk.gpsimd
                def _(g):
                    QG.bind(g)
                    QG.w(dm["dl"], M["sel"])
                    g.indirect_dma_start(f1b[:, :], None, ext["f1v"],
                                         IndirectOffsetOnAxis(ap=selb[:, :], axis=0)
                                         ).then_inc(dm["dl"].inc(16), 16)
                    g.indirect_dma_start(f2b[:, :], None, ext["f2v"],
                                         IndirectOffsetOnAxis(ap=selb[:, :], axis=0)
                                         ).then_inc(dm["dl"].inc(16), 16)
                    g.indirect_dma_start(w1b[:, :], None, ext["w1v"],
                                         IndirectOffsetOnAxis(ap=selb[:, :], axis=0)
                                         ).then_inc(dm["dl"].inc(16), 16)
                    g.indirect_dma_start(w2b[:, :], None, ext["w2v"],
                                         IndirectOffsetOnAxis(ap=selb[:, :], axis=0)
                                         ).then_inc(dm["dl"].inc(16), 16)
                    M["ld"] = dm["dl"].val
            else:
                @blk.sync
                def _(s):
                    QS.bind(s)
                    QS.dma(f1b[:, :], f1s, dm["dl"])
                    QS.dma(f2b[:, :], f2s, dm["dl"])
                    QS.dma(w1b[:, :], w1s, dm["dl"])
                    M["ld"] = QS.dma(w2b[:, :], w2s, dm["dl"])

        with nc.Block() as blk:
            for j in range(NBL):
                sl = j % 2

                @blk.sync
                def _(s, j=j, sl=sl):
                    QS.bind(s)
                    if j >= 2:
                        QS.w(SV.sem, M[("cmb", j - 2)])
                    M[("x", j)] = QS.dma(xsb[:, sl * 1024:(sl + 1) * 1024],
                                         xin[j * 128:(j + 1) * 128, :],
                                         dm["dx0"] if sl == 0 else dm["dx1"])

                @blk.gpsimd
                def _(g, j=j, sl=sl):
                    QG.bind(g)
                    if j == 0:
                        QG.w(dm["dl"], M["ld"])
                        if "_ccs_eo" in ext:
                            g.wait_ge(ext["_ccs_eo"][0], ext["_ccs_eo"][1])
                    if j >= 2:
                        QG.w(SV.sem, M[("cmb", j - 2)])
                    g.indirect_dma_start(
                        e1b[:, sl * 1024:(sl + 1) * 1024], None,
                        eoall, IndirectOffsetOnAxis(ap=f1b[:, j:j + 1], axis=0),
                    ).then_inc((dm["dg10"] if sl == 0 else dm["dg11"]).inc(16), 16)
                    M[("g1", j)] = (dm["dg10"] if sl == 0 else dm["dg11"]).val
                    g.indirect_dma_start(
                        e2b[:, sl * 1024:(sl + 1) * 1024], None,
                        eoall, IndirectOffsetOnAxis(ap=f2b[:, j:j + 1], axis=0),
                    ).then_inc((dm["dg20"] if sl == 0 else dm["dg21"]).inc(16), 16)
                    M[("g2", j)] = (dm["dg20"] if sl == 0 else dm["dg21"]).val

                @blk.vector
                def _(v, j=j, sl=sl):
                    SV.bind(v)
                    SV.w(dm["dg10"] if sl == 0 else dm["dg11"], M[("g1", j)])
                    SV.w(dm["dg20"] if sl == 0 else dm["dg21"], M[("g2", j)])
                    SV.w(dm["dx0"] if sl == 0 else dm["dx1"], M[("x", j)])
                    if j >= 2:
                        SV.w(dm["dost"], M.get(("ost", j - 2), 0))
                    SV.do("tensor_scalar", acc[:, :], e1b[:, sl * 1024:(sl + 1) * 1024],
                          w1b[:, j:j + 1], None, ALU.mult)
                    SV.do("scalar_tensor_tensor", acc[:, :], e2b[:, sl * 1024:(sl + 1) * 1024],
                          w2b[:, j:j + 1], acc[:, :], ALU.mult, ALU.add)
                    SV.do("tensor_tensor", osb[:, sl * 1024:(sl + 1) * 1024],
                          acc[:, :], xsb[:, sl * 1024:(sl + 1) * 1024], ALU.add)
                    M[("cmb", j)] = SV.sem.val

                @blk.sync
                def _(s, j=j, sl=sl):
                    QS.bind(s)
                    QS.w(SV.sem, M[("cmb", j)])
                    M[("ost", j)] = QS.dma(xout[j * 128:(j + 1) * 128, :],
                                           osb[:, sl * 1024:(sl + 1) * 1024], dm["dost"])

            @blk.sync
            def _(s):
                for _ds in dm.values():
                    if _ds.val:
                        s.wait_ge(_ds.h, _ds.val)

    return nc


def _d_consts():
    tri = np.tril(np.ones((128, 128), dtype=np.float32), -1).T.astype(BF16NP)
    tid = (np.arange(NB)[None, :] * 128 + np.arange(128)[:, None]).astype(np.int32)
    ioe = np.arange(8, dtype=np.float32)
    return tri, tid, ioe


def run_d(x3_full, inputs):
    tri, tid, ioe = _d_consts()
    nc1 = build_stage_d1()
    maps = [dict(xin=np.ascontiguousarray(x3_full[c * TOK:(c + 1) * TOK]),
                 wg=inputs["L1_ffn_wg"]) for c in range(N_CORES)]
    r1 = run_bass_kernel_spmd(nc1, maps, core_ids=list(range(N_CORES)))
    xln_all = np.concatenate([r1.results[c]["xln3"] for c in range(N_CORES)], axis=0)
    xln_all = np.concatenate([xln_all, np.zeros((1, E), dtype=xln_all.dtype)], axis=0)
    lg_full = np.concatenate([r1.results[c]["lgw"] for c in range(N_CORES)], axis=1)
    nc2a = build_stage_d2a()
    maps = [dict(lg_in=lg_full, tri=tri, tid=tid, ioe=ioe,
                 expid=np.full((1, 1), float(c), np.float32)) for c in range(N_CORES)]
    r2a = run_bass_kernel_spmd(nc2a, maps, core_ids=list(range(N_CORES)))
    nc2b = build_stage_d2b()
    maps = [dict(xall=xln_all, stid=r2a.results[c]["stid"],
                 wgate=np.ascontiguousarray(inputs["L1_ffn_w_gate"][c]),
                 wdown=np.ascontiguousarray(inputs["L1_ffn_w_down"][c]))
            for c in range(N_CORES)]
    r2b = run_bass_kernel_spmd(nc2b, maps, core_ids=list(range(N_CORES)))
    eo_all = np.concatenate([r2b.results[c]["eo"] for c in range(N_CORES)], axis=0)
    f1 = r2a.results[0]["f1o"]
    f2 = r2a.results[0]["f2o"]
    w1 = r2a.results[0]["w1o"]
    w2 = r2a.results[0]["w2o"]
    nc3 = build_stage_d3()
    NBL = TOK // 128
    maps = []
    for c in range(N_CORES):
        sl = slice(c * NBL, (c + 1) * NBL)
        maps.append(dict(
            xin=np.ascontiguousarray(x3_full[c * TOK:(c + 1) * TOK]),
            eoall=eo_all,
            f1s=np.ascontiguousarray(f1[:, sl]), f2s=np.ascontiguousarray(f2[:, sl]),
            w1s=np.ascontiguousarray(w1[:, sl]), w2s=np.ascontiguousarray(w2[:, sl]),
        ))
    r3 = run_bass_kernel_spmd(nc3, maps, core_ids=list(range(N_CORES)))
    return np.concatenate([r3.results[c]["xout"] for c in range(N_CORES)], axis=0)



def emit_barrier(nc, bsem, idx, ctx):
    with nc.Block() as blk:
        def _s(e):
            for k in ("SV", "SA", "SG", "SP"):
                v = ctx[k].sem.val
                if v:
                    e.wait_ge(ctx[k].sem.h, v)
            e.nop().then_inc(bsem, 1)
            e.wait_ge(bsem, 5 * idx)
        blk.sync(_s)
        for engname in ["tensor", "vector", "scalar", "gpsimd"]:
            def _f(e):
                e.nop().then_inc(bsem, 1)
                e.wait_ge(bsem, 5 * idx)
            getattr(blk, engname)(_f)


def build_fused(dbg=False):
    nc = bass.Bass("TRN2", target_bir_lowering=False)
    DI = lambda n, sh, dt: nc.dram_tensor(n, sh, dt, kind="ExternalInput").ap()
    x = DI("x", [TOK, E], F32)
    out = nc.dram_tensor("out", [TOK, E], F32, kind="ExternalOutput").ap()
    wA = {n: DI(n, shp, F32) for n, shp in [
        ("wq1", [E, E]), ("wq2", [E, E]), ("wk1", [E, 128]), ("wk2", [E, 128]),
        ("wv", [E, 128]), ("wout", [E, E]), ("bout", [1, E]), ("lqk", [1, 256]),
        ("sintok", [TA, 32]), ("costok", [TA, 32])]}
    wB = {n: DI(n, shp, F32) for n, shp in [
        ("w1", [4096, 1024]), ("b1", [1, 4096]), ("w2", [1024, 4096]), ("b2", [1, 1024])]}
    wC = {n: DI("c" + n, shp, F32) for n, shp in [
        ("wq", [E, E]), ("wk", [E, E]), ("wv", [E, E]), ("wo", [E, E]),
        ("bq", [1, E]), ("bk", [1, E]), ("bv", [1, E]), ("bo", [1, E])]}
    wg = DI("wg", [E, 8], F32)
    tri = DI("tri", [128, 128], BF16)
    tid = DI("tid", [128, NB], I32)
    ioe = DI("ioe", [8], F32)
    expid = DI("expid", [1, 1], F32)
    wgate = DI("wgate", [E, 4096], F32)
    wdown = DI("wdown", [4096, E], F32)
    selidx = DI("selidx", [128, 1], I32)

    ko = "ExternalOutput" if dbg else "Internal"
    x1 = nc.dram_tensor("x1i", [TOK, E], F32, kind=ko)
    x2 = nc.dram_tensor("x2i", [TOK, E], F32, kind=ko)
    x3 = nc.dram_tensor("x3i", [TOK, E], F32, kind=ko)
    lgl = nc.dram_tensor("lgl", [128, 512], F32)
    lgall = nc.dram_tensor("lgall", [1024, 512], F32)
    xlnl = nc.dram_tensor("xlnl", [TOK + 1, E], BF16)
    xall = nc.dram_tensor("xallg", [(TOK + 1) * 8, E], BF16)
    f1i = nc.dram_tensor("f1i", [128, NB], I32)
    f2i = nc.dram_tensor("f2i", [128, NB], I32)
    w1i = nc.dram_tensor("w1i", [128, NB], F32)
    w2i = nc.dram_tensor("w2i", [128, NB], F32)
    stidi = nc.dram_tensor("stidi", [CEXP], I32)
    eol = nc.dram_tensor("eol", [CEXP, E], BF16)
    eoall = nc.dram_tensor("eoallg", [8 * CEXP, E], BF16)

    with ExitStack() as stk:
        En = stk.enter_context
        ctx = {
            "SV": Eng(Sem(En(nc.semaphore("zsv")))),
            "SA": Eng(Sem(En(nc.semaphore("zsa")))),
            "SG": Eng(Sem(En(nc.semaphore("zsg")))),
            "SP": Eng(Sem(En(nc.semaphore("zsp"))), serialize=False),
        }
        bsem = En(nc.semaphore("zbar"))
        ccs = En(nc.semaphore("zcc"))
        dgl = Sem(En(nc.semaphore("zgl")))
        nb = [0]

        def bar():
            nb[0] += 1
            emit_barrier(nc, bsem, nb[0], ctx)

        build_stage_a(nc, "a_", dict(xin=x, xout=x1.ap(), **wA), ctx)
        bar()
        build_stage_b(nc, "b_", dict(xin=x1.ap(), xout=x2.ap(), **wB), ctx)
        bar()
        build_stage_c(nc, "c_", dict(xin=x2.ap(), xout=x3.ap(), **wC), ctx)
        bar()
        build_stage_d1(nc, "e_", dict(xin=x3.ap(), xln3=xlnl.ap()[0:TOK, :],
                                      lgw=lgl.ap(), wg=wg), ctx)
        bar()
        with nc.sbuf_tensor("zrow", [1, 1024], BF16) as zrow, nc.Block() as blk:
            @blk.gpsimd
            def _(g):
                g.memset(zrow[0:1, :], 0.0)
                g.nop().then_inc(dgl.inc(16), 16)
                g.wait_ge(dgl.h, dgl.val)
                g.dma_start(xlnl.ap()[TOK:TOK + 1, :], zrow[0:1, :]).then_inc(dgl.inc(16), 16)
                g.wait_ge(dgl.h, dgl.val)
                g.collective_compute(
                    "AllGather", ALU.bypass, replica_groups=[list(range(8))],
                    ins=[lgl.ap()], outs=[lgall.ap()]).then_inc(ccs, 1)
                g.collective_compute(
                    "AllGather", ALU.bypass, replica_groups=[list(range(8))],
                    ins=[xlnl.ap()], outs=[xall.ap()]).then_inc(ccs, 1)
                g.wait_ge(ccs, 1)
        bar()
        build_stage_d2a(nc, "f_", dict(
            lg_in=lgall.ap().rearrange("(r p) je -> p r je", p=128),
            tri=tri, tid=tid, ioe=ioe, expid=expid,
            w1o=w1i.ap(), w2o=w2i.ap(), f1o=f1i.ap(), f2o=f2i.ap(), stid=stidi.ap(),
            _sentinel=float(TOK)), ctx)
        bar()
        build_stage_d2b(nc, "g_", dict(xall=xall.ap(), stid=stidi.ap(),
                                       wgate=wgate, wdown=wdown, eo=eol.ap(),
                                       _ccs_x=(ccs, 2), eoall_cc=(eoall.ap(), ccs)), ctx)
        bar()
        build_stage_d3(nc, "h_", dict(
            xin=x3.ap(), eoall=eoall.ap(), xout=out, selidx=selidx, _ccs_eo=(ccs, 6),
            f1v=f1i.ap().rearrange("p (h j) -> (p h) j", h=8),
            f2v=f2i.ap().rearrange("p (h j) -> (p h) j", h=8),
            w1v=w1i.ap().rearrange("p (h j) -> (p h) j", h=8),
            w2v=w2i.ap().rearrange("p (h j) -> (p h) j", h=8)), ctx)
    return nc


def kernel(**inputs):
    xfull = np.ascontiguousarray(np.asarray(inputs["L0_attn_x"], dtype=np.float32).reshape(-1, E))
    sintab, costab = _rope_tables()
    tri, tidc, ioe = _d_consts()
    tglob = tidc.astype(np.int64)
    tid_f = (tglob + tglob // TOK).astype(np.int32)
    nc = build_fused()
    maps = []
    for c in range(N_CORES):
        maps.append(dict(
            x=xfull[c * TOK:(c + 1) * TOK],
            wq1=inputs["L0_attn_Wq1"], wq2=inputs["L0_attn_Wq2"],
            wk1=inputs["L0_attn_Wk1"], wk2=inputs["L0_attn_Wk2"],
            wv=inputs["L0_attn_Wv"], wout=inputs["L0_attn_Wout"],
            bout=inputs["L0_attn_bout"].reshape(1, -1),
            lqk=np.concatenate([inputs["L0_attn_lq1"], inputs["L0_attn_lk1"],
                                inputs["L0_attn_lq2"], inputs["L0_attn_lk2"]]).reshape(1, -1),
            sintok=sintab, costok=costab,
            w1=inputs["L0_ffn_W1"], b1=inputs["L0_ffn_b1"].reshape(1, -1),
            w2=inputs["L0_ffn_W2"], b2=inputs["L0_ffn_b2"].reshape(1, -1),
            cwq=inputs["L1_attn_Wq"], cbq=inputs["L1_attn_bq"].reshape(1, -1),
            cwk=inputs["L1_attn_Wk"], cbk=inputs["L1_attn_bk"].reshape(1, -1),
            cwv=inputs["L1_attn_Wv"], cbv=inputs["L1_attn_bv"].reshape(1, -1),
            cwo=inputs["L1_attn_Wo"], cbo=inputs["L1_attn_bo"].reshape(1, -1),
            wg=inputs["L1_ffn_wg"],
            tri=tri, tid=tid_f, ioe=ioe,
            expid=np.full((1, 1), float(c), np.float32),
            wgate=np.ascontiguousarray(inputs["L1_ffn_w_gate"][c]),
            wdown=np.ascontiguousarray(inputs["L1_ffn_w_down"][c]),
            selidx=(np.arange(128, dtype=np.int32) * 8 + c).reshape(128, 1),
        ))
    res = run_bass_kernel_spmd(nc, maps, core_ids=list(range(N_CORES)))
    outf = np.concatenate([res.results[c]["out"] for c in range(N_CORES)], axis=0)
    return outf.reshape(16, L, E)


def kernel_unfused(**inputs):
    x = np.ascontiguousarray(np.asarray(inputs["L0_attn_x"], dtype=np.float32).reshape(-1, E))
    x1 = run_a(x, inputs)
    x2 = run_b(x1, inputs)
    x3 = run_c(x2, inputs)
    return run_d(x3, inputs).reshape(16, L, E)
